# version 01
# baseline (speedup 1.0000x reference)
import math
import numpy as np
from contextlib import ExitStack

import concourse.bass as bass
import concourse.mybir as mybir
from concourse.bass_utils import run_bass_kernel_spmd

F32 = mybir.dt.float32
BF16 = mybir.dt.bfloat16
AF = mybir.ActivationFunctionType
ALU = mybir.AluOpType
AX = mybir.AxisListType

NCORES = 8
B, L, D = 2, 8192, 1024
T = 2048
NT = 16
H, E = 16, 64
HID = 4096
CH = 128
NCH = 16
EPS = 1e-5
GN_N = float(E * L)

_STAGE = 3
_APARTS = 8
_AP5 = "both"
_DBG = 0


ENGINES = ("pe", "act", "dve", "pool", "sp")
DMA_QUEUES = ("sp", "act", "pool")
NLANES = 8


class _Op:
    __slots__ = ("idx", "eng", "fn", "kind", "reads", "writes", "qidx",
                 "deps", "marked", "inc_no", "ccsem", "tag", "wl", "nop_after")

    def __init__(self, idx, eng, fn, kind, reads, writes, tag):
        self.nop_after = 0
        self.idx = idx
        self.eng = eng
        self.fn = fn
        self.kind = kind
        self.reads = reads
        self.writes = writes
        self.qidx = None
        self.deps = set()
        self.marked = False
        self.inc_no = None
        self.ccsem = None
        self.tag = tag


class Buf:
    __slots__ = ("name", "writes", "reads")

    def __init__(self, name):
        self.name = name
        self.writes = []
        self.reads = []


class Prog:
    def __init__(self):
        self.ops = []
        self.bufs = {}

    def buf(self, name):
        b = self.bufs.get(name)
        if b is None:
            b = self.bufs[name] = Buf(name)
        return b

    def op(self, eng, fn, r=(), w=(), kind="c", tag="", nop=0):
        o = _Op(len(self.ops), eng, fn, kind, list(r), list(w), tag)
        o.nop_after = nop
        self.ops.append(o)
        return o

    def dma(self, queue, fn, r=(), w=(), tag=""):
        return self.op(queue, fn, r=r, w=w, kind="d", tag=tag)

    def cc(self, fn, r=(), w=(), tag=""):
        return self.op("pool", fn, r=r, w=w, kind="cc", tag=tag)

    def resolve(self):
        for o in self.ops:
            for (b, lo, hi) in o.reads:
                for seg in b.writes:
                    if seg[0] < hi and lo < seg[1]:
                        o.deps.add(seg[2])
            for (b, lo, hi) in o.writes:
                for seg in b.writes:
                    if seg[0] < hi and lo < seg[1]:
                        o.deps.add(seg[2])
                for seg in b.reads:
                    if seg[0] < hi and lo < seg[1]:
                        o.deps.add(seg[2])
            for (b, lo, hi) in o.writes:
                b.writes = [s for s in b.writes if not (lo <= s[0] and s[1] <= hi)]
                b.reads = [s for s in b.reads if not (lo <= s[0] and s[1] <= hi)]
                b.writes.append([lo, hi, o.idx])
            for (b, lo, hi) in o.reads:
                b.reads.append([lo, hi, o.idx])

        qctr = {q: 0 for q in DMA_QUEUES}
        for o in self.ops:
            if o.kind == "d":
                o.qidx = qctr[o.eng]
                qctr[o.eng] += 1
        self.qtotal = dict(qctr)

        for o in self.ops:
            for d in o.deps:
                p = self.ops[d]
                if p.kind == "c" and (p.eng != o.eng or p.eng != "pe"):
                    p.marked = True
        inc_ctr = {e: 0 for e in ENGINES}
        ncc = 0
        for o in self.ops:
            if o.kind == "c" and o.marked:
                inc_ctr[o.eng] += 1
                o.inc_no = inc_ctr[o.eng]
            elif o.kind == "cc":
                o.ccsem = ncc
                ncc += 1
        self.inc_total = inc_ctr
        self.ncc = ncc

        best = {e: {} for e in ENGINES}
        for o in self.ops:
            waits = {}
            for d in sorted(o.deps):
                p = self.ops[d]
                if p.kind == "d":
                    key = ("q", p.eng, p.qidx % NLANES)
                    val = 16 * (p.qidx // NLANES + 1)
                elif p.kind == "cc":
                    key = ("cc", p.ccsem)
                    val = 1
                else:
                    if p.eng == o.eng and p.eng == "pe":
                        continue
                    key = ("e", p.eng)
                    val = p.inc_no
                    assert val is not None
                if waits.get(key, -1) < val:
                    waits[key] = val
            myb = best[o.eng]
            o.wl = []
            for key, val in sorted(waits.items()):
                if myb.get(key, 0) >= val:
                    continue
                myb[key] = val
                o.wl.append((key, val))

    def emit(self, nc, ctx):
        esems = {e: ctx.enter_context(nc.semaphore(f"es_{e}")) for e in ENGINES}
        qsems = {q: [ctx.enter_context(nc.semaphore(f"qs_{q}{l}"))
                     for l in range(NLANES)] for q in DMA_QUEUES}
        ccsems = [ctx.enter_context(nc.semaphore(f"cc_{i}")) for i in range(self.ncc)]

        def semof(key):
            if key[0] == "e":
                return esems[key[1]]
            if key[0] == "q":
                return qsems[key[1]][key[2]]
            return ccsems[key[1]]

        per_eng = {e: [] for e in ENGINES}
        for o in self.ops:
            per_eng[o.eng].append(o)

        block = ctx.enter_context(nc.Block())

        def runner(eng_name):
            def run(eng):
                for o in per_eng[eng_name]:
                    for key, val in o.wl:
                        eng.wait_ge(semof(key), val)
                    if o.kind == "d":
                        lane = o.qidx % NLANES
                        use = o.qidx // NLANES
                        if use > 0:
                            eng.wait_ge(qsems[o.eng][lane], 16 * use)
                        inst = o.fn(eng)
                        inst.then_inc(qsems[o.eng][lane], 16)
                        continue
                    inst = o.fn(eng)
                    if o.kind == "cc":
                        inst.then_inc(ccsems[o.ccsem])
                    elif o.marked:
                        inst.then_inc(esems[o.eng], 1)
                    if o.nop_after:
                        eng.drain()
                if eng_name in DMA_QUEUES and self.qtotal[eng_name]:
                    tot = self.qtotal[eng_name]
                    for l in range(min(NLANES, tot)):
                        n_l = (tot - 1 - l) // NLANES + 1
                        eng.wait_ge(qsems[eng_name][l], 16 * n_l)
            return run

        block.tensor(runner("pe"))
        block.scalar(runner("act"))
        block.vector(runner("dve"))
        block.gpsimd(runner("pool"))
        block.sync(runner("sp"))



def build_nc(stage, aparts=4, ap5="both", dbg=0):
    nc = bass.Bass("TRN2", target_bir_lowering=False)
    P = Prog()

    d_in = {}
    def din(name, shape, dtype=F32):
        d_in[name] = nc.dram_tensor(name, shape, dtype, kind="ExternalInput").ap()
        return d_in[name]

    x_in = din("x", [T, D])
    w1_in = din("w1", [HID, D])
    w2_in = din("w2", [D, HID])
    b1_in = din("b1l", [128, 32])
    b2_in = din("b2r", [1, D])
    wqkv_in = din("wqkv", [3, D, D])
    wout_in = din("wout", [D, D])
    gnw_in = din("gnw", [128, 8])
    gnb_in = din("gnb", [128, 8])
    qscale_in = din("qscale", [128, 8 * 128])
    kscale_in = din("kscale", [128, 8 * 128])
    causal_in = din("causal", [128, 128])
    ident_in = din("ident", [128, 128], BF16)
    gdecay_in = din("gdecay", [128, 512])
    prefw_in = din("prefw", [4, 128, 512])
    hsum_in = din("hsum16", [128, 8 * 16])
    hexp_in = din("hexp", [16, 8 * 128], BF16)
    ones_in = din("ones1", [1, 128], BF16)

    y_out = nc.dram_tensor("y", [T, D], F32, kind="ExternalOutput").ap()
    if dbg:
        dbgB = nc.dram_tensor("dbgB", [6, 128, 1024], F32, kind="ExternalOutput").ap()
        dbgF = nc.dram_tensor("dbgF", [3, 128, 512], F32, kind="ExternalOutput").ap()
        bdbg = None

    w1t_scr = nc.dram_tensor("w1t_scr", [32, 128, 1024], BF16).ap()
    w2t_scr = nc.dram_tensor("w2t_scr", [32, 128, 1024], BF16).ap()
    ag_in = nc.dram_tensor("ag_in", [128, 512], F32).ap()
    ag_out = nc.dram_tensor("ag_out", [4 * 128, 512], F32).ap()
    ar_in = nc.dram_tensor("ar_in", [16, 2], F32).ap()
    ar_out = nc.dram_tensor("ar_out", [16, 2], F32).ap()

    groups_state = [[0, 1, 2, 3], [4, 5, 6, 7]]

    ctx = ExitStack()
    sb = lambda n, s, d: ctx.enter_context(nc.sbuf_tensor("S_" + n, s, d))
    psal = lambda n, s, d: ctx.enter_context(nc.psum_tensor(n, s, d))

    A_res = sb("A_res", [128, NT * 1024], F32)
    A_x = sb("A_x", [128, 8 * 2048], BF16)
    A_h = sb("A_h", [128, 32768], BF16)
    A_w = sb("A_w", [128, 8192], BF16)
    A_s32 = sb("A_s32", [128, 1024], F32)
    A_s16 = sb("A_s16", [128, 1024], BF16)
    w1tb = sb("w1tb", [128, 2 * 1024], BF16)
    kdbuf = sb("kdbuf", [128, 1024], BF16)
    xlnT = kdbuf
    scb = sb("scb", [128, 1024], BF16)
    qksc = sb("qksc", [128, 1024], F32)
    stS = sb("stS", [128, 512], F32)
    stSb = sb("stSb", [128, 1024], BF16)
    gdecT = sb("gdecT", [128, 512], F32)
    causal = sb("causal", [128, 128], F32)
    ident = sb("ident", [128, 128], BF16)
    ones1 = sb("ones1", [1, 128], BF16)
    b2row = sb("b2row", [1, 1024], BF16)
    b1l = sb("b1l", [128, 32], F32)
    al1 = sb("al1", [128, 32], F32)
    al2 = sb("al2", [128, 8], F32)
    gnw2 = sb("gnw2", [128, 8], F32)
    gnb2 = sb("gnb2", [128, 8], F32)
    lnS1 = sb("lnS1", [128, 16], F32)
    lnS2 = sb("lnS2", [128, 16], F32)
    lnM = sb("lnM", [128, 16], F32)
    lnT = sb("lnT", [128, 16], F32)
    lnR = sb("lnR", [128, 16], F32)
    lnB = sb("lnB", [128, 16], F32)
    gnsum = sb("gnsum", [128, 8], F32)
    gnsq = sb("gnsq", [128, 8], F32)
    gnst = sb("gnst", [128, 16], F32)
    st16 = sb("st16", [16, 2], F32)
    st16g = sb("st16g", [16, 2], F32)
    me16 = sb("me16", [16, 2], F32)
    v16 = sb("v16", [16, 1], F32)
    t16 = sb("t16", [16, 1], F32)
    mr16 = sb("mr16", [16, 2], F32)
    mrk = sb("mrk", [128, 2], F32)
    gsc = sb("gsc", [128, 8], F32)
    gbi = sb("gbi", [128, 8], F32)
    hsum = sb("hsum", [128, 8 * 16], F32)
    hexp = sb("hexp", [16, 8 * 128], BF16)
    mr16b = sb("mr16b", [16, 2], BF16)

    P1 = psal("P1", [128, 512], F32)
    P2 = psal("P2", [128, 512], F32)
    P3 = psal("P3", [128, 1024], BF16)
    P4 = psal("P4", [128, 512], F32)
    P5 = psal("P5", [128, 1024], F32)
    P6 = psal("P6", [128, 1024], F32)

    bres = P.buf("A_res"); bx = P.buf("A_x"); bh = P.buf("A_h"); bw = P.buf("A_w")
    bs32 = P.buf("A_s32"); bs16 = P.buf("A_s16"); bw1tb = P.buf("w1tb")
    bkd = P.buf("kdbuf"); bxlnT = bkd; bscb = P.buf("scb")
    bstS = P.buf("stS"); bstSb = P.buf("stSb")
    bp = {id(P1): P.buf("P1"), id(P2): P.buf("P2"), id(P3): P.buf("P3"),
          id(P4): P.buf("P4"), id(P5): P.buf("P5"), id(P6): P.buf("P6")}
    bconst = P.buf("consts")
    bln = P.buf("lnstats")
    bal = P.buf("alphas")
    bgn = P.buf("gnstats")
    bscr1 = P.buf("w1t_scr")
    bscr2 = P.buf("w2t_scr")
    bdram = P.buf("dram_cc")
    bout = P.buf("yout")


    def ld_const(dst_ap, src_ap, q="sp"):
        P.dma(q, lambda e, o=dst_ap, i=src_ap: e.dma_start(out=o, in_=i),
              w=[(bconst, 0, 1)], tag="ldc")

    ld_const(b1l[:], b1_in)
    ld_const(gnw2[:], gnw_in)
    ld_const(gnb2[:], gnb_in)
    bqk = P.buf("qksc")
    P.dma("sp", lambda e: e.dma_start(out=qksc[:], in_=kscale_in),
          w=[(bqk, 0, 1024)], tag="ldksc")
    ld_const(causal[:], causal_in)
    ld_const(ident[:], ident_in)
    ld_const(gdecT[:], gdecay_in)
    ld_const(hsum[:], hsum_in)
    ld_const(hexp[:], hexp_in)
    ld_const(ones1[:], ones_in)

    P.dma("sp", lambda e: e.dma_start(
        out=A_res[:].rearrange("p (t n) -> p t n", t=NT),
        in_=x_in.rearrange("(t p) d -> p t d", p=128)),
        w=[(bres, 0, NT * 1024)], tag="ldx")

    P.dma("pool", lambda e: e.dma_start(out=b2row[:], in_=b2_in),
          w=[(P.buf("b2row"), 0, 1024)], tag="b2cast")
    bb2 = P.buf("b2row")

    def ln_phase(phase):
        for i in range(NT):
            rt = (bres, i * 1024, (i + 1) * 1024)
            P.op("dve", lambda e, i=i: e.tensor_reduce(
                out=lnS1[:, i:i + 1], in_=A_res[:, i * 1024:(i + 1) * 1024],
                axis=AX.X, op=ALU.add),
                r=[rt], w=[(bln, i, i + 1)], tag=f"{phase}s1")
            P.op("act", lambda e, i=i: e.activation(
                out=xlnT[:], in_=A_res[:, i * 1024:(i + 1) * 1024],
                func=AF.Square, accum_out=lnS2[:, i:i + 1]),
                r=[rt], w=[(bln, 16 + i, 17 + i), (bxlnT, 0, 1024)], tag=f"{phase}s2")
        P.op("dve", lambda e: e.tensor_scalar(
            out=lnM[:], in0=lnS1[:], scalar1=1.0 / D, scalar2=None, op0=ALU.mult),
            r=[(bln, 0, 16)], w=[(bln, 32, 48)], tag=f"{phase}mean")
        P.op("dve", lambda e: e.tensor_mul(out=lnT[:], in0=lnM[:], in1=lnM[:]),
            r=[(bln, 32, 48)], w=[(bln, 48, 64)], tag=f"{phase}m2")
        P.op("dve", lambda e: e.tensor_scalar(
            out=lnS2[:], in0=lnS2[:], scalar1=1.0 / D, scalar2=None, op0=ALU.mult),
            r=[(bln, 16, 32)], w=[(bln, 16, 32)], tag=f"{phase}ex2")
        P.op("dve", lambda e: e.tensor_sub(out=lnT[:], in0=lnS2[:], in1=lnT[:]),
            r=[(bln, 16, 64)], w=[(bln, 48, 64)], tag=f"{phase}var")
        P.op("act", lambda e: e.activation(out=lnT[:], in_=lnT[:], func=AF.Ln, bias=np.float32(EPS)),
            r=[(bln, 48, 64)], w=[(bln, 48, 64)], tag=f"{phase}ln")
        P.op("act", lambda e: e.activation(out=lnR[:], in_=lnT[:], func=AF.Exp, scale=-0.5),
            r=[(bln, 48, 64)], w=[(bln, 64, 80)], tag=f"{phase}rstd")
        P.op("dve", lambda e: e.tensor_mul(out=lnB[:], in0=lnM[:], in1=lnR[:]),
            r=[(bln, 32, 48), (bln, 64, 80)], w=[(bln, 80, 96)], tag=f"{phase}mb")
        P.op("dve", lambda e: e.tensor_scalar(
            out=lnB[:], in0=lnB[:], scalar1=-1.0, scalar2=None, op0=ALU.mult),
            r=[(bln, 80, 96)], w=[(bln, 80, 96)], tag=f"{phase}nb")
        for i in range(NT):
            P.op("dve", lambda e, i=i: e.tensor_scalar(
                out=xlnT[:], in0=A_res[:, i * 1024:(i + 1) * 1024],
                scalar1=lnR[:, i:i + 1], scalar2=lnB[:, i:i + 1],
                op0=ALU.mult, op1=ALU.add),
                r=[(bres, i * 1024, (i + 1) * 1024), (bln, 64, 96)],
                w=[(bxlnT, 0, 1024)], tag=f"{phase}app{i}")
            for k in range(8):
                P.op("pe", lambda e, k=k: e.transpose(
                    out=P3[:, k * 128:(k + 1) * 128],
                    in_=xlnT[:, k * 128:(k + 1) * 128], identity=ident[:]),
                    r=[(bxlnT, k * 128, (k + 1) * 128), (bconst, 0, 1)],
                    w=[(bp[id(P3)], k * 128, (k + 1) * 128)], tag=f"{phase}tp")
            P.op("act", lambda e, i=i: e.activation(
                out=A_x[:].rearrange("p (k n) -> p k n", k=8)[:, :, i * 128:(i + 1) * 128],
                in_=P3[:].rearrange("p (k n) -> p k n", k=8), func=AF.Copy),
                r=[(bp[id(P3)], 0, 1024)],
                w=[(bx, k * 2048 + i * 128, k * 2048 + (i + 1) * 128) for k in range(8)],
                tag=f"{phase}xev{i}")

    bmrk = P.buf("mrk")

    def w1_prep(ms):
        slot = (ms % 2) * 1024
        P.dma("sp", lambda e, ms=ms: e.dma_start(
            out=A_s32[:, 0:1024], in_=w1_in[ms * 128:(ms + 1) * 128, :]),
            w=[(bs32, 0, 1024)], tag=f"w1ld{ms}")
        P.op("dve", lambda e, ms=ms: e.tensor_reduce(
            out=al1[:, ms:ms + 1], in_=A_s32[:, 0:1024], axis=AX.X,
            op=ALU.add, apply_absolute_value=True),
            r=[(bs32, 0, 1024)], w=[(bal, ms, ms + 1)], tag=f"w1al{ms}")
        P.op("act", lambda e: e.activation(
            out=A_s16[:, 0:1024], in_=A_s32[:, 0:1024], func=AF.Sign),
            r=[(bs32, 0, 1024)], w=[(bs16, 0, 1024)], tag=f"w1sg{ms}")
        P.op("dve", lambda e, ms=ms: e.tensor_scalar(
            out=A_s16[:, 0:1024], in0=A_s16[:, 0:1024],
            scalar1=al1[:, ms:ms + 1], scalar2=1.0 / D, op0=ALU.mult, op1=ALU.mult),
            r=[(bs16, 0, 1024), (bal, ms, ms + 1)], w=[(bs16, 0, 1024)], tag=f"w1sc{ms}")
        for k in range(8):
            P.op("pe", lambda e, k=k: e.transpose(
                out=P3[:, k * 128:(k + 1) * 128],
                in_=A_s16[:, k * 128:(k + 1) * 128], identity=ident[:]),
                r=[(bs16, k * 128, (k + 1) * 128), (bconst, 0, 1)],
                w=[(bp[id(P3)], k * 128, (k + 1) * 128)], tag=f"w1tp{ms}")
        P.op("act", lambda e, slot=slot: e.activation(
            out=w1tb[:, slot:slot + 1024], in_=P3[:], func=AF.Copy),
            r=[(bp[id(P3)], 0, 1024)], w=[(bw1tb, slot, slot + 1024)], tag=f"w1ev{ms}")
        P.dma("act", lambda e, ms=ms, slot=slot: e.dma_start(
            out=w1t_scr[ms], in_=w1tb[:, slot:slot + 1024]),
            r=[(bw1tb, slot, slot + 1024)], w=[(bscr1, ms, ms + 1)], tag=f"w1st{ms}")

    def w2_alpha(kd):
        for qq in range(4):
            P.dma("pool", lambda e, kd=kd, qq=qq: e.dma_start(
                out=A_s32[:, 0:1024],
                in_=w2_in[kd * 128:(kd + 1) * 128, qq * 1024:(qq + 1) * 1024]),
                w=[(bs32, 0, 1024)], tag=f"w2a_ld{kd}_{qq}")
            if qq == 0:
                P.op("dve", lambda e, kd=kd: e.tensor_reduce(
                    out=al2[:, kd:kd + 1], in_=A_s32[:, 0:1024], axis=AX.X,
                    op=ALU.add, apply_absolute_value=True),
                    r=[(bs32, 0, 1024)], w=[(bal, 40 + kd, 41 + kd)], tag=f"w2a1{kd}")
            else:
                P.op("dve", lambda e: e.tensor_reduce(
                    out=mrk[:, 0:1], in_=A_s32[:, 0:1024], axis=AX.X,
                    op=ALU.add, apply_absolute_value=True),
                    r=[(bs32, 0, 1024)], w=[(bmrk, 0, 1)], tag=f"w2a2{kd}_{qq}")
                P.op("dve", lambda e, kd=kd: e.tensor_add(
                    out=al2[:, kd:kd + 1], in0=al2[:, kd:kd + 1], in1=mrk[:, 0:1]),
                    r=[(bal, 40 + kd, 41 + kd), (bmrk, 0, 1)],
                    w=[(bal, 40 + kd, 41 + kd)], tag=f"w2aa{kd}_{qq}")
        P.op("dve", lambda e, kd=kd: e.tensor_scalar(
            out=al2[:, kd:kd + 1], in0=al2[:, kd:kd + 1],
            scalar1=1.0 / HID, scalar2=None, op0=ALU.mult),
            r=[(bal, 40 + kd, 41 + kd)], w=[(bal, 40 + kd, 41 + kd)], tag=f"w2am{kd}")

    def w2_prep2(kq):
        q, kd = divmod(kq, 8)
        P.dma("pool", lambda e, kd=kd, q=q: e.dma_start(
            out=A_s32[:, 0:1024],
            in_=w2_in[kd * 128:(kd + 1) * 128, q * 1024:(q + 1) * 1024]),
            w=[(bs32, 0, 1024)], tag=f"w2ld{kq}")
        P.op("act", lambda e: e.activation(
            out=A_s16[:, 0:1024], in_=A_s32[:, 0:1024], func=AF.Sign),
            r=[(bs32, 0, 1024)], w=[(bs16, 0, 1024)], tag=f"w2sg{kq}")
        P.op("dve", lambda e, kd=kd: e.tensor_scalar(
            out=A_s16[:, 0:1024], in0=A_s16[:, 0:1024],
            scalar1=al2[:, kd:kd + 1], scalar2=None, op0=ALU.mult),
            r=[(bs16, 0, 1024), (bal, 40 + kd, 41 + kd)],
            w=[(bs16, 0, 1024)], tag=f"w2sc{kq}")
        for j in range(8):
            P.op("pe", lambda e, j=j: e.transpose(
                out=P3[:, j * 128:(j + 1) * 128],
                in_=A_s16[:, j * 128:(j + 1) * 128], identity=ident[:]),
                r=[(bs16, j * 128, (j + 1) * 128), (bconst, 0, 1)],
                w=[(bp[id(P3)], j * 128, (j + 1) * 128)], tag=f"w2tp{kq}")
        P.op("act", lambda e: e.activation(
            out=kdbuf[:], in_=P3[:], func=AF.Copy),
            r=[(bp[id(P3)], 0, 1024)], w=[(bkd, 0, 1024)], tag=f"w2ev{kq}")
        P.dma("act", lambda e, kd=kd, q=q: e.dma_start(
            out=w2t_scr[q * 8:(q + 1) * 8, :, kd * 128:(kd + 1) * 128].rearrange(
                "j p d -> p j d"),
            in_=kdbuf[:].rearrange("p (j d) -> p j d", j=8)),
            r=[(bkd, 0, 1024)],
            w=[(bscr2, (q * 8 + j) * 8 + kd, (q * 8 + j) * 8 + kd + 1)
               for j in range(8)],
            tag=f"w2st{kq}")

    def ffn_phase(first):
        if first:
            for kd in range(8):
                w2_alpha(kd)
        w2q_emitted = 0
        for pair in range(2):
            for mloc in range(16):
                ms = pair * 16 + mloc
                slot = (ms % 2) * 1024
                if first:
                    w1_prep(ms)
                    if w2q_emitted < 32:
                        w2_prep2(w2q_emitted)
                        w2q_emitted += 1
                else:
                    P.dma("sp", lambda e, ms=ms, slot=slot: e.dma_start(
                        out=w1tb[:, slot:slot + 1024], in_=w1t_scr[ms]),
                        r=[(bscr1, ms, ms + 1)], w=[(bw1tb, slot, slot + 1024)],
                        tag=f"w1rd{ms}")
                for n in range(4):
                    ps = (P1, P2)[n % 2]
                    pb = bp[id(ps)]
                    for k in range(8):
                        P.op("pe", lambda e, ps=ps, slot=slot, k=k, n=n: e.matmul(
                            out=ps[:, 0:512],
                            lhsT=w1tb[:, slot + k * 128:slot + (k + 1) * 128],
                            rhs=A_x[:, k * 2048 + n * 512:k * 2048 + (n + 1) * 512],
                            start=(k == 0), stop=(k == 7)),
                            r=[(bw1tb, slot + k * 128, slot + (k + 1) * 128),
                               (bx, k * 2048 + n * 512, k * 2048 + (n + 1) * 512)],
                            w=[(pb, 0, 512)], tag=f"mm1_{ms}_{n}")
                    P.op("act", lambda e, ps=ps, mloc=mloc, n=n, ms=ms: e.activation(
                        out=A_h[:, mloc * 2048 + n * 512:mloc * 2048 + (n + 1) * 512],
                        in_=ps[:, 0:512], func=AF.Gelu, bias=b1l[:, ms:ms + 1]),
                        r=[(pb, 0, 512), (bconst, 0, 1)],
                        w=[(bh, mloc * 2048 + n * 512, mloc * 2048 + (n + 1) * 512)],
                        tag=f"gelu_{ms}_{n}")
            for q in range(2):
                s = pair * 2 + q
                P.dma("sp", lambda e, s=s: e.dma_start(
                    out=A_w[:].rearrange("p (j d) -> p j d", j=8),
                    in_=w2t_scr[s * 8:(s + 1) * 8].rearrange("j p d -> p j d")),
                    r=[(bscr2, s * 8 * 8, (s + 1) * 8 * 8)],
                    w=[(bw, 0, 8192)], tag=f"w2rd{s}")
                for mt in range(NT):
                    ps = (P5, P6)[mt % 2]
                    pb = bp[id(ps)]
                    nfirst = True
                    if s == 0:
                        for half in range(2):
                            P.op("pe", lambda e, ps=ps, half=half: e.matmul(
                                out=ps[:, half * 512:(half + 1) * 512],
                                lhsT=ones1[:],
                                rhs=b2row[:, half * 512:(half + 1) * 512],
                                start=True, stop=False),
                                r=[(bconst, 0, 1), (bb2, 0, 1024)],
                                w=[(pb, half * 512, (half + 1) * 512)],
                                tag=f"b2_{mt}")
                        nfirst = False
                    for j in range(8):
                        mloc = q * 8 + j
                        for half in range(2):
                            P.op("pe", lambda e, ps=ps, mloc=mloc, mt=mt, j=j,
                                 half=half, st=(nfirst and j == 0): e.matmul(
                                out=ps[:, half * 512:(half + 1) * 512],
                                lhsT=A_h[:, mloc * 2048 + mt * 128:mloc * 2048 + (mt + 1) * 128],
                                rhs=A_w[:, j * 1024 + half * 512:j * 1024 + (half + 1) * 512],
                                start=st, stop=(j == 7)),
                                r=[(bh, mloc * 2048 + mt * 128, mloc * 2048 + (mt + 1) * 128),
                                   (bw, j * 1024 + half * 512, j * 1024 + (half + 1) * 512)],
                                w=[(pb, half * 512, (half + 1) * 512)],
                                tag=f"mm2_{s}_{mt}")
                    P.op("dve", lambda e, ps=ps, mt=mt: e.scalar_tensor_tensor(
                        out=A_res[:, mt * 1024:(mt + 1) * 1024], in0=ps[:],
                        scalar=0.5, in1=A_res[:, mt * 1024:(mt + 1) * 1024],
                        op0=ALU.mult, op1=ALU.add),
                        r=[(pb, 0, 1024), (bres, mt * 1024, (mt + 1) * 1024)],
                        w=[(bres, mt * 1024, (mt + 1) * 1024)], tag=f"racc_{s}_{mt}")

    ln_phase("ln1")
    ffn_phase(first=True)

    class _EndAttn(Exception):
        pass
    try:
      if stage >= 2:
        bag = P.buf("dram_cc")
        KT0, VB0 = 0, 16384
        bdbgq = P.buf("dbgq")

        def dump16(idx, buf, ap, lo, hi):
            if not dbg:
                return
            P.dma("pool", lambda e, idx=idx, ap=ap: e.dma_start(
                out=dbgB[idx], in_=ap), r=[(buf, lo, hi)],
                w=[(bdbgq, idx, idx + 1)], tag=f"dbg{idx}")

        def dump32(idx, buf, ap, lo, hi):
            if not dbg:
                return
            P.dma("pool", lambda e, idx=idx, ap=ap: e.dma_start(
                out=dbgF[idx], in_=ap), r=[(buf, lo, hi)],
                w=[(bdbgq, 16 + idx, 17 + idx)], tag=f"dbf{idx}")

        ln_phase("ln2")

        def proj_pass(widx, out_base, scale_evac):
            P.dma("pool", lambda e, widx=widx: e.dma_start(
                out=A_w[:].rearrange("p (k n) -> p k n", k=8),
                in_=wqkv_in[widx].rearrange("(k p) n -> p k n", p=128)),
                r=[(bw, 0, 8192)], w=[(bw, 0, 8192)], tag=f"wl{widx}")
            for c in range(NCH):
                ps = (P5, P6)[c % 2]
                pb = bp[id(ps)]
                for half in range(2):
                    for k in range(8):
                        P.op("pe", lambda e, ps=ps, c=c, k=k, half=half: e.matmul(
                            out=ps[:, half * 512:(half + 1) * 512],
                            lhsT=A_x[:, k * 2048 + c * 128:k * 2048 + (c + 1) * 128],
                            rhs=A_w[:, k * 1024 + half * 512:k * 1024 + (half + 1) * 512],
                            start=(k == 0), stop=(k == 7)),
                            r=[(bx, k * 2048 + c * 128, k * 2048 + (c + 1) * 128),
                               (bw, k * 1024 + half * 512, k * 1024 + (half + 1) * 512)],
                            w=[(pb, half * 512, (half + 1) * 512)], tag=f"pj{widx}_{c}",
                            nop=(64 if k == 7 else 0))
                scale_evac(c, ps, pb, out_base)

        def k_evac(c, ps, pb, base):
            P.op("dve", lambda e, ps=ps, c=c, base=base: e.tensor_tensor(
                out=A_h[:, base + c * 1024:base + (c + 1) * 1024],
                in0=ps[:], in1=qksc[:], op=ALU.mult),
                r=[(pb, 0, 1024), (bqk, 0, 1024)],
                w=[(bh, base + c * 1024, base + (c + 1) * 1024)], tag=f"kev{c}")

        def v_evac(c, ps, pb, base):
            P.op("act", lambda e, ps=ps, c=c, base=base: e.activation(
                out=A_h[:, base + c * 1024:base + (c + 1) * 1024],
                in_=ps[:], func=AF.Copy),
                r=[(pb, 0, 1024)],
                w=[(bh, base + c * 1024, base + (c + 1) * 1024)], tag=f"vev{c}")

        proj_pass(1, KT0, k_evac)
        proj_pass(2, VB0, v_evac)
        dump16(0, bh, A_h[:, KT0:KT0 + 1024], KT0, KT0 + 1024)
        dump16(1, bh, A_h[:, VB0:VB0 + 1024], VB0, VB0 + 1024)

        if aparts < 2:
            raise _EndAttn()
        P.op("dve", lambda e: e.memset(stS[:], 0.0),
             w=[(bstS, 0, 512)], tag="st0")
        bp4 = bp[id(P4)]

        def dS_ops(c, tagp):
            for h in range(H):
                pr, pc_ = (h % 2) * 64, (h // 2) * 64
                P.op("pe", lambda e, c=c, h=h, pr=pr, pc_=pc_: e.matmul(
                    out=P4[pr:pr + 64, pc_:pc_ + 64],
                    lhsT=A_h[:, KT0 + c * 1024 + h * 64:KT0 + c * 1024 + (h + 1) * 64],
                    rhs=A_h[:, VB0 + c * 1024 + h * 64:VB0 + c * 1024 + (h + 1) * 64],
                    start=True, stop=True),
                    r=[(bh, KT0 + c * 1024 + h * 64, KT0 + c * 1024 + (h + 1) * 64),
                       (bh, VB0 + c * 1024 + h * 64, VB0 + c * 1024 + (h + 1) * 64)],
                    w=[(bp4, pc_, pc_ + 64)], tag=f"{tagp}dS{c}", nop=64)
            P.op("dve", lambda e: e.tensor_add(out=stS[:], in0=stS[:], in1=P4[:]),
                 r=[(bstS, 0, 512), (bp4, 0, 512)], w=[(bstS, 0, 512)],
                 tag=f"{tagp}su1_{c}")
            P.op("dve", lambda e: e.tensor_mul(out=stS[:], in0=stS[:], in1=gdecT[:]),
                 r=[(bstS, 0, 512), (bconst, 0, 1)], w=[(bstS, 0, 512)],
                 tag=f"{tagp}su2_{c}")
            if tagp == "b":
                for par in range(2):
                    P.op("act", lambda e, par=par: e.activation(
                        out=stSb[par * 64:par * 64 + 64, :].rearrange(
                            "p (h v) -> p h v", v=64)[:, par::2, :],
                        in_=stS[par * 64:par * 64 + 64, :].rearrange(
                            "p (h v) -> p h v", v=64),
                        func=AF.Copy),
                        r=[(bstS, 0, 512)], w=[(bstSb, 0, 1024)], tag=f"bsb{c}_{par}")

        for c in range(NCH):
            dS_ops(c, "a")

        dump32(0, bstS, stS[:], 0, 512)
        P.dma("pool", lambda e: e.dma_start(out=ag_in, in_=stS[:]),
              r=[(bstS, 0, 512)], w=[(bag, 0, 1)], tag="agw")
        P.cc(lambda e: e.collective_compute(
            "AllGather", ALU.bypass, replica_groups=groups_state,
            ins=[ag_in.opt()], outs=[ag_out.opt()]),
            r=[(bag, 0, 1)], w=[(bag, 1, 2)], tag="ag")
        P.op("dve", lambda e: e.memset(stS[:], 0.0),
             r=[(bag, 0, 1)], w=[(bstS, 0, 512)], tag="stz")
        for r_ in range(4):
            P.dma("sp", lambda e, r_=r_: e.dma_start(
                out=A_s32[:, 0:512], in_=ag_out[r_ * 128:(r_ + 1) * 128, :]),
                r=[(bag, 1, 2)], w=[(bs32, 0, 512)], tag=f"agr{r_}")
            P.dma("sp", lambda e, r_=r_: e.dma_start(
                out=A_s32[:, 512:1024], in_=prefw_in[r_]),
                w=[(bs32, 512, 1024)], tag=f"pwr{r_}")
            P.op("dve", lambda e: e.tensor_mul(
                out=A_s32[:, 0:512], in0=A_s32[:, 0:512], in1=A_s32[:, 512:1024]),
                r=[(bs32, 0, 1024)], w=[(bs32, 0, 512)], tag=f"pwm{r_}")
            P.op("dve", lambda e: e.tensor_add(
                out=stS[:], in0=stS[:], in1=A_s32[:, 0:512]),
                r=[(bstS, 0, 512), (bs32, 0, 512)], w=[(bstS, 0, 512)],
                tag=f"pwa{r_}")

        if aparts < 3:
            raise _EndAttn()
        P.op("dve", lambda e: e.memset(stSb[:], 0.0),
             w=[(bstSb, 0, 1024)], tag="sbz")
        for par in range(2):
            P.op("act", lambda e, par=par: e.activation(
                out=stSb[par * 64:par * 64 + 64, :].rearrange(
                    "p (h v) -> p h v", v=64)[:, par::2, :],
                in_=stS[par * 64:par * 64 + 64, :].rearrange(
                    "p (h v) -> p h v", v=64),
                func=AF.Copy),
                r=[(bstS, 0, 512)], w=[(bstSb, 0, 1024)], tag=f"sb0_{par}")
        dump32(1, bstS, stS[:], 0, 512)

        P.dma("pool", lambda e: e.dma_start(
            out=A_w[:].rearrange("p (k n) -> p k n", k=8),
            in_=wqkv_in[0].rearrange("(k p) n -> p k n", p=128)),
            r=[(bw, 0, 8192)], w=[(bw, 0, 8192)], tag="wlq")
        P.dma("sp", lambda e: e.dma_start(out=qksc[:], in_=qscale_in),
              r=[(bqk, 0, 1024)], w=[(bqk, 0, 1024)], tag="ldqsc")
        bscb_ = P.buf("scb")
        bgn_ = P.buf("gnacc")
        bgnst = P.buf("gnst_t")
        P.op("dve", lambda e: e.memset(gnsum[:], 0.0), w=[(bgn_, 0, 8)], tag="gz1")
        P.op("dve", lambda e: e.memset(gnsq[:], 0.0), w=[(bgn_, 8, 16)], tag="gz2")
        bp6 = bp[id(P6)]
        bp5 = bp[id(P5)]
        for c in range(NCH):
            for dt in range(8):
                for k in range(8):
                    P.op("pe", lambda e, c=c, dt=dt, k=k: e.matmul(
                        out=P6[:, dt * 128:(dt + 1) * 128],
                        lhsT=A_w[:, k * 1024 + dt * 128:k * 1024 + (dt + 1) * 128],
                        rhs=A_x[:, k * 2048 + c * 128:k * 2048 + (c + 1) * 128],
                        start=(k == 0), stop=(k == 7)),
                        r=[(bw, k * 1024 + dt * 128, k * 1024 + (dt + 1) * 128),
                           (bx, k * 2048 + c * 128, k * 2048 + (c + 1) * 128)],
                        w=[(bp6, dt * 128, (dt + 1) * 128)], tag=f"qp{c}",
                        nop=(64 if k == 7 else 0))
            P.op("dve", lambda e: e.tensor_tensor(
                out=A_s16[:, 0:1024], in0=P6[:], in1=qksc[:], op=ALU.mult),
                r=[(bp6, 0, 1024), (bqk, 0, 1024)], w=[(bs16, 0, 1024)],
                tag=f"qev{c}")
            if c == 0:
                dump16(2, bs16, A_s16[:, 0:1024], 0, 1024)
            for k in range(8):
                P.op("pe", lambda e, c=c, k=k: e.transpose(
                    out=P3[:, k * 128:(k + 1) * 128],
                    in_=A_h[:, KT0 + c * 1024 + k * 128:KT0 + c * 1024 + (k + 1) * 128],
                    identity=ident[:]),
                    r=[(bh, KT0 + c * 1024 + k * 128, KT0 + c * 1024 + (k + 1) * 128),
                       (bconst, 0, 1)],
                    w=[(bp[id(P3)], k * 128, (k + 1) * 128)], tag=f"ktp{c}")
            P.op("act", lambda e: e.activation(out=kdbuf[:], in_=P3[:], func=AF.Copy),
                 r=[(bp[id(P3)], 0, 1024)], w=[(bkd, 0, 1024)], tag=f"kde{c}")
            if c == 0:
                dump16(3, bkd, kdbuf[:], 0, 1024)
            for hb in range(2 if aparts >= 4 else 0):
                for q in range(2):
                    quad = hb * 2 + q
                    ps = (P1, P2)[q]
                    pbq = bp[id(ps)]
                    for hh in range(4):
                        h = quad * 4 + hh
                        pr, dt = (h % 2) * 64, h // 2
                        P.op("pe", lambda e, h=h, hh=hh, pr=pr, dt=dt, ps=ps: e.matmul(
                            out=ps[:, hh * 128:(hh + 1) * 128],
                            lhsT=kdbuf[pr:pr + 64, dt * 128:(dt + 1) * 128],
                            rhs=A_s16[pr:pr + 64, dt * 128:(dt + 1) * 128],
                            start=True, stop=True),
                            r=[(bkd, dt * 128, (dt + 1) * 128),
                               (bs16, dt * 128, (dt + 1) * 128)],
                            w=[(pbq, hh * 128, (hh + 1) * 128)], tag=f"sc{c}_{quad}",
                            nop=64)
                    P.op("dve", lambda e, ps=ps, q=q: e.tensor_tensor(
                        out=scb[:, q * 512:(q + 1) * 512].rearrange(
                            "p (g n) -> p g n", g=4),
                        in0=ps[:].rearrange("p (g n) -> p g n", g=4),
                        in1=causal[:].rearrange("p (g n) -> p g n", g=1
                                                ).broadcast_to((128, 4, 128)),
                        op=ALU.mult),
                        r=[(pbq, 0, 512), (bconst, 0, 1)],
                        w=[(bscb_, q * 512, (q + 1) * 512)], tag=f"sce{c}_{quad}")
                for hh8 in range(8 if aparts >= 5 else 0):
                    h = hb * 8 + hh8
                    pr, dt = (h % 2) * 64, h // 2
                    soff = (hh8 // 4) * 512 + (hh8 % 4) * 128
                    if ap5 in ("both", "intra", "noevac"):
                        P.op("pe", lambda e, h=h, pr=pr, dt=dt, soff=soff, c=c,
                             st=(ap5 not in ("both", "cross")): e.matmul(
                            out=P5[pr:pr + 64, dt * 128:(dt + 1) * 128],
                            lhsT=A_h[:, VB0 + c * 1024 + h * 64:VB0 + c * 1024 + (h + 1) * 64],
                            rhs=scb[:, soff:soff + 128],
                            start=True, stop=st),
                            r=[(bh, VB0 + c * 1024 + h * 64, VB0 + c * 1024 + (h + 1) * 64),
                               (bscb_, soff, soff + 128)],
                            w=[(bp5, dt * 128, (dt + 1) * 128)], tag=f"ri{c}_{h}",
                            nop=64)
                    if ap5 in ("both", "cross", "noevac"):
                        P.op("pe", lambda e, h=h, pr=pr, dt=dt,
                             st=(ap5 not in ("both", "intra")): e.matmul(
                            out=P5[pr:pr + 64, dt * 128:(dt + 1) * 128],
                            lhsT=stSb[:, h * 64:(h + 1) * 64],
                            rhs=A_s16[:, dt * 128:(dt + 1) * 128],
                            start=st, stop=True),
                            r=[(bstSb, h * 64, (h + 1) * 64),
                               (bs16, dt * 128, (dt + 1) * 128)],
                            w=[(bp5, dt * 128, (dt + 1) * 128)], tag=f"rc{c}_{h}",
                            nop=64)
            if c == 0:
                dump16(4, bscb_, scb[:], 0, 1024)
            if aparts >= 7:
                dS_ops(c, "b")
            if aparts < 5 or ap5 == "noevac":
                continue
            P.op("dve", lambda e, c=c: e.tensor_copy(
                out=A_h[:, KT0 + c * 1024:KT0 + (c + 1) * 1024], in_=P5[:]),
                r=[(bp5, 0, 1024)],
                w=[(bh, KT0 + c * 1024, KT0 + (c + 1) * 1024)], tag=f"roe{c}")
            if aparts < 6:
                continue
            if c == 0:
                dump16(5, bh, A_h[:, KT0:KT0 + 1024], KT0, KT0 + 1024)
            P.op("dve", lambda e: e.tensor_reduce(
                out=gnst[:, 0:8], in_=P5[:].rearrange("p (g n) -> p g n", g=8),
                axis=AX.X, op=ALU.add),
                r=[(bp5, 0, 1024)], w=[(bgnst, 0, 8)], tag=f"gs1{c}")
            P.op("dve", lambda e: e.tensor_add(
                out=gnsum[:], in0=gnsum[:], in1=gnst[:, 0:8]),
                r=[(bgn_, 0, 8), (bgnst, 0, 8)], w=[(bgn_, 0, 8)], tag=f"gs2{c}")
            P.op("act", lambda e: e.activation(
                out=kdbuf[:], in_=P5[:], func=AF.Square),
                r=[(bp5, 0, 1024)], w=[(bkd, 0, 1024)], tag=f"gq1{c}")
            P.op("dve", lambda e: e.tensor_reduce(
                out=gnst[:, 8:16], in_=kdbuf[:].rearrange("p (g n) -> p g n", g=8),
                axis=AX.X, op=ALU.add),
                r=[(bkd, 0, 1024)], w=[(bgnst, 8, 16)], tag=f"gq2{c}")
            P.op("dve", lambda e: e.tensor_add(
                out=gnsq[:], in0=gnsq[:], in1=gnst[:, 8:16]),
                r=[(bgn_, 8, 16), (bgnst, 8, 16)], w=[(bgn_, 8, 16)], tag=f"gq3{c}")

        if aparts < 8:
            raise _EndAttn()
        P.op("dve", lambda e: e.tensor_copy(
            out=gnst[:].rearrange("p (k two) -> p k two", two=2)[:, :, 0:1],
            in_=gnsum[:].rearrange("p (k one) -> p k one", one=1)),
            r=[(bgn_, 0, 8)], w=[(bgnst, 0, 16)], tag="gc1")
        P.op("dve", lambda e: e.tensor_copy(
            out=gnst[:].rearrange("p (k two) -> p k two", two=2)[:, :, 1:2],
            in_=gnsq[:].rearrange("p (k one) -> p k one", one=1)),
            r=[(bgn_, 8, 16)], w=[(bgnst, 0, 16)], tag="gc2")
        for k in range(8):
            P.op("pe", lambda e, k=k: e.matmul(
                out=P4[0:16, 0:2], lhsT=hsum[:, k * 16:(k + 1) * 16],
                rhs=gnst[:, 2 * k:2 * k + 2], start=(k == 0), stop=(k == 7)),
                r=[(bconst, 0, 1), (bgnst, 0, 16)],
                w=[(bp4, 0, 2)], tag=f"gst{k}", nop=(64 if k == 7 else 0))
        P.op("dve", lambda e: e.tensor_copy(out=st16[:], in_=P4[0:16, 0:2]),
             r=[(bp4, 0, 2)], w=[(bgn_, 48, 50)], tag="gse")
        P.dma("pool", lambda e: e.dma_start(out=ar_in, in_=st16[:]),
              r=[(bgn_, 48, 50)], w=[(bag, 2, 3)], tag="arw")
        P.cc(lambda e: e.collective_compute(
            "AllReduce", ALU.add, replica_groups=groups_state,
            ins=[ar_in.opt()], outs=[ar_out.opt()]),
            r=[(bag, 2, 3)], w=[(bag, 3, 4)], tag="ar")
        P.dma("sp", lambda e: e.dma_start(out=st16g[:], in_=ar_out),
              r=[(bag, 3, 4)], w=[(bgn_, 50, 52)], tag="arr")
        P.op("dve", lambda e: e.tensor_scalar(
            out=me16[:], in0=st16g[:], scalar1=1.0 / GN_N, scalar2=None,
            op0=ALU.mult),
            r=[(bgn_, 50, 52)], w=[(bgn_, 52, 54)], tag="gm")
        P.op("dve", lambda e: e.tensor_mul(
            out=t16[:], in0=me16[:, 0:1], in1=me16[:, 0:1]),
            r=[(bgn_, 52, 54)], w=[(bgn_, 54, 55)], tag="gm2")
        P.op("dve", lambda e: e.tensor_sub(
            out=v16[:], in0=me16[:, 1:2], in1=t16[:]),
            r=[(bgn_, 52, 55)], w=[(bgn_, 55, 56)], tag="gv")
        P.op("act", lambda e: e.activation(
            out=v16[:], in_=v16[:], func=AF.Ln, bias=np.float32(EPS)),
            r=[(bgn_, 55, 56)], w=[(bgn_, 55, 56)], tag="gln")
        P.op("act", lambda e: e.activation(
            out=v16[:], in_=v16[:], func=AF.Exp, scale=-0.5),
            r=[(bgn_, 55, 56)], w=[(bgn_, 55, 56)], tag="gexp")
        P.op("dve", lambda e: e.tensor_copy(out=mr16[:, 0:1], in_=me16[:, 0:1]),
             r=[(bgn_, 52, 54)], w=[(bgn_, 56, 57)], tag="gp1")
        P.op("dve", lambda e: e.tensor_copy(out=mr16[:, 1:2], in_=v16[:]),
             r=[(bgn_, 55, 56)], w=[(bgn_, 57, 58)], tag="gp2")
        P.op("act", lambda e: e.activation(out=mr16b[:], in_=mr16[:], func=AF.Copy),
             r=[(bgn_, 56, 58)], w=[(bgn_, 58, 59)], tag="gp3")
        for k in range(8):
            P.op("pe", lambda e, k=k: e.matmul(
                out=P4[:, 32:34], lhsT=hexp[:, k * 128:(k + 1) * 128],
                rhs=mr16b[:], start=True, stop=True),
                r=[(bconst, 0, 1), (bgn_, 58, 59)],
                w=[(bp4, 32, 34)], tag=f"ge{k}", nop=64)
            P.op("dve", lambda e: e.tensor_copy(out=mrk[:], in_=P4[:, 32:34]),
                 r=[(bp4, 32, 34)], w=[(bmrk, 0, 2)], tag=f"gee{k}")
            P.op("dve", lambda e, k=k: e.tensor_mul(
                out=gsc[:, k:k + 1], in0=mrk[:, 1:2], in1=gnw2[:, k:k + 1]),
                r=[(bmrk, 0, 2), (bconst, 0, 1)], w=[(bgn_, 60 + k, 61 + k)],
                tag=f"gsc{k}")
            P.op("dve", lambda e, k=k: e.tensor_mul(
                out=mrk[:, 0:1], in0=mrk[:, 0:1], in1=gsc[:, k:k + 1]),
                r=[(bmrk, 0, 2), (bgn_, 60 + k, 61 + k)], w=[(bmrk, 0, 1)],
                tag=f"gbm{k}")
            P.op("dve", lambda e, k=k: e.tensor_sub(
                out=gbi[:, k:k + 1], in0=gnb2[:, k:k + 1], in1=mrk[:, 0:1]),
                r=[(bmrk, 0, 1), (bconst, 0, 1)], w=[(bgn_, 70 + k, 71 + k)],
                tag=f"gbi{k}")
        for k in range(8):
            P.op("dve", lambda e, k=k: e.tensor_scalar(
                out=A_h[:, KT0:KT0 + 16384].rearrange(
                    "p (c x) -> p c x", c=16)[:, :, k * 128:(k + 1) * 128],
                in0=A_h[:, KT0:KT0 + 16384].rearrange(
                    "p (c x) -> p c x", c=16)[:, :, k * 128:(k + 1) * 128],
                scalar1=gsc[:, k:k + 1], scalar2=gbi[:, k:k + 1],
                op0=ALU.mult, op1=ALU.add),
                r=[(bh, KT0, KT0 + 16384), (bgn_, 60 + k, 61 + k),
                   (bgn_, 70 + k, 71 + k)],
                w=[(bh, KT0, KT0 + 16384)], tag=f"gna{k}")
        P.dma("pool", lambda e: e.dma_start(
            out=A_w[:].rearrange("p (k n) -> p k n", k=8),
            in_=wout_in.rearrange("(k p) n -> p k n", p=128)),
            r=[(bw, 0, 8192)], w=[(bw, 0, 8192)], tag="wlo")
        for mt in range(NT):
            ps = (P5, P6)[mt % 2]
            pb = bp[id(ps)]
            for half in range(2):
                for k in range(8):
                    P.op("pe", lambda e, ps=ps, mt=mt, k=k, half=half: e.matmul(
                        out=ps[:, half * 512:(half + 1) * 512],
                        lhsT=A_h[:, KT0 + mt * 1024 + k * 128:KT0 + mt * 1024 + (k + 1) * 128],
                        rhs=A_w[:, k * 1024 + half * 512:k * 1024 + (half + 1) * 512],
                        start=(k == 0), stop=(k == 7)),
                        r=[(bh, KT0 + mt * 1024 + k * 128, KT0 + mt * 1024 + (k + 1) * 128),
                           (bw, k * 1024 + half * 512, k * 1024 + (half + 1) * 512)],
                        w=[(pb, half * 512, (half + 1) * 512)], tag=f"wo{mt}")
            P.op("dve", lambda e, ps=ps, mt=mt: e.tensor_add(
                out=A_res[:, mt * 1024:(mt + 1) * 1024],
                in0=A_res[:, mt * 1024:(mt + 1) * 1024], in1=ps[:]),
                r=[(pb, 0, 1024), (bres, mt * 1024, (mt + 1) * 1024)],
                w=[(bres, mt * 1024, (mt + 1) * 1024)], tag=f"woa{mt}")

      if stage >= 3:
        ln_phase("ln3")
        ffn_phase(first=False)
    except _EndAttn:
        pass

    for g in range(4):
        P.dma("act", lambda e, g=g: e.dma_start(
            out=y_out.rearrange("(t p) d -> p t d", p=128)[:, g * 4:(g + 1) * 4, :],
            in_=A_res[:].rearrange("p (t n) -> p t n", t=NT)[:, g * 4:(g + 1) * 4, :]),
            r=[(bres, g * 4 * 1024, (g + 1) * 4 * 1024)],
            w=[(bout, g, g + 1)], tag=f"sto{g}")

    P.resolve()
    P.emit(nc, ctx)
    ctx.close()
    return nc



def _consts():
    gam = np.array([1.0 - 2.0 ** (-5.0 - math.floor(8.0 * i / H)) for i in range(H)],
                   dtype=np.float64)
    a = np.arange(CH, dtype=np.float64)
    qs = np.zeros((128, 8 * 128), np.float64)
    ks = np.zeros((128, 8 * 128), np.float64)
    for dt in range(8):
        for par in range(2):
            h = dt * 2 + par
            qs[par * 64:(par + 1) * 64, dt * 128:(dt + 1) * 128] = \
                (gam[h] ** a)[None, :] / 8.0
    for h in range(H):
        ks[:, h * 64:(h + 1) * 64] = (gam[h] ** (-a))[:, None]
    causal = (np.arange(CH)[None, :] >= np.arange(CH)[:, None]).astype(np.float32)
    ident = np.eye(128, dtype=np.float32)
    gdec = np.zeros((128, 512), np.float64)
    for h in range(H):
        gdec[(h % 2) * 64:(h % 2) * 64 + 64, (h // 2) * 64:(h // 2) * 64 + 64] = \
            gam[h] ** CH
    hs = np.zeros((128, 8 * 16), np.float32)
    for k in range(8):
        for par in range(2):
            hs[par * 64:(par + 1) * 64, k * 16 + 2 * k + par] = 1.0
    he = np.zeros((16, 8 * 128), np.float32)
    for k in range(8):
        for par in range(2):
            he[2 * k + par, k * 128 + par * 64:k * 128 + (par + 1) * 64] = 1.0
    return gam, qs, ks, causal, ident, gdec, hs, he


def _prefw(core, gam):
    pos = core % 4
    w = np.zeros((4, 128, 512), np.float64)
    for r in range(4):
        if r < pos:
            for h in range(H):
                w[r, (h % 2) * 64:(h % 2) * 64 + 64,
                  (h // 2) * 64:(h // 2) * 64 + 64] = gam[h] ** (T * (pos - 1 - r))
    return w.astype(np.float32)


def kernel(L0_attn_x, L0_attn_Wq, L0_attn_Wk, L0_attn_Wv, L0_attn_gn_weight,
           L0_attn_gn_bias, L0_attn_Wout, L0_ffn_w1, L0_ffn_b1, L0_ffn_w2,
           L0_ffn_b2):
    import ml_dtypes
    x = np.ascontiguousarray(np.asarray(L0_attn_x, dtype=np.float32))
    w1 = np.ascontiguousarray(np.asarray(L0_ffn_w1, dtype=np.float32))
    w2 = np.ascontiguousarray(np.asarray(L0_ffn_w2, dtype=np.float32))
    b1 = np.asarray(L0_ffn_b1, dtype=np.float32)
    b2 = np.asarray(L0_ffn_b2, dtype=np.float32)
    wq = np.asarray(L0_attn_Wq, dtype=np.float32)
    wk = np.asarray(L0_attn_Wk, dtype=np.float32)
    wv = np.asarray(L0_attn_Wv, dtype=np.float32)
    wout = np.ascontiguousarray(np.asarray(L0_attn_Wout, dtype=np.float32))
    gnw = np.asarray(L0_attn_gn_weight, dtype=np.float32)
    gnb = np.asarray(L0_attn_gn_bias, dtype=np.float32)

    gam, qs, ks, causal, ident, gdec, hs, he = _consts()
    wqkv = np.stack([w.transpose(1, 0, 2).reshape(D, D) for w in (wq, wk, wv)])
    wqkv = np.ascontiguousarray(wqkv)
    b1l = np.ascontiguousarray(b1.reshape(32, 128).T)
    gnw2 = np.ascontiguousarray(gnw.reshape(8, 128).T)
    gnb2 = np.ascontiguousarray(gnb.reshape(8, 128).T)

    shared = {
        "w1": w1, "w2": w2, "b1l": b1l, "b2r": b2[None, :].copy(),
        "wqkv": wqkv, "wout": wout, "gnw": gnw2, "gnb": gnb2,
        "qscale": qs.astype(np.float32),
        "kscale": ks.astype(np.float32),
        "causal": causal, "ident": ident.astype(ml_dtypes.bfloat16),
        "gdecay": gdec.astype(np.float32),
        "hsum16": hs, "hexp": he.astype(ml_dtypes.bfloat16),
        "ones1": np.ones((1, 128), ml_dtypes.bfloat16),
    }
    in_maps = []
    for core in range(NCORES):
        b = core // 4
        t0 = (core % 4) * T
        m = dict(shared)
        m["x"] = np.ascontiguousarray(x[b, t0:t0 + T, :])
        m["prefw"] = _prefw(core, gam)
        in_maps.append(m)

    nc = build_nc(_STAGE, _APARTS, _AP5, _DBG)
    res = run_bass_kernel_spmd(nc, in_maps, core_ids=list(range(NCORES)))
    out = np.empty((B, L, D), dtype=np.float32)
    for core in range(NCORES):
        b = core // 4
        t0 = (core % 4) * T
        out[b, t0:t0 + T, :] = res.results[core]["y"]
    return out


# version 02
# speedup vs baseline: 1.0831x; 1.0831x over previous
import math
import numpy as np
from contextlib import ExitStack

import concourse.bass as bass
import concourse.mybir as mybir
from concourse.bass_utils import run_bass_kernel_spmd

F32 = mybir.dt.float32
BF16 = mybir.dt.bfloat16
AF = mybir.ActivationFunctionType
ALU = mybir.AluOpType
AX = mybir.AxisListType

NCORES = 8
B, L, D = 2, 8192, 1024
T = 2048
NT = 16
H, E = 16, 64
HID = 4096
CH = 128
NCH = 16
EPS = 1e-5
GN_N = float(E * L)

_STAGE = 3
_APARTS = 8
_AP5 = "both"
_DBG = 0


ENGINES = ("pe", "act", "dve", "pool", "sp")
DMA_QUEUES = ("sp", "act", "pool")
NLANES = 8


class _Op:
    __slots__ = ("idx", "eng", "fn", "kind", "reads", "writes", "qidx",
                 "deps", "marked", "inc_no", "ccsem", "tag", "wl", "nop_after")

    def __init__(self, idx, eng, fn, kind, reads, writes, tag):
        self.nop_after = 0
        self.idx = idx
        self.eng = eng
        self.fn = fn
        self.kind = kind
        self.reads = reads
        self.writes = writes
        self.qidx = None
        self.deps = set()
        self.marked = False
        self.inc_no = None
        self.ccsem = None
        self.tag = tag


class Buf:
    __slots__ = ("name", "writes", "reads")

    def __init__(self, name):
        self.name = name
        self.writes = []
        self.reads = []


class Prog:
    def __init__(self):
        self.ops = []
        self.bufs = {}

    def buf(self, name):
        b = self.bufs.get(name)
        if b is None:
            b = self.bufs[name] = Buf(name)
        return b

    def op(self, eng, fn, r=(), w=(), kind="c", tag="", nop=0):
        o = _Op(len(self.ops), eng, fn, kind, list(r), list(w), tag)
        o.nop_after = nop
        self.ops.append(o)
        return o

    def dma(self, queue, fn, r=(), w=(), tag=""):
        return self.op(queue, fn, r=r, w=w, kind="d", tag=tag)

    def cc(self, fn, r=(), w=(), tag=""):
        return self.op("pool", fn, r=r, w=w, kind="cc", tag=tag)

    def resolve(self):
        for o in self.ops:
            for (b, lo, hi) in o.reads:
                for seg in b.writes:
                    if seg[0] < hi and lo < seg[1]:
                        o.deps.add(seg[2])
            for (b, lo, hi) in o.writes:
                for seg in b.writes:
                    if seg[0] < hi and lo < seg[1]:
                        o.deps.add(seg[2])
                for seg in b.reads:
                    if seg[0] < hi and lo < seg[1]:
                        o.deps.add(seg[2])
            for (b, lo, hi) in o.writes:
                b.writes = [s for s in b.writes if not (lo <= s[0] and s[1] <= hi)]
                b.reads = [s for s in b.reads if not (lo <= s[0] and s[1] <= hi)]
                b.writes.append([lo, hi, o.idx])
            for (b, lo, hi) in o.reads:
                b.reads.append([lo, hi, o.idx])

        qctr = {q: 0 for q in DMA_QUEUES}
        for o in self.ops:
            if o.kind == "d":
                o.qidx = qctr[o.eng]
                qctr[o.eng] += 1
        self.qtotal = dict(qctr)

        for o in self.ops:
            for d in o.deps:
                p = self.ops[d]
                if p.kind == "c" and (p.eng != o.eng or p.eng != "pe"):
                    p.marked = True
        inc_ctr = {e: 0 for e in ENGINES}
        ncc = 0
        for o in self.ops:
            if o.kind == "c" and o.marked:
                inc_ctr[o.eng] += 1
                o.inc_no = inc_ctr[o.eng]
            elif o.kind == "cc":
                o.ccsem = ncc
                ncc += 1
        self.inc_total = inc_ctr
        self.ncc = ncc

        best = {e: {} for e in ENGINES}
        for o in self.ops:
            waits = {}
            for d in sorted(o.deps):
                p = self.ops[d]
                if p.kind == "d":
                    key = ("q", p.eng, p.qidx % NLANES)
                    val = 16 * (p.qidx // NLANES + 1)
                elif p.kind == "cc":
                    key = ("cc", p.ccsem)
                    val = 1
                else:
                    if p.eng == o.eng and p.eng == "pe":
                        continue
                    key = ("e", p.eng)
                    val = p.inc_no
                    assert val is not None
                if waits.get(key, -1) < val:
                    waits[key] = val
            myb = best[o.eng]
            o.wl = []
            for key, val in sorted(waits.items()):
                if myb.get(key, 0) >= val:
                    continue
                myb[key] = val
                o.wl.append((key, val))

    def emit(self, nc, ctx):
        esems = {e: ctx.enter_context(nc.semaphore(f"es_{e}")) for e in ENGINES}
        qsems = {q: [ctx.enter_context(nc.semaphore(f"qs_{q}{l}"))
                     for l in range(NLANES)] for q in DMA_QUEUES}
        ccsems = [ctx.enter_context(nc.semaphore(f"cc_{i}")) for i in range(self.ncc)]

        def semof(key):
            if key[0] == "e":
                return esems[key[1]]
            if key[0] == "q":
                return qsems[key[1]][key[2]]
            return ccsems[key[1]]

        per_eng = {e: [] for e in ENGINES}
        for o in self.ops:
            per_eng[o.eng].append(o)

        block = ctx.enter_context(nc.Block())

        def runner(eng_name):
            def run(eng):
                for o in per_eng[eng_name]:
                    for key, val in o.wl:
                        eng.wait_ge(semof(key), val)
                    if o.kind == "d":
                        lane = o.qidx % NLANES
                        use = o.qidx // NLANES
                        if use > 0:
                            eng.wait_ge(qsems[o.eng][lane], 16 * use)
                        inst = o.fn(eng)
                        inst.then_inc(qsems[o.eng][lane], 16)
                        continue
                    inst = o.fn(eng)
                    if o.kind == "cc":
                        inst.then_inc(ccsems[o.ccsem])
                    elif o.marked:
                        inst.then_inc(esems[o.eng], 1)
                    if o.nop_after:
                        eng.drain()
                if eng_name in DMA_QUEUES and self.qtotal[eng_name]:
                    tot = self.qtotal[eng_name]
                    for l in range(min(NLANES, tot)):
                        n_l = (tot - 1 - l) // NLANES + 1
                        eng.wait_ge(qsems[eng_name][l], 16 * n_l)
            return run

        block.tensor(runner("pe"))
        block.scalar(runner("act"))
        block.vector(runner("dve"))
        block.gpsimd(runner("pool"))
        block.sync(runner("sp"))



def build_nc(stage, aparts=4, ap5="both", dbg=0):
    nc = bass.Bass("TRN2", target_bir_lowering=False)
    P = Prog()

    d_in = {}
    def din(name, shape, dtype=F32):
        d_in[name] = nc.dram_tensor(name, shape, dtype, kind="ExternalInput").ap()
        return d_in[name]

    x_in = din("x", [T, D])
    w1_in = din("w1", [HID, D])
    w2_in = din("w2", [D, HID])
    b1_in = din("b1l", [128, 32])
    b2_in = din("b2r", [1, D])
    wqkv_in = din("wqkv", [3, D, D])
    wout_in = din("wout", [D, D])
    gnw_in = din("gnw", [128, 8])
    gnb_in = din("gnb", [128, 8])
    qscale_in = din("qscale", [128, 8 * 128])
    kscale_in = din("kscale", [128, 8 * 128])
    causal_in = din("causal", [128, 128])
    ident_in = din("ident", [128, 128], BF16)
    gdecay_in = din("gdecay", [128, 512])
    prefw_in = din("prefw", [4, 128, 512])
    hsum_in = din("hsum16", [128, 8 * 16])
    hexp_in = din("hexp", [16, 8 * 128], BF16)
    ones_in = din("ones1", [1, 128], BF16)

    y_out = nc.dram_tensor("y", [T, D], F32, kind="ExternalOutput").ap()
    if dbg:
        dbgB = nc.dram_tensor("dbgB", [6, 128, 1024], F32, kind="ExternalOutput").ap()
        dbgF = nc.dram_tensor("dbgF", [3, 128, 512], F32, kind="ExternalOutput").ap()
        bdbg = None

    w1t_scr = nc.dram_tensor("w1t_scr", [32, 128, 1024], BF16).ap()
    w2t_scr = nc.dram_tensor("w2t_scr", [32, 128, 1024], BF16).ap()
    ag_in = nc.dram_tensor("ag_in", [128, 512], F32).ap()
    ag_out = nc.dram_tensor("ag_out", [4 * 128, 512], F32).ap()
    ar_in = nc.dram_tensor("ar_in", [16, 2], F32).ap()
    ar_out = nc.dram_tensor("ar_out", [16, 2], F32).ap()

    groups_state = [[0, 1, 2, 3], [4, 5, 6, 7]]

    ctx = ExitStack()
    sb = lambda n, s, d: ctx.enter_context(nc.sbuf_tensor("S_" + n, s, d))
    psal = lambda n, s, d: ctx.enter_context(nc.psum_tensor(n, s, d))

    A_res = sb("A_res", [128, NT * 1024], F32)
    A_x = sb("A_x", [128, 8 * 2048], BF16)
    A_h = sb("A_h", [128, 32768], BF16)
    A_w = sb("A_w", [128, 8192], BF16)
    A_s32 = sb("A_s32", [128, 1024], F32)
    A_s16 = sb("A_s16", [128, 1024], BF16)
    w1tb = sb("w1tb", [128, 2 * 1024], BF16)
    kdbuf = sb("kdbuf", [128, 1024], BF16)
    xlnT = kdbuf
    scb = sb("scb", [128, 1024], BF16)
    qksc = sb("qksc", [128, 1024], F32)
    stS = sb("stS", [128, 512], F32)
    stSb = sb("stSb", [128, 1024], BF16)
    gdecT = sb("gdecT", [128, 512], F32)
    causal = sb("causal", [128, 128], F32)
    ident = sb("ident", [128, 128], BF16)
    ones1 = sb("ones1", [1, 128], BF16)
    b2row = sb("b2row", [1, 1024], BF16)
    b1l = sb("b1l", [128, 32], F32)
    al1 = sb("al1", [128, 32], F32)
    al2 = sb("al2", [128, 8], F32)
    gnw2 = sb("gnw2", [128, 8], F32)
    gnb2 = sb("gnb2", [128, 8], F32)
    lnS1 = sb("lnS1", [128, 16], F32)
    lnS2 = sb("lnS2", [128, 16], F32)
    lnM = sb("lnM", [128, 16], F32)
    lnT = sb("lnT", [128, 16], F32)
    lnR = sb("lnR", [128, 16], F32)
    lnB = sb("lnB", [128, 16], F32)
    gnsum = sb("gnsum", [128, 8], F32)
    gnsq = sb("gnsq", [128, 8], F32)
    gnst = sb("gnst", [128, 16], F32)
    st16 = sb("st16", [16, 2], F32)
    st16g = sb("st16g", [16, 2], F32)
    me16 = sb("me16", [16, 2], F32)
    v16 = sb("v16", [16, 1], F32)
    t16 = sb("t16", [16, 1], F32)
    mr16 = sb("mr16", [16, 2], F32)
    mrk = sb("mrk", [128, 2], F32)
    gsc = sb("gsc", [128, 8], F32)
    gbi = sb("gbi", [128, 8], F32)
    hsum = sb("hsum", [128, 8 * 16], F32)
    hexp = sb("hexp", [16, 8 * 128], BF16)
    mr16b = sb("mr16b", [16, 2], BF16)

    P1 = psal("P1", [128, 512], F32)
    P2 = psal("P2", [128, 512], F32)
    P3 = psal("P3", [128, 1024], BF16)
    P4 = psal("P4", [128, 512], F32)
    P5 = psal("P5", [128, 1024], F32)
    P6 = psal("P6", [128, 1024], F32)

    bres = P.buf("A_res"); bx = P.buf("A_x"); bh = P.buf("A_h"); bw = P.buf("A_w")
    bs32 = P.buf("A_s32"); bs16 = P.buf("A_s16"); bw1tb = P.buf("w1tb")
    bkd = P.buf("kdbuf"); bxlnT = bkd; bscb = P.buf("scb")
    bstS = P.buf("stS"); bstSb = P.buf("stSb")
    bp = {id(P1): P.buf("P1"), id(P2): P.buf("P2"), id(P3): P.buf("P3"),
          id(P4): P.buf("P4"), id(P5): P.buf("P5"), id(P6): P.buf("P6")}
    bconst = P.buf("consts")
    bln = P.buf("lnstats")
    bal = P.buf("alphas")
    bgn = P.buf("gnstats")
    bscr1 = P.buf("w1t_scr")
    bscr2 = P.buf("w2t_scr")
    bdram = P.buf("dram_cc")
    bout = P.buf("yout")


    def ld_const(dst_ap, src_ap, q="sp"):
        P.dma(q, lambda e, o=dst_ap, i=src_ap: e.dma_start(out=o, in_=i),
              w=[(bconst, 0, 1)], tag="ldc")

    ld_const(b1l[:], b1_in)
    ld_const(gnw2[:], gnw_in)
    ld_const(gnb2[:], gnb_in)
    bqk = P.buf("qksc")
    P.dma("sp", lambda e: e.dma_start(out=qksc[:], in_=kscale_in),
          w=[(bqk, 0, 1024)], tag="ldksc")
    ld_const(causal[:], causal_in)
    ld_const(ident[:], ident_in)
    ld_const(gdecT[:], gdecay_in)
    ld_const(hsum[:], hsum_in)
    ld_const(hexp[:], hexp_in)
    ld_const(ones1[:], ones_in)

    P.dma("sp", lambda e: e.dma_start(
        out=A_res[:].rearrange("p (t n) -> p t n", t=NT),
        in_=x_in.rearrange("(t p) d -> p t d", p=128)),
        w=[(bres, 0, NT * 1024)], tag="ldx")

    P.dma("pool", lambda e: e.dma_start(out=b2row[:], in_=b2_in),
          w=[(P.buf("b2row"), 0, 1024)], tag="b2cast")
    bb2 = P.buf("b2row")

    def ln_phase(phase):
        for i in range(NT):
            rt = (bres, i * 1024, (i + 1) * 1024)
            P.op("dve", lambda e, i=i: e.tensor_reduce(
                out=lnS1[:, i:i + 1], in_=A_res[:, i * 1024:(i + 1) * 1024],
                axis=AX.X, op=ALU.add),
                r=[rt], w=[(bln, i, i + 1)], tag=f"{phase}s1")
            P.op("act", lambda e, i=i: e.activation(
                out=xlnT[:], in_=A_res[:, i * 1024:(i + 1) * 1024],
                func=AF.Square, accum_out=lnS2[:, i:i + 1]),
                r=[rt], w=[(bln, 16 + i, 17 + i), (bxlnT, 0, 1024)], tag=f"{phase}s2")
        P.op("dve", lambda e: e.tensor_scalar(
            out=lnM[:], in0=lnS1[:], scalar1=1.0 / D, scalar2=None, op0=ALU.mult),
            r=[(bln, 0, 16)], w=[(bln, 32, 48)], tag=f"{phase}mean")
        P.op("dve", lambda e: e.tensor_mul(out=lnT[:], in0=lnM[:], in1=lnM[:]),
            r=[(bln, 32, 48)], w=[(bln, 48, 64)], tag=f"{phase}m2")
        P.op("dve", lambda e: e.tensor_scalar(
            out=lnS2[:], in0=lnS2[:], scalar1=1.0 / D, scalar2=None, op0=ALU.mult),
            r=[(bln, 16, 32)], w=[(bln, 16, 32)], tag=f"{phase}ex2")
        P.op("dve", lambda e: e.tensor_sub(out=lnT[:], in0=lnS2[:], in1=lnT[:]),
            r=[(bln, 16, 64)], w=[(bln, 48, 64)], tag=f"{phase}var")
        P.op("act", lambda e: e.activation(out=lnT[:], in_=lnT[:], func=AF.Ln, bias=np.float32(EPS)),
            r=[(bln, 48, 64)], w=[(bln, 48, 64)], tag=f"{phase}ln")
        P.op("act", lambda e: e.activation(out=lnR[:], in_=lnT[:], func=AF.Exp, scale=-0.5),
            r=[(bln, 48, 64)], w=[(bln, 64, 80)], tag=f"{phase}rstd")
        P.op("dve", lambda e: e.tensor_mul(out=lnB[:], in0=lnM[:], in1=lnR[:]),
            r=[(bln, 32, 48), (bln, 64, 80)], w=[(bln, 80, 96)], tag=f"{phase}mb")
        P.op("dve", lambda e: e.tensor_scalar(
            out=lnB[:], in0=lnB[:], scalar1=-1.0, scalar2=None, op0=ALU.mult),
            r=[(bln, 80, 96)], w=[(bln, 80, 96)], tag=f"{phase}nb")
        for i in range(NT):
            P.op("dve", lambda e, i=i: e.tensor_scalar(
                out=xlnT[:], in0=A_res[:, i * 1024:(i + 1) * 1024],
                scalar1=lnR[:, i:i + 1], scalar2=lnB[:, i:i + 1],
                op0=ALU.mult, op1=ALU.add),
                r=[(bres, i * 1024, (i + 1) * 1024), (bln, 64, 96)],
                w=[(bxlnT, 0, 1024)], tag=f"{phase}app{i}")
            for k in range(8):
                P.op("pe", lambda e, k=k: e.transpose(
                    out=P3[:, k * 128:(k + 1) * 128],
                    in_=xlnT[:, k * 128:(k + 1) * 128], identity=ident[:]),
                    r=[(bxlnT, k * 128, (k + 1) * 128), (bconst, 0, 1)],
                    w=[(bp[id(P3)], k * 128, (k + 1) * 128)], tag=f"{phase}tp")
            P.op("act", lambda e, i=i: e.activation(
                out=A_x[:].rearrange("p (k n) -> p k n", k=8)[:, :, i * 128:(i + 1) * 128],
                in_=P3[:].rearrange("p (k n) -> p k n", k=8), func=AF.Copy),
                r=[(bp[id(P3)], 0, 1024)],
                w=[(bx, k * 2048 + i * 128, k * 2048 + (i + 1) * 128) for k in range(8)],
                tag=f"{phase}xev{i}")

    bmrk = P.buf("mrk")

    def w1_prep(ms):
        slot = (ms % 2) * 1024
        P.dma("sp", lambda e, ms=ms: e.dma_start(
            out=A_s32[:, 0:1024], in_=w1_in[ms * 128:(ms + 1) * 128, :]),
            w=[(bs32, 0, 1024)], tag=f"w1ld{ms}")
        P.op("dve", lambda e, ms=ms: e.tensor_reduce(
            out=al1[:, ms:ms + 1], in_=A_s32[:, 0:1024], axis=AX.X,
            op=ALU.add, apply_absolute_value=True),
            r=[(bs32, 0, 1024)], w=[(bal, ms, ms + 1)], tag=f"w1al{ms}")
        P.op("act", lambda e: e.activation(
            out=A_s16[:, 0:1024], in_=A_s32[:, 0:1024], func=AF.Sign),
            r=[(bs32, 0, 1024)], w=[(bs16, 0, 1024)], tag=f"w1sg{ms}")
        P.op("dve", lambda e, ms=ms: e.tensor_scalar(
            out=A_s16[:, 0:1024], in0=A_s16[:, 0:1024],
            scalar1=al1[:, ms:ms + 1], scalar2=1.0 / D, op0=ALU.mult, op1=ALU.mult),
            r=[(bs16, 0, 1024), (bal, ms, ms + 1)], w=[(bs16, 0, 1024)], tag=f"w1sc{ms}")
        for k in range(8):
            P.op("pe", lambda e, k=k: e.transpose(
                out=P3[:, k * 128:(k + 1) * 128],
                in_=A_s16[:, k * 128:(k + 1) * 128], identity=ident[:]),
                r=[(bs16, k * 128, (k + 1) * 128), (bconst, 0, 1)],
                w=[(bp[id(P3)], k * 128, (k + 1) * 128)], tag=f"w1tp{ms}")
        P.op("act", lambda e, slot=slot: e.activation(
            out=w1tb[:, slot:slot + 1024], in_=P3[:], func=AF.Copy),
            r=[(bp[id(P3)], 0, 1024)], w=[(bw1tb, slot, slot + 1024)], tag=f"w1ev{ms}")
        P.dma("act", lambda e, ms=ms, slot=slot: e.dma_start(
            out=w1t_scr[ms], in_=w1tb[:, slot:slot + 1024]),
            r=[(bw1tb, slot, slot + 1024)], w=[(bscr1, ms, ms + 1)], tag=f"w1st{ms}")

    def w2_alpha(kd):
        for qq in range(4):
            P.dma("pool", lambda e, kd=kd, qq=qq: e.dma_start(
                out=A_s32[:, 0:1024],
                in_=w2_in[kd * 128:(kd + 1) * 128, qq * 1024:(qq + 1) * 1024]),
                w=[(bs32, 0, 1024)], tag=f"w2a_ld{kd}_{qq}")
            if qq == 0:
                P.op("dve", lambda e, kd=kd: e.tensor_reduce(
                    out=al2[:, kd:kd + 1], in_=A_s32[:, 0:1024], axis=AX.X,
                    op=ALU.add, apply_absolute_value=True),
                    r=[(bs32, 0, 1024)], w=[(bal, 40 + kd, 41 + kd)], tag=f"w2a1{kd}")
            else:
                P.op("dve", lambda e: e.tensor_reduce(
                    out=mrk[:, 0:1], in_=A_s32[:, 0:1024], axis=AX.X,
                    op=ALU.add, apply_absolute_value=True),
                    r=[(bs32, 0, 1024)], w=[(bmrk, 0, 1)], tag=f"w2a2{kd}_{qq}")
                P.op("dve", lambda e, kd=kd: e.tensor_add(
                    out=al2[:, kd:kd + 1], in0=al2[:, kd:kd + 1], in1=mrk[:, 0:1]),
                    r=[(bal, 40 + kd, 41 + kd), (bmrk, 0, 1)],
                    w=[(bal, 40 + kd, 41 + kd)], tag=f"w2aa{kd}_{qq}")
        P.op("dve", lambda e, kd=kd: e.tensor_scalar(
            out=al2[:, kd:kd + 1], in0=al2[:, kd:kd + 1],
            scalar1=1.0 / HID, scalar2=None, op0=ALU.mult),
            r=[(bal, 40 + kd, 41 + kd)], w=[(bal, 40 + kd, 41 + kd)], tag=f"w2am{kd}")

    def w2_prep2(kq):
        q, kd = divmod(kq, 8)
        P.dma("pool", lambda e, kd=kd, q=q: e.dma_start(
            out=A_s32[:, 0:1024],
            in_=w2_in[kd * 128:(kd + 1) * 128, q * 1024:(q + 1) * 1024]),
            w=[(bs32, 0, 1024)], tag=f"w2ld{kq}")
        P.op("act", lambda e: e.activation(
            out=A_s16[:, 0:1024], in_=A_s32[:, 0:1024], func=AF.Sign),
            r=[(bs32, 0, 1024)], w=[(bs16, 0, 1024)], tag=f"w2sg{kq}")
        P.op("dve", lambda e, kd=kd: e.tensor_scalar(
            out=A_s16[:, 0:1024], in0=A_s16[:, 0:1024],
            scalar1=al2[:, kd:kd + 1], scalar2=None, op0=ALU.mult),
            r=[(bs16, 0, 1024), (bal, 40 + kd, 41 + kd)],
            w=[(bs16, 0, 1024)], tag=f"w2sc{kq}")
        for j in range(8):
            P.op("pe", lambda e, j=j: e.transpose(
                out=P3[:, j * 128:(j + 1) * 128],
                in_=A_s16[:, j * 128:(j + 1) * 128], identity=ident[:]),
                r=[(bs16, j * 128, (j + 1) * 128), (bconst, 0, 1)],
                w=[(bp[id(P3)], j * 128, (j + 1) * 128)], tag=f"w2tp{kq}")
        P.op("act", lambda e: e.activation(
            out=kdbuf[:], in_=P3[:], func=AF.Copy),
            r=[(bp[id(P3)], 0, 1024)], w=[(bkd, 0, 1024)], tag=f"w2ev{kq}")
        P.dma("act", lambda e, kd=kd, q=q: e.dma_start(
            out=w2t_scr[q * 8:(q + 1) * 8, :, kd * 128:(kd + 1) * 128].rearrange(
                "j p d -> p j d"),
            in_=kdbuf[:].rearrange("p (j d) -> p j d", j=8)),
            r=[(bkd, 0, 1024)],
            w=[(bscr2, (q * 8 + j) * 8 + kd, (q * 8 + j) * 8 + kd + 1)
               for j in range(8)],
            tag=f"w2st{kq}")

    def ffn_phase(first):
        if first:
            for kd in range(8):
                w2_alpha(kd)
        w2q_emitted = 0
        for pair in range(2):
            for mloc in range(16):
                ms = pair * 16 + mloc
                slot = (ms % 2) * 1024
                if first:
                    w1_prep(ms)
                    if w2q_emitted < 32:
                        w2_prep2(w2q_emitted)
                        w2q_emitted += 1
                else:
                    P.dma("sp", lambda e, ms=ms, slot=slot: e.dma_start(
                        out=w1tb[:, slot:slot + 1024], in_=w1t_scr[ms]),
                        r=[(bscr1, ms, ms + 1)], w=[(bw1tb, slot, slot + 1024)],
                        tag=f"w1rd{ms}")
                for n in range(4):
                    ps = (P1, P2)[n % 2]
                    pb = bp[id(ps)]
                    for k in range(8):
                        P.op("pe", lambda e, ps=ps, slot=slot, k=k, n=n: e.matmul(
                            out=ps[:, 0:512],
                            lhsT=w1tb[:, slot + k * 128:slot + (k + 1) * 128],
                            rhs=A_x[:, k * 2048 + n * 512:k * 2048 + (n + 1) * 512],
                            start=(k == 0), stop=(k == 7)),
                            r=[(bw1tb, slot + k * 128, slot + (k + 1) * 128),
                               (bx, k * 2048 + n * 512, k * 2048 + (n + 1) * 512)],
                            w=[(pb, 0, 512)], tag=f"mm1_{ms}_{n}")
                    P.op("act", lambda e, ps=ps, mloc=mloc, n=n, ms=ms: e.activation(
                        out=A_h[:, mloc * 2048 + n * 512:mloc * 2048 + (n + 1) * 512],
                        in_=ps[:, 0:512], func=AF.Gelu, bias=b1l[:, ms:ms + 1]),
                        r=[(pb, 0, 512), (bconst, 0, 1)],
                        w=[(bh, mloc * 2048 + n * 512, mloc * 2048 + (n + 1) * 512)],
                        tag=f"gelu_{ms}_{n}")
            for q in range(2):
                s = pair * 2 + q
                P.dma("sp", lambda e, s=s: e.dma_start(
                    out=A_w[:].rearrange("p (j d) -> p j d", j=8),
                    in_=w2t_scr[s * 8:(s + 1) * 8].rearrange("j p d -> p j d")),
                    r=[(bscr2, s * 8 * 8, (s + 1) * 8 * 8)],
                    w=[(bw, 0, 8192)], tag=f"w2rd{s}")
                for mt in range(NT):
                    ps = (P5, P6)[mt % 2]
                    pb = bp[id(ps)]
                    nfirst = True
                    if s == 0:
                        for half in range(2):
                            P.op("pe", lambda e, ps=ps, half=half: e.matmul(
                                out=ps[:, half * 512:(half + 1) * 512],
                                lhsT=ones1[:],
                                rhs=b2row[:, half * 512:(half + 1) * 512],
                                start=True, stop=False),
                                r=[(bconst, 0, 1), (bb2, 0, 1024)],
                                w=[(pb, half * 512, (half + 1) * 512)],
                                tag=f"b2_{mt}")
                        nfirst = False
                    for j in range(8):
                        mloc = q * 8 + j
                        for half in range(2):
                            P.op("pe", lambda e, ps=ps, mloc=mloc, mt=mt, j=j,
                                 half=half, st=(nfirst and j == 0): e.matmul(
                                out=ps[:, half * 512:(half + 1) * 512],
                                lhsT=A_h[:, mloc * 2048 + mt * 128:mloc * 2048 + (mt + 1) * 128],
                                rhs=A_w[:, j * 1024 + half * 512:j * 1024 + (half + 1) * 512],
                                start=st, stop=(j == 7)),
                                r=[(bh, mloc * 2048 + mt * 128, mloc * 2048 + (mt + 1) * 128),
                                   (bw, j * 1024 + half * 512, j * 1024 + (half + 1) * 512)],
                                w=[(pb, half * 512, (half + 1) * 512)],
                                tag=f"mm2_{s}_{mt}")
                    P.op("dve", lambda e, ps=ps, mt=mt: e.scalar_tensor_tensor(
                        out=A_res[:, mt * 1024:(mt + 1) * 1024], in0=ps[:],
                        scalar=0.5, in1=A_res[:, mt * 1024:(mt + 1) * 1024],
                        op0=ALU.mult, op1=ALU.add),
                        r=[(pb, 0, 1024), (bres, mt * 1024, (mt + 1) * 1024)],
                        w=[(bres, mt * 1024, (mt + 1) * 1024)], tag=f"racc_{s}_{mt}")

    ln_phase("ln1")
    ffn_phase(first=True)

    class _EndAttn(Exception):
        pass
    try:
      if stage >= 2:
        bag = P.buf("dram_cc")
        KT0, VB0 = 0, 16384
        bdbgq = P.buf("dbgq")

        def dump16(idx, buf, ap, lo, hi):
            if not dbg:
                return
            P.dma("pool", lambda e, idx=idx, ap=ap: e.dma_start(
                out=dbgB[idx], in_=ap), r=[(buf, lo, hi)],
                w=[(bdbgq, idx, idx + 1)], tag=f"dbg{idx}")

        def dump32(idx, buf, ap, lo, hi):
            if not dbg:
                return
            P.dma("pool", lambda e, idx=idx, ap=ap: e.dma_start(
                out=dbgF[idx], in_=ap), r=[(buf, lo, hi)],
                w=[(bdbgq, 16 + idx, 17 + idx)], tag=f"dbf{idx}")

        ln_phase("ln2")

        def proj_pass(widx, out_base, scale_evac):
            P.dma("pool", lambda e, widx=widx: e.dma_start(
                out=A_w[:].rearrange("p (k n) -> p k n", k=8),
                in_=wqkv_in[widx].rearrange("(k p) n -> p k n", p=128)),
                r=[(bw, 0, 8192)], w=[(bw, 0, 8192)], tag=f"wl{widx}")
            for c in range(NCH):
                ps = (P5, P6)[c % 2]
                pb = bp[id(ps)]
                for half in range(2):
                    for k in range(8):
                        P.op("pe", lambda e, ps=ps, c=c, k=k, half=half: e.matmul(
                            out=ps[:, half * 512:(half + 1) * 512],
                            lhsT=A_x[:, k * 2048 + c * 128:k * 2048 + (c + 1) * 128],
                            rhs=A_w[:, k * 1024 + half * 512:k * 1024 + (half + 1) * 512],
                            start=(k == 0), stop=(k == 7)),
                            r=[(bx, k * 2048 + c * 128, k * 2048 + (c + 1) * 128),
                               (bw, k * 1024 + half * 512, k * 1024 + (half + 1) * 512)],
                            w=[(pb, half * 512, (half + 1) * 512)], tag=f"pj{widx}_{c}")
                scale_evac(c, ps, pb, out_base)

        def k_evac(c, ps, pb, base):
            P.op("dve", lambda e, ps=ps, c=c, base=base: e.tensor_tensor(
                out=A_h[:, base + c * 1024:base + (c + 1) * 1024],
                in0=ps[:], in1=qksc[:], op=ALU.mult),
                r=[(pb, 0, 1024), (bqk, 0, 1024)],
                w=[(bh, base + c * 1024, base + (c + 1) * 1024)], tag=f"kev{c}")

        def v_evac(c, ps, pb, base):
            P.op("act", lambda e, ps=ps, c=c, base=base: e.activation(
                out=A_h[:, base + c * 1024:base + (c + 1) * 1024],
                in_=ps[:], func=AF.Copy),
                r=[(pb, 0, 1024)],
                w=[(bh, base + c * 1024, base + (c + 1) * 1024)], tag=f"vev{c}")

        proj_pass(1, KT0, k_evac)
        proj_pass(2, VB0, v_evac)
        dump16(0, bh, A_h[:, KT0:KT0 + 1024], KT0, KT0 + 1024)
        dump16(1, bh, A_h[:, VB0:VB0 + 1024], VB0, VB0 + 1024)

        if aparts < 2:
            raise _EndAttn()
        P.op("dve", lambda e: e.memset(stS[:], 0.0),
             w=[(bstS, 0, 512)], tag="st0")
        bp4 = bp[id(P4)]

        def dS_ops(c, tagp):
            bp1_ = bp[id(P1)]
            for pair in range(8):
                ps = (P4, P1)[pair % 2]
                pbp = (bp4, bp1_)[pair % 2]
                reg = (pair // 2) * 128
                P.op("pe", lambda e, c=c, pair=pair, ps=ps, reg=reg: e.matmul(
                    out=ps[:, reg:reg + 128],
                    lhsT=A_h[:, KT0 + c * 1024 + pair * 128:KT0 + c * 1024 + (pair + 1) * 128],
                    rhs=A_h[:, VB0 + c * 1024 + pair * 128:VB0 + c * 1024 + (pair + 1) * 128],
                    start=True, stop=True),
                    r=[(bh, KT0 + c * 1024 + pair * 128, KT0 + c * 1024 + (pair + 1) * 128),
                       (bh, VB0 + c * 1024 + pair * 128, VB0 + c * 1024 + (pair + 1) * 128)],
                    w=[(pbp, reg, reg + 128)], tag=f"{tagp}dS{c}", nop=64)
            for par in range(2):
                for bsel in range(2):
                    ps = (P4, P1)[bsel]
                    pbp = (bp4, bp1_)[bsel]
                    P.op("dve", lambda e, par=par, ps=ps, bsel=bsel: e.tensor_add(
                        out=stS[par * 64:par * 64 + 64, :].rearrange(
                            "p (q v) -> p q v", v=64)[:, bsel::2, :],
                        in0=stS[par * 64:par * 64 + 64, :].rearrange(
                            "p (q v) -> p q v", v=64)[:, bsel::2, :],
                        in1=ps[par * 64:par * 64 + 64, :].rearrange(
                            "p (r v) -> p r v", v=64)[:, par::2, :]),
                        r=[(bstS, 0, 512), (pbp, 0, 512)], w=[(bstS, 0, 512)],
                        tag=f"{tagp}su1_{c}_{par}{bsel}")
            P.op("dve", lambda e: e.tensor_mul(out=stS[:], in0=stS[:], in1=gdecT[:]),
                 r=[(bstS, 0, 512), (bconst, 0, 1)], w=[(bstS, 0, 512)],
                 tag=f"{tagp}su2_{c}")
            if tagp == "b":
                for par in range(2):
                    P.op("act", lambda e, par=par: e.activation(
                        out=stSb[par * 64:par * 64 + 64, :].rearrange(
                            "p (h v) -> p h v", v=64)[:, par::2, :],
                        in_=stS[par * 64:par * 64 + 64, :].rearrange(
                            "p (h v) -> p h v", v=64),
                        func=AF.Copy),
                        r=[(bstS, 0, 512)], w=[(bstSb, 0, 1024)], tag=f"bsb{c}_{par}")

        for c in range(NCH):
            dS_ops(c, "a")

        dump32(0, bstS, stS[:], 0, 512)
        P.dma("pool", lambda e: e.dma_start(out=ag_in, in_=stS[:]),
              r=[(bstS, 0, 512)], w=[(bag, 0, 1)], tag="agw")
        P.cc(lambda e: e.collective_compute(
            "AllGather", ALU.bypass, replica_groups=groups_state,
            ins=[ag_in.opt()], outs=[ag_out.opt()]),
            r=[(bag, 0, 1)], w=[(bag, 1, 2)], tag="ag")
        P.op("dve", lambda e: e.memset(stS[:], 0.0),
             r=[(bag, 0, 1)], w=[(bstS, 0, 512)], tag="stz")
        for r_ in range(4):
            P.dma("sp", lambda e, r_=r_: e.dma_start(
                out=A_s32[:, 0:512], in_=ag_out[r_ * 128:(r_ + 1) * 128, :]),
                r=[(bag, 1, 2)], w=[(bs32, 0, 512)], tag=f"agr{r_}")
            P.dma("sp", lambda e, r_=r_: e.dma_start(
                out=A_s32[:, 512:1024], in_=prefw_in[r_]),
                w=[(bs32, 512, 1024)], tag=f"pwr{r_}")
            P.op("dve", lambda e: e.tensor_mul(
                out=A_s32[:, 0:512], in0=A_s32[:, 0:512], in1=A_s32[:, 512:1024]),
                r=[(bs32, 0, 1024)], w=[(bs32, 0, 512)], tag=f"pwm{r_}")
            P.op("dve", lambda e: e.tensor_add(
                out=stS[:], in0=stS[:], in1=A_s32[:, 0:512]),
                r=[(bstS, 0, 512), (bs32, 0, 512)], w=[(bstS, 0, 512)],
                tag=f"pwa{r_}")

        if aparts < 3:
            raise _EndAttn()
        P.op("dve", lambda e: e.memset(stSb[:], 0.0),
             w=[(bstSb, 0, 1024)], tag="sbz")
        for par in range(2):
            P.op("act", lambda e, par=par: e.activation(
                out=stSb[par * 64:par * 64 + 64, :].rearrange(
                    "p (h v) -> p h v", v=64)[:, par::2, :],
                in_=stS[par * 64:par * 64 + 64, :].rearrange(
                    "p (h v) -> p h v", v=64),
                func=AF.Copy),
                r=[(bstS, 0, 512)], w=[(bstSb, 0, 1024)], tag=f"sb0_{par}")
        dump32(1, bstS, stS[:], 0, 512)

        P.dma("pool", lambda e: e.dma_start(
            out=A_w[:].rearrange("p (k n) -> p k n", k=8),
            in_=wqkv_in[0].rearrange("(k p) n -> p k n", p=128)),
            r=[(bw, 0, 8192)], w=[(bw, 0, 8192)], tag="wlq")
        P.dma("sp", lambda e: e.dma_start(out=qksc[:], in_=qscale_in),
              r=[(bqk, 0, 1024)], w=[(bqk, 0, 1024)], tag="ldqsc")
        bscb_ = P.buf("scb")
        bgn_ = P.buf("gnacc")
        bgnst = P.buf("gnst_t")
        P.op("dve", lambda e: e.memset(gnsum[:], 0.0), w=[(bgn_, 0, 8)], tag="gz1")
        P.op("dve", lambda e: e.memset(gnsq[:], 0.0), w=[(bgn_, 8, 16)], tag="gz2")
        bp6 = bp[id(P6)]
        bp5 = bp[id(P5)]
        for c in range(NCH):
            for dt in (0, 4, 1, 5, 2, 6, 3, 7):
                for k in range(8):
                    P.op("pe", lambda e, c=c, dt=dt, k=k: e.matmul(
                        out=P6[:, dt * 128:(dt + 1) * 128],
                        lhsT=A_w[:, k * 1024 + dt * 128:k * 1024 + (dt + 1) * 128],
                        rhs=A_x[:, k * 2048 + c * 128:k * 2048 + (c + 1) * 128],
                        start=(k == 0), stop=(k == 7)),
                        r=[(bw, k * 1024 + dt * 128, k * 1024 + (dt + 1) * 128),
                           (bx, k * 2048 + c * 128, k * 2048 + (c + 1) * 128)],
                        w=[(bp6, dt * 128, (dt + 1) * 128)], tag=f"qp{c}")
            P.op("dve", lambda e: e.tensor_tensor(
                out=A_s16[:, 0:1024], in0=P6[:], in1=qksc[:], op=ALU.mult),
                r=[(bp6, 0, 1024), (bqk, 0, 1024)], w=[(bs16, 0, 1024)],
                tag=f"qev{c}")
            if c == 0:
                dump16(2, bs16, A_s16[:, 0:1024], 0, 1024)
            for k in range(8):
                P.op("pe", lambda e, c=c, k=k: e.transpose(
                    out=P3[:, k * 128:(k + 1) * 128],
                    in_=A_h[:, KT0 + c * 1024 + k * 128:KT0 + c * 1024 + (k + 1) * 128],
                    identity=ident[:]),
                    r=[(bh, KT0 + c * 1024 + k * 128, KT0 + c * 1024 + (k + 1) * 128),
                       (bconst, 0, 1)],
                    w=[(bp[id(P3)], k * 128, (k + 1) * 128)], tag=f"ktp{c}")
            P.op("act", lambda e: e.activation(out=kdbuf[:], in_=P3[:], func=AF.Copy),
                 r=[(bp[id(P3)], 0, 1024)], w=[(bkd, 0, 1024)], tag=f"kde{c}")
            if c == 0:
                dump16(3, bkd, kdbuf[:], 0, 1024)
            for hb in range(2 if aparts >= 4 else 0):
                for q in range(2):
                    quad = hb * 2 + q
                    ps = (P1, P2)[q]
                    pbq = bp[id(ps)]
                    for hh in range(4):
                        h = quad * 4 + hh
                        pr, dt = (h % 2) * 64, h // 2
                        P.op("pe", lambda e, h=h, hh=hh, pr=pr, dt=dt, ps=ps: e.matmul(
                            out=ps[:, hh * 128:(hh + 1) * 128],
                            lhsT=kdbuf[pr:pr + 64, dt * 128:(dt + 1) * 128],
                            rhs=A_s16[pr:pr + 64, dt * 128:(dt + 1) * 128],
                            start=True, stop=True),
                            r=[(bkd, dt * 128, (dt + 1) * 128),
                               (bs16, dt * 128, (dt + 1) * 128)],
                            w=[(pbq, hh * 128, (hh + 1) * 128)], tag=f"sc{c}_{quad}",
                            nop=64)
                    P.op("dve", lambda e, ps=ps, q=q: e.tensor_tensor(
                        out=scb[:, q * 512:(q + 1) * 512].rearrange(
                            "p (g n) -> p g n", g=4),
                        in0=ps[:].rearrange("p (g n) -> p g n", g=4),
                        in1=causal[:].rearrange("p (g n) -> p g n", g=1
                                                ).broadcast_to((128, 4, 128)),
                        op=ALU.mult),
                        r=[(pbq, 0, 512), (bconst, 0, 1)],
                        w=[(bscb_, q * 512, (q + 1) * 512)], tag=f"sce{c}_{quad}")
                for hh8 in range(8 if aparts >= 5 else 0):
                    h = hb * 8 + hh8
                    pr, dt = (h % 2) * 64, h // 2
                    soff = (hh8 // 4) * 512 + (hh8 % 4) * 128
                    if ap5 in ("both", "intra", "noevac"):
                        P.op("pe", lambda e, h=h, pr=pr, dt=dt, soff=soff, c=c,
                             st=(ap5 not in ("both", "cross")): e.matmul(
                            out=P5[pr:pr + 64, dt * 128:(dt + 1) * 128],
                            lhsT=A_h[:, VB0 + c * 1024 + h * 64:VB0 + c * 1024 + (h + 1) * 64],
                            rhs=scb[:, soff:soff + 128],
                            start=True, stop=st),
                            r=[(bh, VB0 + c * 1024 + h * 64, VB0 + c * 1024 + (h + 1) * 64),
                               (bscb_, soff, soff + 128)],
                            w=[(bp5, dt * 128, (dt + 1) * 128)], tag=f"ri{c}_{h}")
                    if ap5 in ("both", "cross", "noevac"):
                        P.op("pe", lambda e, h=h, pr=pr, dt=dt,
                             st=(ap5 not in ("both", "intra")): e.matmul(
                            out=P5[pr:pr + 64, dt * 128:(dt + 1) * 128],
                            lhsT=stSb[:, h * 64:(h + 1) * 64],
                            rhs=A_s16[:, dt * 128:(dt + 1) * 128],
                            start=st, stop=True),
                            r=[(bstSb, h * 64, (h + 1) * 64),
                               (bs16, dt * 128, (dt + 1) * 128)],
                            w=[(bp5, dt * 128, (dt + 1) * 128)], tag=f"rc{c}_{h}",
                            nop=64)
            if c == 0:
                dump16(4, bscb_, scb[:], 0, 1024)
            if aparts >= 7:
                dS_ops(c, "b")
            if aparts < 5 or ap5 == "noevac":
                continue
            P.op("dve", lambda e, c=c: e.tensor_copy(
                out=A_h[:, KT0 + c * 1024:KT0 + (c + 1) * 1024], in_=P5[:]),
                r=[(bp5, 0, 1024)],
                w=[(bh, KT0 + c * 1024, KT0 + (c + 1) * 1024)], tag=f"roe{c}")
            if aparts < 6:
                continue
            if c == 0:
                dump16(5, bh, A_h[:, KT0:KT0 + 1024], KT0, KT0 + 1024)
            P.op("dve", lambda e: e.tensor_reduce(
                out=gnst[:, 0:8], in_=P5[:].rearrange("p (g n) -> p g n", g=8),
                axis=AX.X, op=ALU.add),
                r=[(bp5, 0, 1024)], w=[(bgnst, 0, 8)], tag=f"gs1{c}")
            P.op("dve", lambda e: e.tensor_add(
                out=gnsum[:], in0=gnsum[:], in1=gnst[:, 0:8]),
                r=[(bgn_, 0, 8), (bgnst, 0, 8)], w=[(bgn_, 0, 8)], tag=f"gs2{c}")
            P.op("act", lambda e: e.activation(
                out=kdbuf[:], in_=P5[:], func=AF.Square),
                r=[(bp5, 0, 1024)], w=[(bkd, 0, 1024)], tag=f"gq1{c}")
            P.op("dve", lambda e: e.tensor_reduce(
                out=gnst[:, 8:16], in_=kdbuf[:].rearrange("p (g n) -> p g n", g=8),
                axis=AX.X, op=ALU.add),
                r=[(bkd, 0, 1024)], w=[(bgnst, 8, 16)], tag=f"gq2{c}")
            P.op("dve", lambda e: e.tensor_add(
                out=gnsq[:], in0=gnsq[:], in1=gnst[:, 8:16]),
                r=[(bgn_, 8, 16), (bgnst, 8, 16)], w=[(bgn_, 8, 16)], tag=f"gq3{c}")

        if aparts < 8:
            raise _EndAttn()
        P.op("dve", lambda e: e.tensor_copy(
            out=gnst[:].rearrange("p (k two) -> p k two", two=2)[:, :, 0:1],
            in_=gnsum[:].rearrange("p (k one) -> p k one", one=1)),
            r=[(bgn_, 0, 8)], w=[(bgnst, 0, 16)], tag="gc1")
        P.op("dve", lambda e: e.tensor_copy(
            out=gnst[:].rearrange("p (k two) -> p k two", two=2)[:, :, 1:2],
            in_=gnsq[:].rearrange("p (k one) -> p k one", one=1)),
            r=[(bgn_, 8, 16)], w=[(bgnst, 0, 16)], tag="gc2")
        for k in range(8):
            P.op("pe", lambda e, k=k: e.matmul(
                out=P4[0:16, 0:2], lhsT=hsum[:, k * 16:(k + 1) * 16],
                rhs=gnst[:, 2 * k:2 * k + 2], start=(k == 0), stop=(k == 7)),
                r=[(bconst, 0, 1), (bgnst, 0, 16)],
                w=[(bp4, 0, 2)], tag=f"gst{k}", nop=(64 if k == 7 else 0))
        P.op("dve", lambda e: e.tensor_copy(out=st16[:], in_=P4[0:16, 0:2]),
             r=[(bp4, 0, 2)], w=[(bgn_, 48, 50)], tag="gse")
        P.dma("pool", lambda e: e.dma_start(out=ar_in, in_=st16[:]),
              r=[(bgn_, 48, 50)], w=[(bag, 2, 3)], tag="arw")
        P.cc(lambda e: e.collective_compute(
            "AllReduce", ALU.add, replica_groups=groups_state,
            ins=[ar_in.opt()], outs=[ar_out.opt()]),
            r=[(bag, 2, 3)], w=[(bag, 3, 4)], tag="ar")
        P.dma("sp", lambda e: e.dma_start(out=st16g[:], in_=ar_out),
              r=[(bag, 3, 4)], w=[(bgn_, 50, 52)], tag="arr")
        P.op("dve", lambda e: e.tensor_scalar(
            out=me16[:], in0=st16g[:], scalar1=1.0 / GN_N, scalar2=None,
            op0=ALU.mult),
            r=[(bgn_, 50, 52)], w=[(bgn_, 52, 54)], tag="gm")
        P.op("dve", lambda e: e.tensor_mul(
            out=t16[:], in0=me16[:, 0:1], in1=me16[:, 0:1]),
            r=[(bgn_, 52, 54)], w=[(bgn_, 54, 55)], tag="gm2")
        P.op("dve", lambda e: e.tensor_sub(
            out=v16[:], in0=me16[:, 1:2], in1=t16[:]),
            r=[(bgn_, 52, 55)], w=[(bgn_, 55, 56)], tag="gv")
        P.op("act", lambda e: e.activation(
            out=v16[:], in_=v16[:], func=AF.Ln, bias=np.float32(EPS)),
            r=[(bgn_, 55, 56)], w=[(bgn_, 55, 56)], tag="gln")
        P.op("act", lambda e: e.activation(
            out=v16[:], in_=v16[:], func=AF.Exp, scale=-0.5),
            r=[(bgn_, 55, 56)], w=[(bgn_, 55, 56)], tag="gexp")
        P.op("dve", lambda e: e.tensor_copy(out=mr16[:, 0:1], in_=me16[:, 0:1]),
             r=[(bgn_, 52, 54)], w=[(bgn_, 56, 57)], tag="gp1")
        P.op("dve", lambda e: e.tensor_copy(out=mr16[:, 1:2], in_=v16[:]),
             r=[(bgn_, 55, 56)], w=[(bgn_, 57, 58)], tag="gp2")
        P.op("act", lambda e: e.activation(out=mr16b[:], in_=mr16[:], func=AF.Copy),
             r=[(bgn_, 56, 58)], w=[(bgn_, 58, 59)], tag="gp3")
        for k in range(8):
            P.op("pe", lambda e, k=k: e.matmul(
                out=P4[:, 32:34], lhsT=hexp[:, k * 128:(k + 1) * 128],
                rhs=mr16b[:], start=True, stop=True),
                r=[(bconst, 0, 1), (bgn_, 58, 59)],
                w=[(bp4, 32, 34)], tag=f"ge{k}", nop=64)
            P.op("dve", lambda e: e.tensor_copy(out=mrk[:], in_=P4[:, 32:34]),
                 r=[(bp4, 32, 34)], w=[(bmrk, 0, 2)], tag=f"gee{k}")
            P.op("dve", lambda e, k=k: e.tensor_mul(
                out=gsc[:, k:k + 1], in0=mrk[:, 1:2], in1=gnw2[:, k:k + 1]),
                r=[(bmrk, 0, 2), (bconst, 0, 1)], w=[(bgn_, 60 + k, 61 + k)],
                tag=f"gsc{k}")
            P.op("dve", lambda e, k=k: e.tensor_mul(
                out=mrk[:, 0:1], in0=mrk[:, 0:1], in1=gsc[:, k:k + 1]),
                r=[(bmrk, 0, 2), (bgn_, 60 + k, 61 + k)], w=[(bmrk, 0, 1)],
                tag=f"gbm{k}")
            P.op("dve", lambda e, k=k: e.tensor_sub(
                out=gbi[:, k:k + 1], in0=gnb2[:, k:k + 1], in1=mrk[:, 0:1]),
                r=[(bmrk, 0, 1), (bconst, 0, 1)], w=[(bgn_, 70 + k, 71 + k)],
                tag=f"gbi{k}")
        for k in range(8):
            P.op("dve", lambda e, k=k: e.tensor_scalar(
                out=A_h[:, KT0:KT0 + 16384].rearrange(
                    "p (c x) -> p c x", c=16)[:, :, k * 128:(k + 1) * 128],
                in0=A_h[:, KT0:KT0 + 16384].rearrange(
                    "p (c x) -> p c x", c=16)[:, :, k * 128:(k + 1) * 128],
                scalar1=gsc[:, k:k + 1], scalar2=gbi[:, k:k + 1],
                op0=ALU.mult, op1=ALU.add),
                r=[(bh, KT0, KT0 + 16384), (bgn_, 60 + k, 61 + k),
                   (bgn_, 70 + k, 71 + k)],
                w=[(bh, KT0, KT0 + 16384)], tag=f"gna{k}")
        P.dma("pool", lambda e: e.dma_start(
            out=A_w[:].rearrange("p (k n) -> p k n", k=8),
            in_=wout_in.rearrange("(k p) n -> p k n", p=128)),
            r=[(bw, 0, 8192)], w=[(bw, 0, 8192)], tag="wlo")
        for mt in range(NT):
            ps = (P5, P6)[mt % 2]
            pb = bp[id(ps)]
            for half in range(2):
                for k in range(8):
                    P.op("pe", lambda e, ps=ps, mt=mt, k=k, half=half: e.matmul(
                        out=ps[:, half * 512:(half + 1) * 512],
                        lhsT=A_h[:, KT0 + mt * 1024 + k * 128:KT0 + mt * 1024 + (k + 1) * 128],
                        rhs=A_w[:, k * 1024 + half * 512:k * 1024 + (half + 1) * 512],
                        start=(k == 0), stop=(k == 7)),
                        r=[(bh, KT0 + mt * 1024 + k * 128, KT0 + mt * 1024 + (k + 1) * 128),
                           (bw, k * 1024 + half * 512, k * 1024 + (half + 1) * 512)],
                        w=[(pb, half * 512, (half + 1) * 512)], tag=f"wo{mt}")
            P.op("dve", lambda e, ps=ps, mt=mt: e.tensor_add(
                out=A_res[:, mt * 1024:(mt + 1) * 1024],
                in0=A_res[:, mt * 1024:(mt + 1) * 1024], in1=ps[:]),
                r=[(pb, 0, 1024), (bres, mt * 1024, (mt + 1) * 1024)],
                w=[(bres, mt * 1024, (mt + 1) * 1024)], tag=f"woa{mt}")

      if stage >= 3:
        ln_phase("ln3")
        ffn_phase(first=False)
    except _EndAttn:
        pass

    for g in range(4):
        P.dma("act", lambda e, g=g: e.dma_start(
            out=y_out.rearrange("(t p) d -> p t d", p=128)[:, g * 4:(g + 1) * 4, :],
            in_=A_res[:].rearrange("p (t n) -> p t n", t=NT)[:, g * 4:(g + 1) * 4, :]),
            r=[(bres, g * 4 * 1024, (g + 1) * 4 * 1024)],
            w=[(bout, g, g + 1)], tag=f"sto{g}")

    P.resolve()
    P.emit(nc, ctx)
    ctx.close()
    return nc



def _consts():
    gam = np.array([1.0 - 2.0 ** (-5.0 - math.floor(8.0 * i / H)) for i in range(H)],
                   dtype=np.float64)
    a = np.arange(CH, dtype=np.float64)
    qs = np.zeros((128, 8 * 128), np.float64)
    ks = np.zeros((128, 8 * 128), np.float64)
    for dt in range(8):
        for par in range(2):
            h = dt * 2 + par
            qs[par * 64:(par + 1) * 64, dt * 128:(dt + 1) * 128] = \
                (gam[h] ** a)[None, :] / 8.0
    for h in range(H):
        ks[:, h * 64:(h + 1) * 64] = (gam[h] ** (-a))[:, None]
    causal = (np.arange(CH)[None, :] >= np.arange(CH)[:, None]).astype(np.float32)
    ident = np.eye(128, dtype=np.float32)
    gdec = np.zeros((128, 512), np.float64)
    for h in range(H):
        gdec[(h % 2) * 64:(h % 2) * 64 + 64, (h // 2) * 64:(h // 2) * 64 + 64] = \
            gam[h] ** CH
    hs = np.zeros((128, 8 * 16), np.float32)
    for k in range(8):
        for par in range(2):
            hs[par * 64:(par + 1) * 64, k * 16 + 2 * k + par] = 1.0
    he = np.zeros((16, 8 * 128), np.float32)
    for k in range(8):
        for par in range(2):
            he[2 * k + par, k * 128 + par * 64:k * 128 + (par + 1) * 64] = 1.0
    return gam, qs, ks, causal, ident, gdec, hs, he


def _prefw(core, gam):
    pos = core % 4
    w = np.zeros((4, 128, 512), np.float64)
    for r in range(4):
        if r < pos:
            for h in range(H):
                w[r, (h % 2) * 64:(h % 2) * 64 + 64,
                  (h // 2) * 64:(h // 2) * 64 + 64] = gam[h] ** (T * (pos - 1 - r))
    return w.astype(np.float32)


def kernel(L0_attn_x, L0_attn_Wq, L0_attn_Wk, L0_attn_Wv, L0_attn_gn_weight,
           L0_attn_gn_bias, L0_attn_Wout, L0_ffn_w1, L0_ffn_b1, L0_ffn_w2,
           L0_ffn_b2):
    import ml_dtypes
    x = np.ascontiguousarray(np.asarray(L0_attn_x, dtype=np.float32))
    w1 = np.ascontiguousarray(np.asarray(L0_ffn_w1, dtype=np.float32))
    w2 = np.ascontiguousarray(np.asarray(L0_ffn_w2, dtype=np.float32))
    b1 = np.asarray(L0_ffn_b1, dtype=np.float32)
    b2 = np.asarray(L0_ffn_b2, dtype=np.float32)
    wq = np.asarray(L0_attn_Wq, dtype=np.float32)
    wk = np.asarray(L0_attn_Wk, dtype=np.float32)
    wv = np.asarray(L0_attn_Wv, dtype=np.float32)
    wout = np.ascontiguousarray(np.asarray(L0_attn_Wout, dtype=np.float32))
    gnw = np.asarray(L0_attn_gn_weight, dtype=np.float32)
    gnb = np.asarray(L0_attn_gn_bias, dtype=np.float32)

    gam, qs, ks, causal, ident, gdec, hs, he = _consts()
    wqkv = np.stack([w.transpose(1, 0, 2).reshape(D, D) for w in (wq, wk, wv)])
    wqkv = np.ascontiguousarray(wqkv)
    b1l = np.ascontiguousarray(b1.reshape(32, 128).T)
    gnw2 = np.ascontiguousarray(gnw.reshape(8, 128).T)
    gnb2 = np.ascontiguousarray(gnb.reshape(8, 128).T)

    shared = {
        "w1": w1, "w2": w2, "b1l": b1l, "b2r": b2[None, :].copy(),
        "wqkv": wqkv, "wout": wout, "gnw": gnw2, "gnb": gnb2,
        "qscale": qs.astype(np.float32),
        "kscale": ks.astype(np.float32),
        "causal": causal, "ident": ident.astype(ml_dtypes.bfloat16),
        "gdecay": gdec.astype(np.float32),
        "hsum16": hs, "hexp": he.astype(ml_dtypes.bfloat16),
        "ones1": np.ones((1, 128), ml_dtypes.bfloat16),
    }
    in_maps = []
    for core in range(NCORES):
        b = core // 4
        t0 = (core % 4) * T
        m = dict(shared)
        m["x"] = np.ascontiguousarray(x[b, t0:t0 + T, :])
        m["prefw"] = _prefw(core, gam)
        in_maps.append(m)

    nc = build_nc(_STAGE, _APARTS, _AP5, _DBG)
    res = run_bass_kernel_spmd(nc, in_maps, core_ids=list(range(NCORES)))
    out = np.empty((B, L, D), dtype=np.float32)
    for core in range(NCORES):
        b = core // 4
        t0 = (core % 4) * T
        out[b, t0:t0 + T, :] = res.results[core]["y"]
    return out


# version 03
# speedup vs baseline: 1.0858x; 1.0024x over previous
import math
import numpy as np
from contextlib import ExitStack

import concourse.bass as bass
import concourse.mybir as mybir
from concourse.bass_utils import run_bass_kernel_spmd

F32 = mybir.dt.float32
BF16 = mybir.dt.bfloat16
AF = mybir.ActivationFunctionType
ALU = mybir.AluOpType
AX = mybir.AxisListType

NCORES = 8
B, L, D = 2, 8192, 1024
T = 2048
NT = 16
H, E = 16, 64
HID = 4096
CH = 128
NCH = 16
EPS = 1e-5
GN_N = float(E * L)

_STAGE = 3
_APARTS = 8
_AP5 = "both"
_DBG = 0


ENGINES = ("pe", "act", "dve", "pool", "sp")
DMA_QUEUES = ("sp", "act", "pool")
NLANES = 8


class _Op:
    __slots__ = ("idx", "eng", "fn", "kind", "reads", "writes", "qidx",
                 "deps", "marked", "inc_no", "ccsem", "tag", "wl", "nop_after")

    def __init__(self, idx, eng, fn, kind, reads, writes, tag):
        self.nop_after = 0
        self.idx = idx
        self.eng = eng
        self.fn = fn
        self.kind = kind
        self.reads = reads
        self.writes = writes
        self.qidx = None
        self.deps = set()
        self.marked = False
        self.inc_no = None
        self.ccsem = None
        self.tag = tag


class Buf:
    __slots__ = ("name", "writes", "reads")

    def __init__(self, name):
        self.name = name
        self.writes = []
        self.reads = []


class Prog:
    def __init__(self):
        self.ops = []
        self.bufs = {}

    def buf(self, name):
        b = self.bufs.get(name)
        if b is None:
            b = self.bufs[name] = Buf(name)
        return b

    def op(self, eng, fn, r=(), w=(), kind="c", tag="", nop=0):
        o = _Op(len(self.ops), eng, fn, kind, list(r), list(w), tag)
        o.nop_after = nop
        self.ops.append(o)
        return o

    def dma(self, queue, fn, r=(), w=(), tag=""):
        return self.op(queue, fn, r=r, w=w, kind="d", tag=tag)

    def cc(self, fn, r=(), w=(), tag=""):
        return self.op("pool", fn, r=r, w=w, kind="cc", tag=tag)

    def resolve(self):
        for o in self.ops:
            for (b, lo, hi) in o.reads:
                for seg in b.writes:
                    if seg[0] < hi and lo < seg[1]:
                        o.deps.add(seg[2])
            for (b, lo, hi) in o.writes:
                for seg in b.writes:
                    if seg[0] < hi and lo < seg[1]:
                        o.deps.add(seg[2])
                for seg in b.reads:
                    if seg[0] < hi and lo < seg[1]:
                        o.deps.add(seg[2])
            for (b, lo, hi) in o.writes:
                b.writes = [s for s in b.writes if not (lo <= s[0] and s[1] <= hi)]
                b.reads = [s for s in b.reads if not (lo <= s[0] and s[1] <= hi)]
                b.writes.append([lo, hi, o.idx])
            for (b, lo, hi) in o.reads:
                b.reads.append([lo, hi, o.idx])

        qctr = {q: 0 for q in DMA_QUEUES}
        for o in self.ops:
            if o.kind == "d":
                o.qidx = qctr[o.eng]
                qctr[o.eng] += 1
        self.qtotal = dict(qctr)

        for o in self.ops:
            for d in o.deps:
                p = self.ops[d]
                if p.kind == "c" and (p.eng != o.eng or p.eng != "pe"):
                    p.marked = True
        inc_ctr = {e: 0 for e in ENGINES}
        ncc = 0
        for o in self.ops:
            if o.kind == "c" and o.marked:
                inc_ctr[o.eng] += 1
                o.inc_no = inc_ctr[o.eng]
            elif o.kind == "cc":
                o.ccsem = ncc
                ncc += 1
        self.inc_total = inc_ctr
        self.ncc = ncc

        best = {e: {} for e in ENGINES}
        for o in self.ops:
            waits = {}
            for d in sorted(o.deps):
                p = self.ops[d]
                if p.kind == "d":
                    key = ("q", p.eng, p.qidx % NLANES)
                    val = 16 * (p.qidx // NLANES + 1)
                elif p.kind == "cc":
                    key = ("cc", p.ccsem)
                    val = 1
                else:
                    if p.eng == o.eng and p.eng == "pe":
                        continue
                    key = ("e", p.eng)
                    val = p.inc_no
                    assert val is not None
                if waits.get(key, -1) < val:
                    waits[key] = val
            myb = best[o.eng]
            o.wl = []
            for key, val in sorted(waits.items()):
                if myb.get(key, 0) >= val:
                    continue
                myb[key] = val
                o.wl.append((key, val))

    def emit(self, nc, ctx):
        esems = {e: ctx.enter_context(nc.semaphore(f"es_{e}")) for e in ENGINES}
        qsems = {q: [ctx.enter_context(nc.semaphore(f"qs_{q}{l}"))
                     for l in range(NLANES)] for q in DMA_QUEUES}
        ccsems = [ctx.enter_context(nc.semaphore(f"cc_{i}")) for i in range(self.ncc)]

        def semof(key):
            if key[0] == "e":
                return esems[key[1]]
            if key[0] == "q":
                return qsems[key[1]][key[2]]
            return ccsems[key[1]]

        per_eng = {e: [] for e in ENGINES}
        for o in self.ops:
            per_eng[o.eng].append(o)

        block = ctx.enter_context(nc.Block())

        def runner(eng_name):
            def run(eng):
                for o in per_eng[eng_name]:
                    for key, val in o.wl:
                        eng.wait_ge(semof(key), val)
                    if o.kind == "d":
                        lane = o.qidx % NLANES
                        use = o.qidx // NLANES
                        if use > 0:
                            eng.wait_ge(qsems[o.eng][lane], 16 * use)
                        inst = o.fn(eng)
                        inst.then_inc(qsems[o.eng][lane], 16)
                        continue
                    inst = o.fn(eng)
                    if o.kind == "cc":
                        inst.then_inc(ccsems[o.ccsem])
                    elif o.marked:
                        inst.then_inc(esems[o.eng], 1)
                    if o.nop_after:
                        eng.drain()
                if eng_name in DMA_QUEUES and self.qtotal[eng_name]:
                    tot = self.qtotal[eng_name]
                    for l in range(min(NLANES, tot)):
                        n_l = (tot - 1 - l) // NLANES + 1
                        eng.wait_ge(qsems[eng_name][l], 16 * n_l)
            return run

        block.tensor(runner("pe"))
        block.scalar(runner("act"))
        block.vector(runner("dve"))
        block.gpsimd(runner("pool"))
        block.sync(runner("sp"))



def build_nc(stage, aparts=4, ap5="both", dbg=0):
    nc = bass.Bass("TRN2", target_bir_lowering=False)
    P = Prog()

    d_in = {}
    def din(name, shape, dtype=F32):
        d_in[name] = nc.dram_tensor(name, shape, dtype, kind="ExternalInput").ap()
        return d_in[name]

    x_in = din("x", [T, D])
    w1_in = din("w1", [HID, D])
    w2_in = din("w2", [D, HID])
    b1_in = din("b1l", [128, 32])
    b2_in = din("b2r", [1, D])
    wqkv_in = din("wqkv", [3, D, D])
    wout_in = din("wout", [D, D])
    gnw_in = din("gnw", [128, 8])
    gnb_in = din("gnb", [128, 8])
    qscale_in = din("qscale", [128, 8 * 128])
    kscale_in = din("kscale", [128, 8 * 128])
    causal_in = din("causal", [128, 128])
    ident_in = din("ident", [128, 128], BF16)
    gdecay_in = din("gdecay", [128, 512])
    prefw_in = din("prefw", [4, 128, 512])
    hsum_in = din("hsum16", [128, 8 * 16])
    hexp_in = din("hexp", [16, 8 * 128], BF16)
    ones_in = din("ones1", [1, 128], BF16)

    y_out = nc.dram_tensor("y", [T, D], F32, kind="ExternalOutput").ap()
    if dbg:
        dbgB = nc.dram_tensor("dbgB", [6, 128, 1024], F32, kind="ExternalOutput").ap()
        dbgF = nc.dram_tensor("dbgF", [3, 128, 512], F32, kind="ExternalOutput").ap()
        bdbg = None

    w1t_scr = nc.dram_tensor("w1t_scr", [32, 128, 1024], BF16).ap()
    w2t_scr = nc.dram_tensor("w2t_scr", [32, 128, 1024], BF16).ap()
    ag_in = nc.dram_tensor("ag_in", [128, 512], F32).ap()
    ag_out = nc.dram_tensor("ag_out", [4 * 128, 512], F32).ap()
    ar_in = nc.dram_tensor("ar_in", [16, 2], F32).ap()
    ar_out = nc.dram_tensor("ar_out", [16, 2], F32).ap()

    groups_state = [[0, 1, 2, 3], [4, 5, 6, 7]]

    ctx = ExitStack()
    sb = lambda n, s, d: ctx.enter_context(nc.sbuf_tensor("S_" + n, s, d))
    psal = lambda n, s, d: ctx.enter_context(nc.psum_tensor(n, s, d))

    A_res = sb("A_res", [128, NT * 1024], F32)
    A_x = sb("A_x", [128, 8 * 2048], BF16)
    A_h = sb("A_h", [128, 32768], BF16)
    A_w = sb("A_w", [128, 8192], BF16)
    A_s32 = sb("A_s32", [128, 1024], F32)
    A_s16 = sb("A_s16", [128, 1024], BF16)
    w1tb = sb("w1tb", [128, 2 * 1024], BF16)
    kdbuf = sb("kdbuf", [128, 1024], BF16)
    xlnT = kdbuf
    scb = sb("scb", [128, 1024], BF16)
    qksc = sb("qksc", [128, 1024], F32)
    stS = sb("stS", [128, 512], F32)
    stSb = sb("stSb", [128, 1024], BF16)
    gdecT = sb("gdecT", [128, 512], F32)
    causal = sb("causal", [128, 128], F32)
    ident = sb("ident", [128, 128], BF16)
    ones1 = sb("ones1", [1, 128], BF16)
    b2row = sb("b2row", [1, 1024], BF16)
    b1l = sb("b1l", [128, 32], F32)
    al1 = sb("al1", [128, 32], F32)
    al2 = sb("al2", [128, 8], F32)
    gnw2 = sb("gnw2", [128, 8], F32)
    gnb2 = sb("gnb2", [128, 8], F32)
    lnS1 = sb("lnS1", [128, 16], F32)
    lnS2 = sb("lnS2", [128, 16], F32)
    lnM = sb("lnM", [128, 16], F32)
    lnT = sb("lnT", [128, 16], F32)
    lnR = sb("lnR", [128, 16], F32)
    lnB = sb("lnB", [128, 16], F32)
    gnsum = sb("gnsum", [128, 8], F32)
    gnsq = sb("gnsq", [128, 8], F32)
    gnst = sb("gnst", [128, 16], F32)
    st16 = sb("st16", [16, 2], F32)
    st16g = sb("st16g", [16, 2], F32)
    me16 = sb("me16", [16, 2], F32)
    v16 = sb("v16", [16, 1], F32)
    t16 = sb("t16", [16, 1], F32)
    mr16 = sb("mr16", [16, 2], F32)
    mrk = sb("mrk", [128, 2], F32)
    gsc = sb("gsc", [128, 8], F32)
    gbi = sb("gbi", [128, 8], F32)
    hsum = sb("hsum", [128, 8 * 16], F32)
    hexp = sb("hexp", [16, 8 * 128], BF16)
    mr16b = sb("mr16b", [16, 2], BF16)

    P1 = psal("P1", [128, 512], F32)
    P2 = psal("P2", [128, 512], F32)
    P3 = psal("P3", [128, 1024], BF16)
    P4 = psal("P4", [128, 512], F32)
    P5 = psal("P5", [128, 1024], F32)
    P6 = psal("P6", [128, 1024], F32)

    bres = P.buf("A_res"); bx = P.buf("A_x"); bh = P.buf("A_h"); bw = P.buf("A_w")
    bs32 = P.buf("A_s32"); bs16 = P.buf("A_s16"); bw1tb = P.buf("w1tb")
    bkd = P.buf("kdbuf"); bxlnT = bkd; bscb = P.buf("scb")
    bstS = P.buf("stS"); bstSb = P.buf("stSb")
    bp = {id(P1): P.buf("P1"), id(P2): P.buf("P2"), id(P3): P.buf("P3"),
          id(P4): P.buf("P4"), id(P5): P.buf("P5"), id(P6): P.buf("P6")}
    bconst = P.buf("consts")
    bln = P.buf("lnstats")
    bal = P.buf("alphas")
    bgn = P.buf("gnstats")
    bscr1 = P.buf("w1t_scr")
    bscr2 = P.buf("w2t_scr")
    bdram = P.buf("dram_cc")
    bout = P.buf("yout")


    def ld_const(dst_ap, src_ap, q="sp"):
        P.dma(q, lambda e, o=dst_ap, i=src_ap: e.dma_start(out=o, in_=i),
              w=[(bconst, 0, 1)], tag="ldc")

    ld_const(b1l[:], b1_in)
    ld_const(gnw2[:], gnw_in)
    ld_const(gnb2[:], gnb_in)
    bqk = P.buf("qksc")
    P.dma("sp", lambda e: e.dma_start(out=qksc[:], in_=kscale_in),
          w=[(bqk, 0, 1024)], tag="ldksc")
    ld_const(causal[:], causal_in)
    ld_const(ident[:], ident_in)
    ld_const(gdecT[:], gdecay_in)
    ld_const(hsum[:], hsum_in)
    ld_const(hexp[:], hexp_in)
    ld_const(ones1[:], ones_in)

    for g_ in range(4):
        P.dma("sp", lambda e, g_=g_: e.dma_start(
            out=A_res[:].rearrange("p (t n) -> p t n", t=NT)[:, g_ * 4:(g_ + 1) * 4, :],
            in_=x_in.rearrange("(t p) d -> p t d", p=128)[:, g_ * 4:(g_ + 1) * 4, :]),
            w=[(bres, g_ * 4 * 1024, (g_ + 1) * 4 * 1024)], tag=f"ldx{g_}")

    P.dma("pool", lambda e: e.dma_start(out=b2row[:], in_=b2_in),
          w=[(P.buf("b2row"), 0, 1024)], tag="b2cast")
    bb2 = P.buf("b2row")

    def ln_phase(phase):
        for g in range(4):
            for i in range(g * 4, g * 4 + 4):
                rt = (bres, i * 1024, (i + 1) * 1024)
                tb = (kdbuf, scb)[i % 2]
                tbb = (bkd, bscb)[i % 2]
                P.op("dve", lambda e, i=i: e.tensor_reduce(
                    out=lnS1[:, i:i + 1], in_=A_res[:, i * 1024:(i + 1) * 1024],
                    axis=AX.X, op=ALU.add),
                    r=[rt], w=[(bln, i, i + 1)], tag=f"{phase}s1")
                P.op("act", lambda e, i=i, tb=tb: e.activation(
                    out=tb[:], in_=A_res[:, i * 1024:(i + 1) * 1024],
                    func=AF.Square, accum_out=lnS2[:, i:i + 1]),
                    r=[rt], w=[(bln, 16 + i, 17 + i), (tbb, 0, 1024)],
                    tag=f"{phase}s2")
            lo, hi = g * 4, g * 4 + 4
            P.op("dve", lambda e, lo=lo, hi=hi: e.tensor_scalar(
                out=lnM[:, lo:hi], in0=lnS1[:, lo:hi], scalar1=1.0 / D,
                scalar2=None, op0=ALU.mult),
                r=[(bln, lo, hi)], w=[(bln, 32 + lo, 32 + hi)], tag=f"{phase}mean")
            P.op("dve", lambda e, lo=lo, hi=hi: e.tensor_mul(
                out=lnT[:, lo:hi], in0=lnM[:, lo:hi], in1=lnM[:, lo:hi]),
                r=[(bln, 32 + lo, 32 + hi)], w=[(bln, 48 + lo, 48 + hi)],
                tag=f"{phase}m2")
            P.op("dve", lambda e, lo=lo, hi=hi: e.tensor_scalar(
                out=lnS2[:, lo:hi], in0=lnS2[:, lo:hi], scalar1=1.0 / D,
                scalar2=None, op0=ALU.mult),
                r=[(bln, 16 + lo, 16 + hi)], w=[(bln, 16 + lo, 16 + hi)],
                tag=f"{phase}ex2")
            P.op("dve", lambda e, lo=lo, hi=hi: e.tensor_sub(
                out=lnT[:, lo:hi], in0=lnS2[:, lo:hi], in1=lnT[:, lo:hi]),
                r=[(bln, 16 + lo, 16 + hi), (bln, 48 + lo, 48 + hi)],
                w=[(bln, 48 + lo, 48 + hi)], tag=f"{phase}var")
            P.op("act", lambda e, lo=lo, hi=hi: e.activation(
                out=lnT[:, lo:hi], in_=lnT[:, lo:hi], func=AF.Ln,
                bias=np.float32(EPS)),
                r=[(bln, 48 + lo, 48 + hi)], w=[(bln, 48 + lo, 48 + hi)],
                tag=f"{phase}ln")
            P.op("act", lambda e, lo=lo, hi=hi: e.activation(
                out=lnR[:, lo:hi], in_=lnT[:, lo:hi], func=AF.Exp, scale=-0.5),
                r=[(bln, 48 + lo, 48 + hi)], w=[(bln, 64 + lo, 64 + hi)],
                tag=f"{phase}rstd")
            P.op("dve", lambda e, lo=lo, hi=hi: e.tensor_mul(
                out=lnB[:, lo:hi], in0=lnM[:, lo:hi], in1=lnR[:, lo:hi]),
                r=[(bln, 32 + lo, 32 + hi), (bln, 64 + lo, 64 + hi)],
                w=[(bln, 80 + lo, 80 + hi)], tag=f"{phase}mb")
            P.op("dve", lambda e, lo=lo, hi=hi: e.tensor_scalar(
                out=lnB[:, lo:hi], in0=lnB[:, lo:hi], scalar1=-1.0,
                scalar2=None, op0=ALU.mult),
                r=[(bln, 80 + lo, 80 + hi)], w=[(bln, 80 + lo, 80 + hi)],
                tag=f"{phase}nb")
            for i in range(g * 4, g * 4 + 4):
                tb = (kdbuf, scb)[i % 2]
                tbb = (bkd, bscb)[i % 2]
                P.op("dve", lambda e, i=i, tb=tb: e.tensor_scalar(
                    out=tb[:], in0=A_res[:, i * 1024:(i + 1) * 1024],
                    scalar1=lnR[:, i:i + 1], scalar2=lnB[:, i:i + 1],
                    op0=ALU.mult, op1=ALU.add),
                    r=[(bres, i * 1024, (i + 1) * 1024), (bln, 64 + i, 65 + i),
                       (bln, 80 + i, 81 + i)],
                    w=[(tbb, 0, 1024)], tag=f"{phase}app{i}")
                for k in range(8):
                    P.op("pe", lambda e, k=k, tb=tb: e.transpose(
                        out=P3[:, k * 128:(k + 1) * 128],
                        in_=tb[:, k * 128:(k + 1) * 128], identity=ident[:]),
                        r=[(tbb, k * 128, (k + 1) * 128), (bconst, 0, 1)],
                        w=[(bp[id(P3)], k * 128, (k + 1) * 128)], tag=f"{phase}tp")
                P.op("act", lambda e, i=i: e.activation(
                    out=A_x[:].rearrange("p (k n) -> p k n", k=8)[:, :, i * 128:(i + 1) * 128],
                    in_=P3[:].rearrange("p (k n) -> p k n", k=8), func=AF.Copy),
                    r=[(bp[id(P3)], 0, 1024)],
                    w=[(bx, k * 2048 + i * 128, k * 2048 + (i + 1) * 128) for k in range(8)],
                    tag=f"{phase}xev{i}")

    bmrk = P.buf("mrk")

    def w1_prep(ms):
        slot = (ms % 2) * 1024
        P.dma("sp", lambda e, ms=ms: e.dma_start(
            out=A_s32[:, 0:1024], in_=w1_in[ms * 128:(ms + 1) * 128, :]),
            w=[(bs32, 0, 1024)], tag=f"w1ld{ms}")
        P.op("dve", lambda e, ms=ms: e.tensor_reduce(
            out=al1[:, ms:ms + 1], in_=A_s32[:, 0:1024], axis=AX.X,
            op=ALU.add, apply_absolute_value=True),
            r=[(bs32, 0, 1024)], w=[(bal, ms, ms + 1)], tag=f"w1al{ms}")
        P.op("act", lambda e: e.activation(
            out=A_s16[:, 0:1024], in_=A_s32[:, 0:1024], func=AF.Sign),
            r=[(bs32, 0, 1024)], w=[(bs16, 0, 1024)], tag=f"w1sg{ms}")
        P.op("dve", lambda e, ms=ms: e.tensor_scalar(
            out=A_s16[:, 0:1024], in0=A_s16[:, 0:1024],
            scalar1=al1[:, ms:ms + 1], scalar2=1.0 / D, op0=ALU.mult, op1=ALU.mult),
            r=[(bs16, 0, 1024), (bal, ms, ms + 1)], w=[(bs16, 0, 1024)], tag=f"w1sc{ms}")
        for k in range(8):
            P.op("pe", lambda e, k=k: e.transpose(
                out=P3[:, k * 128:(k + 1) * 128],
                in_=A_s16[:, k * 128:(k + 1) * 128], identity=ident[:]),
                r=[(bs16, k * 128, (k + 1) * 128), (bconst, 0, 1)],
                w=[(bp[id(P3)], k * 128, (k + 1) * 128)], tag=f"w1tp{ms}")
        P.op("act", lambda e, slot=slot: e.activation(
            out=w1tb[:, slot:slot + 1024], in_=P3[:], func=AF.Copy),
            r=[(bp[id(P3)], 0, 1024)], w=[(bw1tb, slot, slot + 1024)], tag=f"w1ev{ms}")
        P.dma("act", lambda e, ms=ms, slot=slot: e.dma_start(
            out=w1t_scr[ms], in_=w1tb[:, slot:slot + 1024]),
            r=[(bw1tb, slot, slot + 1024)], w=[(bscr1, ms, ms + 1)], tag=f"w1st{ms}")

    def w2_alpha(kd):
        for qq in range(4):
            P.dma("pool", lambda e, kd=kd, qq=qq: e.dma_start(
                out=A_s32[:, 0:1024],
                in_=w2_in[kd * 128:(kd + 1) * 128, qq * 1024:(qq + 1) * 1024]),
                w=[(bs32, 0, 1024)], tag=f"w2a_ld{kd}_{qq}")
            if qq == 0:
                P.op("dve", lambda e, kd=kd: e.tensor_reduce(
                    out=al2[:, kd:kd + 1], in_=A_s32[:, 0:1024], axis=AX.X,
                    op=ALU.add, apply_absolute_value=True),
                    r=[(bs32, 0, 1024)], w=[(bal, 40 + kd, 41 + kd)], tag=f"w2a1{kd}")
            else:
                P.op("dve", lambda e: e.tensor_reduce(
                    out=mrk[:, 0:1], in_=A_s32[:, 0:1024], axis=AX.X,
                    op=ALU.add, apply_absolute_value=True),
                    r=[(bs32, 0, 1024)], w=[(bmrk, 0, 1)], tag=f"w2a2{kd}_{qq}")
                P.op("dve", lambda e, kd=kd: e.tensor_add(
                    out=al2[:, kd:kd + 1], in0=al2[:, kd:kd + 1], in1=mrk[:, 0:1]),
                    r=[(bal, 40 + kd, 41 + kd), (bmrk, 0, 1)],
                    w=[(bal, 40 + kd, 41 + kd)], tag=f"w2aa{kd}_{qq}")
        P.op("dve", lambda e, kd=kd: e.tensor_scalar(
            out=al2[:, kd:kd + 1], in0=al2[:, kd:kd + 1],
            scalar1=1.0 / HID, scalar2=None, op0=ALU.mult),
            r=[(bal, 40 + kd, 41 + kd)], w=[(bal, 40 + kd, 41 + kd)], tag=f"w2am{kd}")

    def w2_prep2(kq):
        q, kd = divmod(kq, 8)
        P.dma("pool", lambda e, kd=kd, q=q: e.dma_start(
            out=A_s32[:, 0:1024],
            in_=w2_in[kd * 128:(kd + 1) * 128, q * 1024:(q + 1) * 1024]),
            w=[(bs32, 0, 1024)], tag=f"w2ld{kq}")
        P.op("act", lambda e: e.activation(
            out=A_s16[:, 0:1024], in_=A_s32[:, 0:1024], func=AF.Sign),
            r=[(bs32, 0, 1024)], w=[(bs16, 0, 1024)], tag=f"w2sg{kq}")
        P.op("dve", lambda e, kd=kd: e.tensor_scalar(
            out=A_s16[:, 0:1024], in0=A_s16[:, 0:1024],
            scalar1=al2[:, kd:kd + 1], scalar2=None, op0=ALU.mult),
            r=[(bs16, 0, 1024), (bal, 40 + kd, 41 + kd)],
            w=[(bs16, 0, 1024)], tag=f"w2sc{kq}")
        for j in range(8):
            P.op("pe", lambda e, j=j: e.transpose(
                out=P3[:, j * 128:(j + 1) * 128],
                in_=A_s16[:, j * 128:(j + 1) * 128], identity=ident[:]),
                r=[(bs16, j * 128, (j + 1) * 128), (bconst, 0, 1)],
                w=[(bp[id(P3)], j * 128, (j + 1) * 128)], tag=f"w2tp{kq}")
        P.op("act", lambda e: e.activation(
            out=kdbuf[:], in_=P3[:], func=AF.Copy),
            r=[(bp[id(P3)], 0, 1024)], w=[(bkd, 0, 1024)], tag=f"w2ev{kq}")
        P.dma("act", lambda e, kd=kd, q=q: e.dma_start(
            out=w2t_scr[q * 8:(q + 1) * 8, :, kd * 128:(kd + 1) * 128].rearrange(
                "j p d -> p j d"),
            in_=kdbuf[:].rearrange("p (j d) -> p j d", j=8)),
            r=[(bkd, 0, 1024)],
            w=[(bscr2, (q * 8 + j) * 8 + kd, (q * 8 + j) * 8 + kd + 1)
               for j in range(8)],
            tag=f"w2st{kq}")

    def ffn_phase(first):
        if first:
            for kd in range(8):
                w2_alpha(kd)
        w2q_emitted = 0
        for pair in range(2):
            for mloc in range(16):
                ms = pair * 16 + mloc
                slot = (ms % 2) * 1024
                if first:
                    w1_prep(ms)
                    if w2q_emitted < 32:
                        w2_prep2(w2q_emitted)
                        w2q_emitted += 1
                else:
                    P.dma("sp", lambda e, ms=ms, slot=slot: e.dma_start(
                        out=w1tb[:, slot:slot + 1024], in_=w1t_scr[ms]),
                        r=[(bscr1, ms, ms + 1)], w=[(bw1tb, slot, slot + 1024)],
                        tag=f"w1rd{ms}")
                for n in range(4):
                    ps = (P1, P2)[n % 2]
                    pb = bp[id(ps)]
                    for k in range(8):
                        P.op("pe", lambda e, ps=ps, slot=slot, k=k, n=n: e.matmul(
                            out=ps[:, 0:512],
                            lhsT=w1tb[:, slot + k * 128:slot + (k + 1) * 128],
                            rhs=A_x[:, k * 2048 + n * 512:k * 2048 + (n + 1) * 512],
                            start=(k == 0), stop=(k == 7)),
                            r=[(bw1tb, slot + k * 128, slot + (k + 1) * 128),
                               (bx, k * 2048 + n * 512, k * 2048 + (n + 1) * 512)],
                            w=[(pb, 0, 512)], tag=f"mm1_{ms}_{n}")
                    P.op("act", lambda e, ps=ps, mloc=mloc, n=n, ms=ms: e.activation(
                        out=A_h[:, mloc * 2048 + n * 512:mloc * 2048 + (n + 1) * 512],
                        in_=ps[:, 0:512], func=AF.Gelu, bias=b1l[:, ms:ms + 1]),
                        r=[(pb, 0, 512), (bconst, 0, 1)],
                        w=[(bh, mloc * 2048 + n * 512, mloc * 2048 + (n + 1) * 512)],
                        tag=f"gelu_{ms}_{n}")
            for q in range(2):
                s = pair * 2 + q
                P.dma("sp", lambda e, s=s: e.dma_start(
                    out=A_w[:].rearrange("p (j d) -> p j d", j=8),
                    in_=w2t_scr[s * 8:(s + 1) * 8].rearrange("j p d -> p j d")),
                    r=[(bscr2, s * 8 * 8, (s + 1) * 8 * 8)],
                    w=[(bw, 0, 8192)], tag=f"w2rd{s}")
                for mt in range(NT):
                    ps = (P5, P6)[mt % 2]
                    pb = bp[id(ps)]
                    nfirst = True
                    if s == 0:
                        for half in range(2):
                            P.op("pe", lambda e, ps=ps, half=half: e.matmul(
                                out=ps[:, half * 512:(half + 1) * 512],
                                lhsT=ones1[:],
                                rhs=b2row[:, half * 512:(half + 1) * 512],
                                start=True, stop=False),
                                r=[(bconst, 0, 1), (bb2, 0, 1024)],
                                w=[(pb, half * 512, (half + 1) * 512)],
                                tag=f"b2_{mt}")
                        nfirst = False
                    for j in range(8):
                        mloc = q * 8 + j
                        for half in range(2):
                            P.op("pe", lambda e, ps=ps, mloc=mloc, mt=mt, j=j,
                                 half=half, st=(nfirst and j == 0): e.matmul(
                                out=ps[:, half * 512:(half + 1) * 512],
                                lhsT=A_h[:, mloc * 2048 + mt * 128:mloc * 2048 + (mt + 1) * 128],
                                rhs=A_w[:, j * 1024 + half * 512:j * 1024 + (half + 1) * 512],
                                start=st, stop=(j == 7)),
                                r=[(bh, mloc * 2048 + mt * 128, mloc * 2048 + (mt + 1) * 128),
                                   (bw, j * 1024 + half * 512, j * 1024 + (half + 1) * 512)],
                                w=[(pb, half * 512, (half + 1) * 512)],
                                tag=f"mm2_{s}_{mt}")
                    P.op("dve", lambda e, ps=ps, mt=mt: e.scalar_tensor_tensor(
                        out=A_res[:, mt * 1024:(mt + 1) * 1024], in0=ps[:],
                        scalar=0.5, in1=A_res[:, mt * 1024:(mt + 1) * 1024],
                        op0=ALU.mult, op1=ALU.add),
                        r=[(pb, 0, 1024), (bres, mt * 1024, (mt + 1) * 1024)],
                        w=[(bres, mt * 1024, (mt + 1) * 1024)], tag=f"racc_{s}_{mt}")

    ln_phase("ln1")
    ffn_phase(first=True)

    class _EndAttn(Exception):
        pass
    try:
      if stage >= 2:
        bag = P.buf("dram_cc")
        KT0, VB0 = 0, 16384
        bdbgq = P.buf("dbgq")

        def dump16(idx, buf, ap, lo, hi):
            if not dbg:
                return
            P.dma("pool", lambda e, idx=idx, ap=ap: e.dma_start(
                out=dbgB[idx], in_=ap), r=[(buf, lo, hi)],
                w=[(bdbgq, idx, idx + 1)], tag=f"dbg{idx}")

        def dump32(idx, buf, ap, lo, hi):
            if not dbg:
                return
            P.dma("pool", lambda e, idx=idx, ap=ap: e.dma_start(
                out=dbgF[idx], in_=ap), r=[(buf, lo, hi)],
                w=[(bdbgq, 16 + idx, 17 + idx)], tag=f"dbf{idx}")

        ln_phase("ln2")

        def proj_pass(widx, out_base, scale_evac):
            P.dma("pool", lambda e, widx=widx: e.dma_start(
                out=A_w[:].rearrange("p (k n) -> p k n", k=8),
                in_=wqkv_in[widx].rearrange("(k p) n -> p k n", p=128)),
                r=[(bw, 0, 8192)], w=[(bw, 0, 8192)], tag=f"wl{widx}")
            for c in range(NCH):
                ps = (P5, P6)[c % 2]
                pb = bp[id(ps)]
                for half in range(2):
                    for k in range(8):
                        P.op("pe", lambda e, ps=ps, c=c, k=k, half=half: e.matmul(
                            out=ps[:, half * 512:(half + 1) * 512],
                            lhsT=A_x[:, k * 2048 + c * 128:k * 2048 + (c + 1) * 128],
                            rhs=A_w[:, k * 1024 + half * 512:k * 1024 + (half + 1) * 512],
                            start=(k == 0), stop=(k == 7)),
                            r=[(bx, k * 2048 + c * 128, k * 2048 + (c + 1) * 128),
                               (bw, k * 1024 + half * 512, k * 1024 + (half + 1) * 512)],
                            w=[(pb, half * 512, (half + 1) * 512)], tag=f"pj{widx}_{c}")
                scale_evac(c, ps, pb, out_base)

        def k_evac(c, ps, pb, base):
            P.op("dve", lambda e, ps=ps, c=c, base=base: e.tensor_tensor(
                out=A_h[:, base + c * 1024:base + (c + 1) * 1024],
                in0=ps[:], in1=qksc[:], op=ALU.mult),
                r=[(pb, 0, 1024), (bqk, 0, 1024)],
                w=[(bh, base + c * 1024, base + (c + 1) * 1024)], tag=f"kev{c}")

        def v_evac(c, ps, pb, base):
            P.op("act", lambda e, ps=ps, c=c, base=base: e.activation(
                out=A_h[:, base + c * 1024:base + (c + 1) * 1024],
                in_=ps[:], func=AF.Copy),
                r=[(pb, 0, 1024)],
                w=[(bh, base + c * 1024, base + (c + 1) * 1024)], tag=f"vev{c}")

        proj_pass(1, KT0, k_evac)
        proj_pass(2, VB0, v_evac)
        dump16(0, bh, A_h[:, KT0:KT0 + 1024], KT0, KT0 + 1024)
        dump16(1, bh, A_h[:, VB0:VB0 + 1024], VB0, VB0 + 1024)

        if aparts < 2:
            raise _EndAttn()
        P.op("dve", lambda e: e.memset(stS[:], 0.0),
             w=[(bstS, 0, 512)], tag="st0")
        bp4 = bp[id(P4)]

        def dS_ops(c, tagp):
            bp1_ = bp[id(P1)]
            for pair in range(8):
                ps = (P4, P1)[pair % 2]
                pbp = (bp4, bp1_)[pair % 2]
                reg = (pair // 2) * 128
                P.op("pe", lambda e, c=c, pair=pair, ps=ps, reg=reg: e.matmul(
                    out=ps[:, reg:reg + 128],
                    lhsT=A_h[:, KT0 + c * 1024 + pair * 128:KT0 + c * 1024 + (pair + 1) * 128],
                    rhs=A_h[:, VB0 + c * 1024 + pair * 128:VB0 + c * 1024 + (pair + 1) * 128],
                    start=True, stop=True),
                    r=[(bh, KT0 + c * 1024 + pair * 128, KT0 + c * 1024 + (pair + 1) * 128),
                       (bh, VB0 + c * 1024 + pair * 128, VB0 + c * 1024 + (pair + 1) * 128)],
                    w=[(pbp, reg, reg + 128)], tag=f"{tagp}dS{c}", nop=64)
            for par in range(2):
                for bsel in range(2):
                    ps = (P4, P1)[bsel]
                    pbp = (bp4, bp1_)[bsel]
                    P.op("dve", lambda e, par=par, ps=ps, bsel=bsel: e.tensor_add(
                        out=stS[par * 64:par * 64 + 64, :].rearrange(
                            "p (q v) -> p q v", v=64)[:, bsel::2, :],
                        in0=stS[par * 64:par * 64 + 64, :].rearrange(
                            "p (q v) -> p q v", v=64)[:, bsel::2, :],
                        in1=ps[par * 64:par * 64 + 64, :].rearrange(
                            "p (r v) -> p r v", v=64)[:, par::2, :]),
                        r=[(bstS, 0, 512), (pbp, 0, 512)], w=[(bstS, 0, 512)],
                        tag=f"{tagp}su1_{c}_{par}{bsel}")
            P.op("dve", lambda e: e.tensor_mul(out=stS[:], in0=stS[:], in1=gdecT[:]),
                 r=[(bstS, 0, 512), (bconst, 0, 1)], w=[(bstS, 0, 512)],
                 tag=f"{tagp}su2_{c}")
            if tagp == "b":
                for par in range(2):
                    P.op("act", lambda e, par=par: e.activation(
                        out=stSb[par * 64:par * 64 + 64, :].rearrange(
                            "p (h v) -> p h v", v=64)[:, par::2, :],
                        in_=stS[par * 64:par * 64 + 64, :].rearrange(
                            "p (h v) -> p h v", v=64),
                        func=AF.Copy),
                        r=[(bstS, 0, 512)], w=[(bstSb, 0, 1024)], tag=f"bsb{c}_{par}")

        for c in range(NCH):
            dS_ops(c, "a")

        dump32(0, bstS, stS[:], 0, 512)
        P.dma("pool", lambda e: e.dma_start(out=ag_in, in_=stS[:]),
              r=[(bstS, 0, 512)], w=[(bag, 0, 1)], tag="agw")
        P.cc(lambda e: e.collective_compute(
            "AllGather", ALU.bypass, replica_groups=groups_state,
            ins=[ag_in.opt()], outs=[ag_out.opt()]),
            r=[(bag, 0, 1)], w=[(bag, 1, 2)], tag="ag")
        P.op("dve", lambda e: e.memset(stS[:], 0.0),
             r=[(bag, 0, 1)], w=[(bstS, 0, 512)], tag="stz")
        for r_ in range(4):
            P.dma("sp", lambda e, r_=r_: e.dma_start(
                out=A_s32[:, 0:512], in_=ag_out[r_ * 128:(r_ + 1) * 128, :]),
                r=[(bag, 1, 2)], w=[(bs32, 0, 512)], tag=f"agr{r_}")
            P.dma("sp", lambda e, r_=r_: e.dma_start(
                out=A_s32[:, 512:1024], in_=prefw_in[r_]),
                w=[(bs32, 512, 1024)], tag=f"pwr{r_}")
            P.op("dve", lambda e: e.tensor_mul(
                out=A_s32[:, 0:512], in0=A_s32[:, 0:512], in1=A_s32[:, 512:1024]),
                r=[(bs32, 0, 1024)], w=[(bs32, 0, 512)], tag=f"pwm{r_}")
            P.op("dve", lambda e: e.tensor_add(
                out=stS[:], in0=stS[:], in1=A_s32[:, 0:512]),
                r=[(bstS, 0, 512), (bs32, 0, 512)], w=[(bstS, 0, 512)],
                tag=f"pwa{r_}")

        if aparts < 3:
            raise _EndAttn()
        P.op("dve", lambda e: e.memset(stSb[:], 0.0),
             w=[(bstSb, 0, 1024)], tag="sbz")
        for par in range(2):
            P.op("act", lambda e, par=par: e.activation(
                out=stSb[par * 64:par * 64 + 64, :].rearrange(
                    "p (h v) -> p h v", v=64)[:, par::2, :],
                in_=stS[par * 64:par * 64 + 64, :].rearrange(
                    "p (h v) -> p h v", v=64),
                func=AF.Copy),
                r=[(bstS, 0, 512)], w=[(bstSb, 0, 1024)], tag=f"sb0_{par}")
        dump32(1, bstS, stS[:], 0, 512)

        P.dma("pool", lambda e: e.dma_start(
            out=A_w[:].rearrange("p (k n) -> p k n", k=8),
            in_=wqkv_in[0].rearrange("(k p) n -> p k n", p=128)),
            r=[(bw, 0, 8192)], w=[(bw, 0, 8192)], tag="wlq")
        P.dma("sp", lambda e: e.dma_start(out=qksc[:], in_=qscale_in),
              r=[(bqk, 0, 1024)], w=[(bqk, 0, 1024)], tag="ldqsc")
        bscb_ = P.buf("scb")
        bgn_ = P.buf("gnacc")
        bgnst = P.buf("gnst_t")
        P.op("dve", lambda e: e.memset(gnsum[:], 0.0), w=[(bgn_, 0, 8)], tag="gz1")
        P.op("dve", lambda e: e.memset(gnsq[:], 0.0), w=[(bgn_, 8, 16)], tag="gz2")
        bp6 = bp[id(P6)]
        bp5 = bp[id(P5)]
        for c in range(NCH):
            for dt in (0, 4, 1, 5, 2, 6, 3, 7):
                for k in range(8):
                    P.op("pe", lambda e, c=c, dt=dt, k=k: e.matmul(
                        out=P6[:, dt * 128:(dt + 1) * 128],
                        lhsT=A_w[:, k * 1024 + dt * 128:k * 1024 + (dt + 1) * 128],
                        rhs=A_x[:, k * 2048 + c * 128:k * 2048 + (c + 1) * 128],
                        start=(k == 0), stop=(k == 7)),
                        r=[(bw, k * 1024 + dt * 128, k * 1024 + (dt + 1) * 128),
                           (bx, k * 2048 + c * 128, k * 2048 + (c + 1) * 128)],
                        w=[(bp6, dt * 128, (dt + 1) * 128)], tag=f"qp{c}")
            P.op("dve", lambda e: e.tensor_tensor(
                out=A_s16[:, 0:1024], in0=P6[:], in1=qksc[:], op=ALU.mult),
                r=[(bp6, 0, 1024), (bqk, 0, 1024)], w=[(bs16, 0, 1024)],
                tag=f"qev{c}")
            if c == 0:
                dump16(2, bs16, A_s16[:, 0:1024], 0, 1024)
            for k in range(8):
                P.op("pe", lambda e, c=c, k=k: e.transpose(
                    out=P3[:, k * 128:(k + 1) * 128],
                    in_=A_h[:, KT0 + c * 1024 + k * 128:KT0 + c * 1024 + (k + 1) * 128],
                    identity=ident[:]),
                    r=[(bh, KT0 + c * 1024 + k * 128, KT0 + c * 1024 + (k + 1) * 128),
                       (bconst, 0, 1)],
                    w=[(bp[id(P3)], k * 128, (k + 1) * 128)], tag=f"ktp{c}")
            P.op("act", lambda e: e.activation(out=kdbuf[:], in_=P3[:], func=AF.Copy),
                 r=[(bp[id(P3)], 0, 1024)], w=[(bkd, 0, 1024)], tag=f"kde{c}")
            if c == 0:
                dump16(3, bkd, kdbuf[:], 0, 1024)
            for hb in range(2 if aparts >= 4 else 0):
                for q in range(2):
                    quad = hb * 2 + q
                    ps = (P1, P2)[q]
                    pbq = bp[id(ps)]
                    for hh in range(4):
                        h = quad * 4 + hh
                        pr, dt = (h % 2) * 64, h // 2
                        P.op("pe", lambda e, h=h, hh=hh, pr=pr, dt=dt, ps=ps: e.matmul(
                            out=ps[:, hh * 128:(hh + 1) * 128],
                            lhsT=kdbuf[pr:pr + 64, dt * 128:(dt + 1) * 128],
                            rhs=A_s16[pr:pr + 64, dt * 128:(dt + 1) * 128],
                            start=True, stop=True),
                            r=[(bkd, dt * 128, (dt + 1) * 128),
                               (bs16, dt * 128, (dt + 1) * 128)],
                            w=[(pbq, hh * 128, (hh + 1) * 128)], tag=f"sc{c}_{quad}",
                            nop=64)
                    P.op("dve", lambda e, ps=ps, q=q: e.tensor_tensor(
                        out=scb[:, q * 512:(q + 1) * 512].rearrange(
                            "p (g n) -> p g n", g=4),
                        in0=ps[:].rearrange("p (g n) -> p g n", g=4),
                        in1=causal[:].rearrange("p (g n) -> p g n", g=1
                                                ).broadcast_to((128, 4, 128)),
                        op=ALU.mult),
                        r=[(pbq, 0, 512), (bconst, 0, 1)],
                        w=[(bscb_, q * 512, (q + 1) * 512)], tag=f"sce{c}_{quad}")
                for hh8 in range(8 if aparts >= 5 else 0):
                    h = hb * 8 + hh8
                    pr, dt = (h % 2) * 64, h // 2
                    soff = (hh8 // 4) * 512 + (hh8 % 4) * 128
                    if ap5 in ("both", "intra", "noevac"):
                        P.op("pe", lambda e, h=h, pr=pr, dt=dt, soff=soff, c=c,
                             st=(ap5 not in ("both", "cross")): e.matmul(
                            out=P5[pr:pr + 64, dt * 128:(dt + 1) * 128],
                            lhsT=A_h[:, VB0 + c * 1024 + h * 64:VB0 + c * 1024 + (h + 1) * 64],
                            rhs=scb[:, soff:soff + 128],
                            start=True, stop=st),
                            r=[(bh, VB0 + c * 1024 + h * 64, VB0 + c * 1024 + (h + 1) * 64),
                               (bscb_, soff, soff + 128)],
                            w=[(bp5, dt * 128, (dt + 1) * 128)], tag=f"ri{c}_{h}")
                    if ap5 in ("both", "cross", "noevac"):
                        P.op("pe", lambda e, h=h, pr=pr, dt=dt,
                             st=(ap5 not in ("both", "intra")): e.matmul(
                            out=P5[pr:pr + 64, dt * 128:(dt + 1) * 128],
                            lhsT=stSb[:, h * 64:(h + 1) * 64],
                            rhs=A_s16[:, dt * 128:(dt + 1) * 128],
                            start=st, stop=True),
                            r=[(bstSb, h * 64, (h + 1) * 64),
                               (bs16, dt * 128, (dt + 1) * 128)],
                            w=[(bp5, dt * 128, (dt + 1) * 128)], tag=f"rc{c}_{h}",
                            nop=64)
            if c == 0:
                dump16(4, bscb_, scb[:], 0, 1024)
            if aparts >= 7:
                dS_ops(c, "b")
            if aparts < 5 or ap5 == "noevac":
                continue
            P.op("dve", lambda e, c=c: e.tensor_copy(
                out=A_h[:, KT0 + c * 1024:KT0 + (c + 1) * 1024], in_=P5[:]),
                r=[(bp5, 0, 1024)],
                w=[(bh, KT0 + c * 1024, KT0 + (c + 1) * 1024)], tag=f"roe{c}")
            if aparts < 6:
                continue
            if c == 0:
                dump16(5, bh, A_h[:, KT0:KT0 + 1024], KT0, KT0 + 1024)
            P.op("dve", lambda e: e.tensor_reduce(
                out=gnst[:, 0:8], in_=P5[:].rearrange("p (g n) -> p g n", g=8),
                axis=AX.X, op=ALU.add),
                r=[(bp5, 0, 1024)], w=[(bgnst, 0, 8)], tag=f"gs1{c}")
            P.op("dve", lambda e: e.tensor_add(
                out=gnsum[:], in0=gnsum[:], in1=gnst[:, 0:8]),
                r=[(bgn_, 0, 8), (bgnst, 0, 8)], w=[(bgn_, 0, 8)], tag=f"gs2{c}")
            P.op("act", lambda e: e.activation(
                out=kdbuf[:], in_=P5[:], func=AF.Square),
                r=[(bp5, 0, 1024)], w=[(bkd, 0, 1024)], tag=f"gq1{c}")
            P.op("dve", lambda e: e.tensor_reduce(
                out=gnst[:, 8:16], in_=kdbuf[:].rearrange("p (g n) -> p g n", g=8),
                axis=AX.X, op=ALU.add),
                r=[(bkd, 0, 1024)], w=[(bgnst, 8, 16)], tag=f"gq2{c}")
            P.op("dve", lambda e: e.tensor_add(
                out=gnsq[:], in0=gnsq[:], in1=gnst[:, 8:16]),
                r=[(bgn_, 8, 16), (bgnst, 8, 16)], w=[(bgn_, 8, 16)], tag=f"gq3{c}")

        if aparts < 8:
            raise _EndAttn()
        P.op("dve", lambda e: e.tensor_copy(
            out=gnst[:].rearrange("p (k two) -> p k two", two=2)[:, :, 0:1],
            in_=gnsum[:].rearrange("p (k one) -> p k one", one=1)),
            r=[(bgn_, 0, 8)], w=[(bgnst, 0, 16)], tag="gc1")
        P.op("dve", lambda e: e.tensor_copy(
            out=gnst[:].rearrange("p (k two) -> p k two", two=2)[:, :, 1:2],
            in_=gnsq[:].rearrange("p (k one) -> p k one", one=1)),
            r=[(bgn_, 8, 16)], w=[(bgnst, 0, 16)], tag="gc2")
        for k in range(8):
            P.op("pe", lambda e, k=k: e.matmul(
                out=P4[0:16, 0:2], lhsT=hsum[:, k * 16:(k + 1) * 16],
                rhs=gnst[:, 2 * k:2 * k + 2], start=(k == 0), stop=(k == 7)),
                r=[(bconst, 0, 1), (bgnst, 0, 16)],
                w=[(bp4, 0, 2)], tag=f"gst{k}", nop=(64 if k == 7 else 0))
        P.op("dve", lambda e: e.tensor_copy(out=st16[:], in_=P4[0:16, 0:2]),
             r=[(bp4, 0, 2)], w=[(bgn_, 48, 50)], tag="gse")
        P.dma("pool", lambda e: e.dma_start(out=ar_in, in_=st16[:]),
              r=[(bgn_, 48, 50)], w=[(bag, 2, 3)], tag="arw")
        P.cc(lambda e: e.collective_compute(
            "AllReduce", ALU.add, replica_groups=groups_state,
            ins=[ar_in.opt()], outs=[ar_out.opt()]),
            r=[(bag, 2, 3)], w=[(bag, 3, 4)], tag="ar")
        P.dma("sp", lambda e: e.dma_start(out=st16g[:], in_=ar_out),
              r=[(bag, 3, 4)], w=[(bgn_, 50, 52)], tag="arr")
        P.op("dve", lambda e: e.tensor_scalar(
            out=me16[:], in0=st16g[:], scalar1=1.0 / GN_N, scalar2=None,
            op0=ALU.mult),
            r=[(bgn_, 50, 52)], w=[(bgn_, 52, 54)], tag="gm")
        P.op("dve", lambda e: e.tensor_mul(
            out=t16[:], in0=me16[:, 0:1], in1=me16[:, 0:1]),
            r=[(bgn_, 52, 54)], w=[(bgn_, 54, 55)], tag="gm2")
        P.op("dve", lambda e: e.tensor_sub(
            out=v16[:], in0=me16[:, 1:2], in1=t16[:]),
            r=[(bgn_, 52, 55)], w=[(bgn_, 55, 56)], tag="gv")
        P.op("act", lambda e: e.activation(
            out=v16[:], in_=v16[:], func=AF.Ln, bias=np.float32(EPS)),
            r=[(bgn_, 55, 56)], w=[(bgn_, 55, 56)], tag="gln")
        P.op("act", lambda e: e.activation(
            out=v16[:], in_=v16[:], func=AF.Exp, scale=-0.5),
            r=[(bgn_, 55, 56)], w=[(bgn_, 55, 56)], tag="gexp")
        P.op("dve", lambda e: e.tensor_copy(out=mr16[:, 0:1], in_=me16[:, 0:1]),
             r=[(bgn_, 52, 54)], w=[(bgn_, 56, 57)], tag="gp1")
        P.op("dve", lambda e: e.tensor_copy(out=mr16[:, 1:2], in_=v16[:]),
             r=[(bgn_, 55, 56)], w=[(bgn_, 57, 58)], tag="gp2")
        P.op("act", lambda e: e.activation(out=mr16b[:], in_=mr16[:], func=AF.Copy),
             r=[(bgn_, 56, 58)], w=[(bgn_, 58, 59)], tag="gp3")
        for k in range(8):
            P.op("pe", lambda e, k=k: e.matmul(
                out=P4[:, 32:34], lhsT=hexp[:, k * 128:(k + 1) * 128],
                rhs=mr16b[:], start=True, stop=True),
                r=[(bconst, 0, 1), (bgn_, 58, 59)],
                w=[(bp4, 32, 34)], tag=f"ge{k}", nop=64)
            P.op("dve", lambda e: e.tensor_copy(out=mrk[:], in_=P4[:, 32:34]),
                 r=[(bp4, 32, 34)], w=[(bmrk, 0, 2)], tag=f"gee{k}")
            P.op("dve", lambda e, k=k: e.tensor_mul(
                out=gsc[:, k:k + 1], in0=mrk[:, 1:2], in1=gnw2[:, k:k + 1]),
                r=[(bmrk, 0, 2), (bconst, 0, 1)], w=[(bgn_, 60 + k, 61 + k)],
                tag=f"gsc{k}")
            P.op("dve", lambda e, k=k: e.tensor_mul(
                out=mrk[:, 0:1], in0=mrk[:, 0:1], in1=gsc[:, k:k + 1]),
                r=[(bmrk, 0, 2), (bgn_, 60 + k, 61 + k)], w=[(bmrk, 0, 1)],
                tag=f"gbm{k}")
            P.op("dve", lambda e, k=k: e.tensor_sub(
                out=gbi[:, k:k + 1], in0=gnb2[:, k:k + 1], in1=mrk[:, 0:1]),
                r=[(bmrk, 0, 1), (bconst, 0, 1)], w=[(bgn_, 70 + k, 71 + k)],
                tag=f"gbi{k}")
        for k in range(8):
            P.op("dve", lambda e, k=k: e.tensor_scalar(
                out=A_h[:, KT0:KT0 + 16384].rearrange(
                    "p (c x) -> p c x", c=16)[:, :, k * 128:(k + 1) * 128],
                in0=A_h[:, KT0:KT0 + 16384].rearrange(
                    "p (c x) -> p c x", c=16)[:, :, k * 128:(k + 1) * 128],
                scalar1=gsc[:, k:k + 1], scalar2=gbi[:, k:k + 1],
                op0=ALU.mult, op1=ALU.add),
                r=[(bh, KT0, KT0 + 16384), (bgn_, 60 + k, 61 + k),
                   (bgn_, 70 + k, 71 + k)],
                w=[(bh, KT0, KT0 + 16384)], tag=f"gna{k}")
        P.dma("pool", lambda e: e.dma_start(
            out=A_w[:].rearrange("p (k n) -> p k n", k=8),
            in_=wout_in.rearrange("(k p) n -> p k n", p=128)),
            r=[(bw, 0, 8192)], w=[(bw, 0, 8192)], tag="wlo")
        for mt in range(NT):
            ps = (P5, P6)[mt % 2]
            pb = bp[id(ps)]
            for half in range(2):
                for k in range(8):
                    P.op("pe", lambda e, ps=ps, mt=mt, k=k, half=half: e.matmul(
                        out=ps[:, half * 512:(half + 1) * 512],
                        lhsT=A_h[:, KT0 + mt * 1024 + k * 128:KT0 + mt * 1024 + (k + 1) * 128],
                        rhs=A_w[:, k * 1024 + half * 512:k * 1024 + (half + 1) * 512],
                        start=(k == 0), stop=(k == 7)),
                        r=[(bh, KT0 + mt * 1024 + k * 128, KT0 + mt * 1024 + (k + 1) * 128),
                           (bw, k * 1024 + half * 512, k * 1024 + (half + 1) * 512)],
                        w=[(pb, half * 512, (half + 1) * 512)], tag=f"wo{mt}")
            P.op("dve", lambda e, ps=ps, mt=mt: e.tensor_add(
                out=A_res[:, mt * 1024:(mt + 1) * 1024],
                in0=A_res[:, mt * 1024:(mt + 1) * 1024], in1=ps[:]),
                r=[(pb, 0, 1024), (bres, mt * 1024, (mt + 1) * 1024)],
                w=[(bres, mt * 1024, (mt + 1) * 1024)], tag=f"woa{mt}")

      if stage >= 3:
        ln_phase("ln3")
        ffn_phase(first=False)
    except _EndAttn:
        pass

    for g in range(4):
        P.dma("act", lambda e, g=g: e.dma_start(
            out=y_out.rearrange("(t p) d -> p t d", p=128)[:, g * 4:(g + 1) * 4, :],
            in_=A_res[:].rearrange("p (t n) -> p t n", t=NT)[:, g * 4:(g + 1) * 4, :]),
            r=[(bres, g * 4 * 1024, (g + 1) * 4 * 1024)],
            w=[(bout, g, g + 1)], tag=f"sto{g}")

    P.resolve()
    P.emit(nc, ctx)
    ctx.close()
    return nc



def _consts():
    gam = np.array([1.0 - 2.0 ** (-5.0 - math.floor(8.0 * i / H)) for i in range(H)],
                   dtype=np.float64)
    a = np.arange(CH, dtype=np.float64)
    qs = np.zeros((128, 8 * 128), np.float64)
    ks = np.zeros((128, 8 * 128), np.float64)
    for dt in range(8):
        for par in range(2):
            h = dt * 2 + par
            qs[par * 64:(par + 1) * 64, dt * 128:(dt + 1) * 128] = \
                (gam[h] ** a)[None, :] / 8.0
    for h in range(H):
        ks[:, h * 64:(h + 1) * 64] = (gam[h] ** (-a))[:, None]
    causal = (np.arange(CH)[None, :] >= np.arange(CH)[:, None]).astype(np.float32)
    ident = np.eye(128, dtype=np.float32)
    gdec = np.zeros((128, 512), np.float64)
    for h in range(H):
        gdec[(h % 2) * 64:(h % 2) * 64 + 64, (h // 2) * 64:(h // 2) * 64 + 64] = \
            gam[h] ** CH
    hs = np.zeros((128, 8 * 16), np.float32)
    for k in range(8):
        for par in range(2):
            hs[par * 64:(par + 1) * 64, k * 16 + 2 * k + par] = 1.0
    he = np.zeros((16, 8 * 128), np.float32)
    for k in range(8):
        for par in range(2):
            he[2 * k + par, k * 128 + par * 64:k * 128 + (par + 1) * 64] = 1.0
    return gam, qs, ks, causal, ident, gdec, hs, he


def _prefw(core, gam):
    pos = core % 4
    w = np.zeros((4, 128, 512), np.float64)
    for r in range(4):
        if r < pos:
            for h in range(H):
                w[r, (h % 2) * 64:(h % 2) * 64 + 64,
                  (h // 2) * 64:(h // 2) * 64 + 64] = gam[h] ** (T * (pos - 1 - r))
    return w.astype(np.float32)


def kernel(L0_attn_x, L0_attn_Wq, L0_attn_Wk, L0_attn_Wv, L0_attn_gn_weight,
           L0_attn_gn_bias, L0_attn_Wout, L0_ffn_w1, L0_ffn_b1, L0_ffn_w2,
           L0_ffn_b2):
    import ml_dtypes
    x = np.ascontiguousarray(np.asarray(L0_attn_x, dtype=np.float32))
    w1 = np.ascontiguousarray(np.asarray(L0_ffn_w1, dtype=np.float32))
    w2 = np.ascontiguousarray(np.asarray(L0_ffn_w2, dtype=np.float32))
    b1 = np.asarray(L0_ffn_b1, dtype=np.float32)
    b2 = np.asarray(L0_ffn_b2, dtype=np.float32)
    wq = np.asarray(L0_attn_Wq, dtype=np.float32)
    wk = np.asarray(L0_attn_Wk, dtype=np.float32)
    wv = np.asarray(L0_attn_Wv, dtype=np.float32)
    wout = np.ascontiguousarray(np.asarray(L0_attn_Wout, dtype=np.float32))
    gnw = np.asarray(L0_attn_gn_weight, dtype=np.float32)
    gnb = np.asarray(L0_attn_gn_bias, dtype=np.float32)

    gam, qs, ks, causal, ident, gdec, hs, he = _consts()
    wqkv = np.stack([w.transpose(1, 0, 2).reshape(D, D) for w in (wq, wk, wv)])
    wqkv = np.ascontiguousarray(wqkv)
    b1l = np.ascontiguousarray(b1.reshape(32, 128).T)
    gnw2 = np.ascontiguousarray(gnw.reshape(8, 128).T)
    gnb2 = np.ascontiguousarray(gnb.reshape(8, 128).T)

    shared = {
        "w1": w1, "w2": w2, "b1l": b1l, "b2r": b2[None, :].copy(),
        "wqkv": wqkv, "wout": wout, "gnw": gnw2, "gnb": gnb2,
        "qscale": qs.astype(np.float32),
        "kscale": ks.astype(np.float32),
        "causal": causal, "ident": ident.astype(ml_dtypes.bfloat16),
        "gdecay": gdec.astype(np.float32),
        "hsum16": hs, "hexp": he.astype(ml_dtypes.bfloat16),
        "ones1": np.ones((1, 128), ml_dtypes.bfloat16),
    }
    in_maps = []
    for core in range(NCORES):
        b = core // 4
        t0 = (core % 4) * T
        m = dict(shared)
        m["x"] = np.ascontiguousarray(x[b, t0:t0 + T, :])
        m["prefw"] = _prefw(core, gam)
        in_maps.append(m)

    nc = build_nc(_STAGE, _APARTS, _AP5, _DBG)
    res = run_bass_kernel_spmd(nc, in_maps, core_ids=list(range(NCORES)))
    out = np.empty((B, L, D), dtype=np.float32)
    for core in range(NCORES):
        b = core // 4
        t0 = (core % 4) * T
        out[b, t0:t0 + T, :] = res.results[core]["y"]
    return out


# version 05
# speedup vs baseline: 1.1414x; 1.0513x over previous
import math
import numpy as np
from contextlib import ExitStack

import concourse.bass as bass
import concourse.mybir as mybir
from concourse.bass_utils import run_bass_kernel_spmd

F32 = mybir.dt.float32
BF16 = mybir.dt.bfloat16
AF = mybir.ActivationFunctionType
ALU = mybir.AluOpType
AX = mybir.AxisListType

NCORES = 8
B, L, D = 2, 8192, 1024
T = 2048
NT = 16
H, E = 16, 64
HID = 4096
CH = 128
NCH = 16
EPS = 1e-5
GN_N = float(E * L)

_STAGE = 3
_APARTS = 8
_AP5 = "both"
_DBG = 0


ENGINES = ("pe", "act", "dve", "pool", "sp")
DMA_QUEUES = ("sp", "act", "pool")
NLANES = 8


class _Op:
    __slots__ = ("idx", "eng", "fn", "kind", "reads", "writes", "qidx",
                 "deps", "marked", "inc_no", "ccsem", "tag", "wl", "nop_after")

    def __init__(self, idx, eng, fn, kind, reads, writes, tag):
        self.nop_after = 0
        self.idx = idx
        self.eng = eng
        self.fn = fn
        self.kind = kind
        self.reads = reads
        self.writes = writes
        self.qidx = None
        self.deps = set()
        self.marked = False
        self.inc_no = None
        self.ccsem = None
        self.tag = tag


class Buf:
    __slots__ = ("name", "writes", "reads")

    def __init__(self, name):
        self.name = name
        self.writes = []
        self.reads = []


class Prog:
    def __init__(self):
        self.ops = []
        self.bufs = {}

    def buf(self, name):
        b = self.bufs.get(name)
        if b is None:
            b = self.bufs[name] = Buf(name)
        return b

    def op(self, eng, fn, r=(), w=(), kind="c", tag="", nop=0):
        o = _Op(len(self.ops), eng, fn, kind, list(r), list(w), tag)
        o.nop_after = nop
        self.ops.append(o)
        return o

    def dma(self, queue, fn, r=(), w=(), tag=""):
        return self.op(queue, fn, r=r, w=w, kind="d", tag=tag)

    def cc(self, fn, r=(), w=(), tag=""):
        return self.op("pool", fn, r=r, w=w, kind="cc", tag=tag)

    def resolve(self):
        for o in self.ops:
            for (b, lo, hi) in o.reads:
                for seg in b.writes:
                    if seg[0] < hi and lo < seg[1]:
                        o.deps.add(seg[2])
            for (b, lo, hi) in o.writes:
                for seg in b.writes:
                    if seg[0] < hi and lo < seg[1]:
                        o.deps.add(seg[2])
                for seg in b.reads:
                    if seg[0] < hi and lo < seg[1]:
                        o.deps.add(seg[2])
            for (b, lo, hi) in o.writes:
                b.writes = [s for s in b.writes if not (lo <= s[0] and s[1] <= hi)]
                b.reads = [s for s in b.reads if not (lo <= s[0] and s[1] <= hi)]
                b.writes.append([lo, hi, o.idx])
            for (b, lo, hi) in o.reads:
                b.reads.append([lo, hi, o.idx])

        qctr = {q: 0 for q in DMA_QUEUES}
        for o in self.ops:
            if o.kind == "d":
                o.qidx = qctr[o.eng]
                qctr[o.eng] += 1
        self.qtotal = dict(qctr)

        for o in self.ops:
            for d in o.deps:
                p = self.ops[d]
                if p.kind == "c" and (p.eng != o.eng or p.eng != "pe"):
                    p.marked = True
        inc_ctr = {e: 0 for e in ENGINES}
        ncc = 0
        for o in self.ops:
            if o.kind == "c" and o.marked:
                inc_ctr[o.eng] += 1
                o.inc_no = inc_ctr[o.eng]
            elif o.kind == "cc":
                o.ccsem = ncc
                ncc += 1
        self.inc_total = inc_ctr
        self.ncc = ncc

        best = {e: {} for e in ENGINES}
        for o in self.ops:
            waits = {}
            for d in sorted(o.deps):
                p = self.ops[d]
                if p.kind == "d":
                    key = ("q", p.eng, p.qidx % NLANES)
                    val = 16 * (p.qidx // NLANES + 1)
                elif p.kind == "cc":
                    key = ("cc", p.ccsem)
                    val = 1
                else:
                    if p.eng == o.eng and p.eng == "pe":
                        continue
                    key = ("e", p.eng)
                    val = p.inc_no
                    assert val is not None
                if waits.get(key, -1) < val:
                    waits[key] = val
            myb = best[o.eng]
            o.wl = []
            for key, val in sorted(waits.items()):
                if myb.get(key, 0) >= val:
                    continue
                myb[key] = val
                o.wl.append((key, val))

    def emit(self, nc, ctx):
        esems = {e: ctx.enter_context(nc.semaphore(f"es_{e}")) for e in ENGINES}
        spacer = ctx.enter_context(nc.semaphore("spacer"))
        qsems = {q: [ctx.enter_context(nc.semaphore(f"qs_{q}{l}"))
                     for l in range(NLANES)] for q in DMA_QUEUES}
        ccsems = [ctx.enter_context(nc.semaphore(f"cc_{i}")) for i in range(self.ncc)]

        def semof(key):
            if key[0] == "e":
                return esems[key[1]]
            if key[0] == "q":
                return qsems[key[1]][key[2]]
            return ccsems[key[1]]

        per_eng = {e: [] for e in ENGINES}
        for o in self.ops:
            per_eng[o.eng].append(o)

        block = ctx.enter_context(nc.Block())

        def runner(eng_name):
            def run(eng):
                for o in per_eng[eng_name]:
                    for key, val in o.wl:
                        eng.wait_ge(semof(key), val)
                    if o.kind == "d":
                        lane = o.qidx % NLANES
                        use = o.qidx // NLANES
                        if use > 0:
                            eng.wait_ge(qsems[o.eng][lane], 16 * use)
                        inst = o.fn(eng)
                        inst.then_inc(qsems[o.eng][lane], 16)
                        continue
                    inst = o.fn(eng)
                    if o.kind == "cc":
                        inst.then_inc(ccsems[o.ccsem])
                    elif o.marked:
                        inst.then_inc(esems[o.eng], 1)
                    if o.nop_after:
                        eng.sem_inc(spacer, 1)
                        eng.sem_inc(spacer, 1)
                if eng_name in DMA_QUEUES and self.qtotal[eng_name]:
                    tot = self.qtotal[eng_name]
                    for l in range(min(NLANES, tot)):
                        n_l = (tot - 1 - l) // NLANES + 1
                        eng.wait_ge(qsems[eng_name][l], 16 * n_l)
            return run

        block.tensor(runner("pe"))
        block.scalar(runner("act"))
        block.vector(runner("dve"))
        block.gpsimd(runner("pool"))
        block.sync(runner("sp"))



def build_nc(stage, aparts=4, ap5="both", dbg=0):
    nc = bass.Bass("TRN2", target_bir_lowering=False)
    P = Prog()

    d_in = {}
    def din(name, shape, dtype=F32):
        d_in[name] = nc.dram_tensor(name, shape, dtype, kind="ExternalInput").ap()
        return d_in[name]

    x_in = din("x", [T, D])
    w1_in = din("w1", [HID, D])
    w2_in = din("w2", [D, HID])
    b1_in = din("b1l", [128, 32])
    b2_in = din("b2r", [1, D])
    wqkv_in = din("wqkv", [3, D, D])
    wout_in = din("wout", [D, D])
    gnw_in = din("gnw", [128, 8])
    gnb_in = din("gnb", [128, 8])
    qscale_in = din("qscale", [128, 8 * 128])
    kscale_in = din("kscale", [128, 8 * 128])
    causal_in = din("causal", [128, 128])
    ident_in = din("ident", [128, 128], BF16)
    gdecay_in = din("gdecay", [128, 512])
    prefw_in = din("prefw", [4, 128, 512])
    hsum_in = din("hsum16", [128, 8 * 16])
    hexp_in = din("hexp", [16, 8 * 128], BF16)
    ones_in = din("ones1", [1, 128], BF16)

    y_out = nc.dram_tensor("y", [T, D], F32, kind="ExternalOutput").ap()
    if dbg:
        dbgB = nc.dram_tensor("dbgB", [6, 128, 1024], F32, kind="ExternalOutput").ap()
        dbgF = nc.dram_tensor("dbgF", [3, 128, 512], F32, kind="ExternalOutput").ap()
        bdbg = None

    w1t_scr = nc.dram_tensor("w1t_scr", [32, 128, 1024], BF16).ap()
    w2t_scr = nc.dram_tensor("w2t_scr", [32, 128, 1024], BF16).ap()
    ag_in = nc.dram_tensor("ag_in", [128, 512], F32).ap()
    ag_out = nc.dram_tensor("ag_out", [4 * 128, 512], F32).ap()
    ar_in = nc.dram_tensor("ar_in", [16, 2], F32).ap()
    ar_out = nc.dram_tensor("ar_out", [16, 2], F32).ap()

    groups_state = [[0, 1, 2, 3], [4, 5, 6, 7]]

    ctx = ExitStack()
    sb = lambda n, s, d: ctx.enter_context(nc.sbuf_tensor("S_" + n, s, d))
    psal = lambda n, s, d: ctx.enter_context(nc.psum_tensor(n, s, d))

    A_res = sb("A_res", [128, NT * 1024], F32)
    A_x = sb("A_x", [128, 8 * 2048], BF16)
    A_h = sb("A_h", [128, 32768], BF16)
    A_w = sb("A_w", [128, 8192], BF16)
    A_s32 = sb("A_s32", [128, 1024], F32)
    A_s16 = sb("A_s16", [128, 1024], BF16)
    w1tb = sb("w1tb", [128, 2 * 1024], BF16)
    kdbuf = sb("kdbuf", [128, 1024], BF16)
    xlnT = kdbuf
    scb = sb("scb", [128, 1024], BF16)
    qksc = sb("qksc", [128, 1024], F32)
    stS = sb("stS", [128, 512], F32)
    stSb = sb("stSb", [128, 1024], BF16)
    gdecT = sb("gdecT", [128, 512], F32)
    causal = sb("causal", [128, 128], F32)
    ident = sb("ident", [128, 128], BF16)
    ones1 = sb("ones1", [1, 128], BF16)
    b2row = sb("b2row", [1, 1024], BF16)
    b1l = sb("b1l", [128, 32], F32)
    al1 = sb("al1", [128, 32], F32)
    al2 = sb("al2", [128, 8], F32)
    gnw2 = sb("gnw2", [128, 8], F32)
    gnb2 = sb("gnb2", [128, 8], F32)
    lnS1 = sb("lnS1", [128, 16], F32)
    lnS2 = sb("lnS2", [128, 16], F32)
    lnM = sb("lnM", [128, 16], F32)
    lnT = sb("lnT", [128, 16], F32)
    lnR = sb("lnR", [128, 16], F32)
    lnB = sb("lnB", [128, 16], F32)
    gnsum = sb("gnsum", [128, 8], F32)
    gnsq = sb("gnsq", [128, 8], F32)
    gnst = sb("gnst", [128, 16], F32)
    st16 = sb("st16", [16, 2], F32)
    st16g = sb("st16g", [16, 2], F32)
    me16 = sb("me16", [16, 2], F32)
    v16 = sb("v16", [16, 1], F32)
    t16 = sb("t16", [16, 1], F32)
    mr16 = sb("mr16", [16, 2], F32)
    mrk = sb("mrk", [128, 2], F32)
    gsc = sb("gsc", [128, 8], F32)
    gbi = sb("gbi", [128, 8], F32)
    hsum = sb("hsum", [128, 8 * 16], F32)
    hexp = sb("hexp", [16, 8 * 128], BF16)
    mr16b = sb("mr16b", [16, 2], BF16)

    P1 = psal("P1", [128, 512], F32)
    P2 = psal("P2", [128, 512], F32)
    P3 = psal("P3", [128, 1024], BF16)
    P4 = psal("P4", [128, 512], F32)
    P5 = psal("P5", [128, 1024], F32)
    P6 = psal("P6", [128, 1024], F32)

    bres = P.buf("A_res"); bx = P.buf("A_x"); bh = P.buf("A_h"); bw = P.buf("A_w")
    bs32 = P.buf("A_s32"); bs16 = P.buf("A_s16"); bw1tb = P.buf("w1tb")
    bkd = P.buf("kdbuf"); bxlnT = bkd; bscb = P.buf("scb")
    bstS = P.buf("stS"); bstSb = P.buf("stSb")
    bp = {id(P1): P.buf("P1"), id(P2): P.buf("P2"), id(P3): P.buf("P3"),
          id(P4): P.buf("P4"), id(P5): P.buf("P5"), id(P6): P.buf("P6")}
    bconst = P.buf("consts")
    bln = P.buf("lnstats")
    bal = P.buf("alphas")
    bgn = P.buf("gnstats")
    bscr1 = P.buf("w1t_scr")
    bscr2 = P.buf("w2t_scr")
    bdram = P.buf("dram_cc")
    bout = P.buf("yout")


    def ld_const(dst_ap, src_ap, q="sp"):
        P.dma(q, lambda e, o=dst_ap, i=src_ap: e.dma_start(out=o, in_=i),
              w=[(bconst, 0, 1)], tag="ldc")

    ld_const(b1l[:], b1_in)
    ld_const(gnw2[:], gnw_in)
    ld_const(gnb2[:], gnb_in)
    bqk = P.buf("qksc")
    P.dma("sp", lambda e: e.dma_start(out=qksc[:], in_=kscale_in),
          w=[(bqk, 0, 1024)], tag="ldksc")
    ld_const(causal[:], causal_in)
    ld_const(ident[:], ident_in)
    ld_const(gdecT[:], gdecay_in)
    ld_const(hsum[:], hsum_in)
    ld_const(hexp[:], hexp_in)
    ld_const(ones1[:], ones_in)

    for g_ in range(4):
        P.dma("sp", lambda e, g_=g_: e.dma_start(
            out=A_res[:].rearrange("p (t n) -> p t n", t=NT)[:, g_ * 4:(g_ + 1) * 4, :],
            in_=x_in.rearrange("(t p) d -> p t d", p=128)[:, g_ * 4:(g_ + 1) * 4, :]),
            w=[(bres, g_ * 4 * 1024, (g_ + 1) * 4 * 1024)], tag=f"ldx{g_}")

    P.dma("pool", lambda e: e.dma_start(out=b2row[:], in_=b2_in),
          w=[(P.buf("b2row"), 0, 1024)], tag="b2cast")
    bb2 = P.buf("b2row")

    def ln_phase(phase):
        for g in range(4):
            for i in range(g * 4, g * 4 + 4):
                rt = (bres, i * 1024, (i + 1) * 1024)
                tb = (kdbuf, scb)[i % 2]
                tbb = (bkd, bscb)[i % 2]
                P.op("dve", lambda e, i=i: e.tensor_reduce(
                    out=lnS1[:, i:i + 1], in_=A_res[:, i * 1024:(i + 1) * 1024],
                    axis=AX.X, op=ALU.add),
                    r=[rt], w=[(bln, i, i + 1)], tag=f"{phase}s1")
                P.op("act", lambda e, i=i, tb=tb: e.activation(
                    out=tb[:], in_=A_res[:, i * 1024:(i + 1) * 1024],
                    func=AF.Square, accum_out=lnS2[:, i:i + 1]),
                    r=[rt], w=[(bln, 16 + i, 17 + i), (tbb, 0, 1024)],
                    tag=f"{phase}s2")
            lo, hi = g * 4, g * 4 + 4
            P.op("dve", lambda e, lo=lo, hi=hi: e.tensor_scalar(
                out=lnM[:, lo:hi], in0=lnS1[:, lo:hi], scalar1=1.0 / D,
                scalar2=None, op0=ALU.mult),
                r=[(bln, lo, hi)], w=[(bln, 32 + lo, 32 + hi)], tag=f"{phase}mean")
            P.op("dve", lambda e, lo=lo, hi=hi: e.tensor_mul(
                out=lnT[:, lo:hi], in0=lnM[:, lo:hi], in1=lnM[:, lo:hi]),
                r=[(bln, 32 + lo, 32 + hi)], w=[(bln, 48 + lo, 48 + hi)],
                tag=f"{phase}m2")
            P.op("dve", lambda e, lo=lo, hi=hi: e.tensor_scalar(
                out=lnS2[:, lo:hi], in0=lnS2[:, lo:hi], scalar1=1.0 / D,
                scalar2=None, op0=ALU.mult),
                r=[(bln, 16 + lo, 16 + hi)], w=[(bln, 16 + lo, 16 + hi)],
                tag=f"{phase}ex2")
            P.op("dve", lambda e, lo=lo, hi=hi: e.tensor_sub(
                out=lnT[:, lo:hi], in0=lnS2[:, lo:hi], in1=lnT[:, lo:hi]),
                r=[(bln, 16 + lo, 16 + hi), (bln, 48 + lo, 48 + hi)],
                w=[(bln, 48 + lo, 48 + hi)], tag=f"{phase}var")
            P.op("act", lambda e, lo=lo, hi=hi: e.activation(
                out=lnT[:, lo:hi], in_=lnT[:, lo:hi], func=AF.Ln,
                bias=np.float32(EPS)),
                r=[(bln, 48 + lo, 48 + hi)], w=[(bln, 48 + lo, 48 + hi)],
                tag=f"{phase}ln")
            P.op("act", lambda e, lo=lo, hi=hi: e.activation(
                out=lnR[:, lo:hi], in_=lnT[:, lo:hi], func=AF.Exp, scale=-0.5),
                r=[(bln, 48 + lo, 48 + hi)], w=[(bln, 64 + lo, 64 + hi)],
                tag=f"{phase}rstd")
            P.op("dve", lambda e, lo=lo, hi=hi: e.tensor_mul(
                out=lnB[:, lo:hi], in0=lnM[:, lo:hi], in1=lnR[:, lo:hi]),
                r=[(bln, 32 + lo, 32 + hi), (bln, 64 + lo, 64 + hi)],
                w=[(bln, 80 + lo, 80 + hi)], tag=f"{phase}mb")
            P.op("dve", lambda e, lo=lo, hi=hi: e.tensor_scalar(
                out=lnB[:, lo:hi], in0=lnB[:, lo:hi], scalar1=-1.0,
                scalar2=None, op0=ALU.mult),
                r=[(bln, 80 + lo, 80 + hi)], w=[(bln, 80 + lo, 80 + hi)],
                tag=f"{phase}nb")
            for i in range(g * 4, g * 4 + 4):
                tb = (kdbuf, scb)[i % 2]
                tbb = (bkd, bscb)[i % 2]
                P.op("dve", lambda e, i=i, tb=tb: e.tensor_scalar(
                    out=tb[:], in0=A_res[:, i * 1024:(i + 1) * 1024],
                    scalar1=lnR[:, i:i + 1], scalar2=lnB[:, i:i + 1],
                    op0=ALU.mult, op1=ALU.add),
                    r=[(bres, i * 1024, (i + 1) * 1024), (bln, 64 + i, 65 + i),
                       (bln, 80 + i, 81 + i)],
                    w=[(tbb, 0, 1024)], tag=f"{phase}app{i}")
                for k in range(8):
                    P.op("pe", lambda e, k=k, tb=tb: e.transpose(
                        out=P3[:, k * 128:(k + 1) * 128],
                        in_=tb[:, k * 128:(k + 1) * 128], identity=ident[:]),
                        r=[(tbb, k * 128, (k + 1) * 128), (bconst, 0, 1)],
                        w=[(bp[id(P3)], k * 128, (k + 1) * 128)], tag=f"{phase}tp")
                P.op("act", lambda e, i=i: e.activation(
                    out=A_x[:].rearrange("p (k n) -> p k n", k=8)[:, :, i * 128:(i + 1) * 128],
                    in_=P3[:].rearrange("p (k n) -> p k n", k=8), func=AF.Copy),
                    r=[(bp[id(P3)], 0, 1024)],
                    w=[(bx, k * 2048 + i * 128, k * 2048 + (i + 1) * 128) for k in range(8)],
                    tag=f"{phase}xev{i}")

    bmrk = P.buf("mrk")

    def w1_prep(ms):
        slot = (ms % 2) * 1024
        P.dma("sp", lambda e, ms=ms: e.dma_start(
            out=A_s32[:, 0:1024], in_=w1_in[ms * 128:(ms + 1) * 128, :]),
            w=[(bs32, 0, 1024)], tag=f"w1ld{ms}")
        P.op("dve", lambda e, ms=ms: e.tensor_reduce(
            out=al1[:, ms:ms + 1], in_=A_s32[:, 0:1024], axis=AX.X,
            op=ALU.add, apply_absolute_value=True),
            r=[(bs32, 0, 1024)], w=[(bal, ms, ms + 1)], tag=f"w1al{ms}")
        P.op("act", lambda e: e.activation(
            out=A_s16[:, 0:1024], in_=A_s32[:, 0:1024], func=AF.Sign),
            r=[(bs32, 0, 1024)], w=[(bs16, 0, 1024)], tag=f"w1sg{ms}")
        P.op("dve", lambda e, ms=ms: e.tensor_scalar(
            out=A_s16[:, 0:1024], in0=A_s16[:, 0:1024],
            scalar1=al1[:, ms:ms + 1], scalar2=1.0 / D, op0=ALU.mult, op1=ALU.mult),
            r=[(bs16, 0, 1024), (bal, ms, ms + 1)], w=[(bs16, 0, 1024)], tag=f"w1sc{ms}")
        for k in range(8):
            P.op("pe", lambda e, k=k: e.transpose(
                out=P3[:, k * 128:(k + 1) * 128],
                in_=A_s16[:, k * 128:(k + 1) * 128], identity=ident[:]),
                r=[(bs16, k * 128, (k + 1) * 128), (bconst, 0, 1)],
                w=[(bp[id(P3)], k * 128, (k + 1) * 128)], tag=f"w1tp{ms}")
        P.op("act", lambda e, slot=slot: e.activation(
            out=w1tb[:, slot:slot + 1024], in_=P3[:], func=AF.Copy),
            r=[(bp[id(P3)], 0, 1024)], w=[(bw1tb, slot, slot + 1024)], tag=f"w1ev{ms}")
        P.dma("act", lambda e, ms=ms, slot=slot: e.dma_start(
            out=w1t_scr[ms], in_=w1tb[:, slot:slot + 1024]),
            r=[(bw1tb, slot, slot + 1024)], w=[(bscr1, ms, ms + 1)], tag=f"w1st{ms}")

    def w2_alpha(kd):
        for qq in range(4):
            P.dma("pool", lambda e, kd=kd, qq=qq: e.dma_start(
                out=A_s32[:, 0:1024],
                in_=w2_in[kd * 128:(kd + 1) * 128, qq * 1024:(qq + 1) * 1024]),
                w=[(bs32, 0, 1024)], tag=f"w2a_ld{kd}_{qq}")
            if qq == 0:
                P.op("dve", lambda e, kd=kd: e.tensor_reduce(
                    out=al2[:, kd:kd + 1], in_=A_s32[:, 0:1024], axis=AX.X,
                    op=ALU.add, apply_absolute_value=True),
                    r=[(bs32, 0, 1024)], w=[(bal, 40 + kd, 41 + kd)], tag=f"w2a1{kd}")
            else:
                P.op("dve", lambda e: e.tensor_reduce(
                    out=mrk[:, 0:1], in_=A_s32[:, 0:1024], axis=AX.X,
                    op=ALU.add, apply_absolute_value=True),
                    r=[(bs32, 0, 1024)], w=[(bmrk, 0, 1)], tag=f"w2a2{kd}_{qq}")
                P.op("dve", lambda e, kd=kd: e.tensor_add(
                    out=al2[:, kd:kd + 1], in0=al2[:, kd:kd + 1], in1=mrk[:, 0:1]),
                    r=[(bal, 40 + kd, 41 + kd), (bmrk, 0, 1)],
                    w=[(bal, 40 + kd, 41 + kd)], tag=f"w2aa{kd}_{qq}")
        P.op("dve", lambda e, kd=kd: e.tensor_scalar(
            out=al2[:, kd:kd + 1], in0=al2[:, kd:kd + 1],
            scalar1=1.0 / HID, scalar2=None, op0=ALU.mult),
            r=[(bal, 40 + kd, 41 + kd)], w=[(bal, 40 + kd, 41 + kd)], tag=f"w2am{kd}")

    def w2_prep2(kq):
        q, kd = divmod(kq, 8)
        P.dma("pool", lambda e, kd=kd, q=q: e.dma_start(
            out=A_s32[:, 0:1024],
            in_=w2_in[kd * 128:(kd + 1) * 128, q * 1024:(q + 1) * 1024]),
            w=[(bs32, 0, 1024)], tag=f"w2ld{kq}")
        P.op("act", lambda e: e.activation(
            out=A_s16[:, 0:1024], in_=A_s32[:, 0:1024], func=AF.Sign),
            r=[(bs32, 0, 1024)], w=[(bs16, 0, 1024)], tag=f"w2sg{kq}")
        P.op("dve", lambda e, kd=kd: e.tensor_scalar(
            out=A_s16[:, 0:1024], in0=A_s16[:, 0:1024],
            scalar1=al2[:, kd:kd + 1], scalar2=None, op0=ALU.mult),
            r=[(bs16, 0, 1024), (bal, 40 + kd, 41 + kd)],
            w=[(bs16, 0, 1024)], tag=f"w2sc{kq}")
        for j in range(8):
            P.op("pe", lambda e, j=j: e.transpose(
                out=P3[:, j * 128:(j + 1) * 128],
                in_=A_s16[:, j * 128:(j + 1) * 128], identity=ident[:]),
                r=[(bs16, j * 128, (j + 1) * 128), (bconst, 0, 1)],
                w=[(bp[id(P3)], j * 128, (j + 1) * 128)], tag=f"w2tp{kq}")
        P.op("act", lambda e: e.activation(
            out=kdbuf[:], in_=P3[:], func=AF.Copy),
            r=[(bp[id(P3)], 0, 1024)], w=[(bkd, 0, 1024)], tag=f"w2ev{kq}")
        P.dma("act", lambda e, kd=kd, q=q: e.dma_start(
            out=w2t_scr[q * 8:(q + 1) * 8, :, kd * 128:(kd + 1) * 128].rearrange(
                "j p d -> p j d"),
            in_=kdbuf[:].rearrange("p (j d) -> p j d", j=8)),
            r=[(bkd, 0, 1024)],
            w=[(bscr2, (q * 8 + j) * 8 + kd, (q * 8 + j) * 8 + kd + 1)
               for j in range(8)],
            tag=f"w2st{kq}")

    def ffn_phase(first):
        w2q_emitted = 0
        for pair in range(2):
            for mloc in range(16):
                ms = pair * 16 + mloc
                slot = (ms % 2) * 1024
                if first:
                    w1_prep(ms)
                    if ms < 8:
                        w2_alpha(ms)
                    else:
                        for _ in range(2):
                            if w2q_emitted < 32:
                                w2_prep2(w2q_emitted)
                                w2q_emitted += 1
                else:
                    P.dma("sp", lambda e, ms=ms, slot=slot: e.dma_start(
                        out=w1tb[:, slot:slot + 1024], in_=w1t_scr[ms]),
                        r=[(bscr1, ms, ms + 1)], w=[(bw1tb, slot, slot + 1024)],
                        tag=f"w1rd{ms}")
                for n in range(4):
                    ps = (P1, P2)[n % 2]
                    pb = bp[id(ps)]
                    for k in range(8):
                        P.op("pe", lambda e, ps=ps, slot=slot, k=k, n=n: e.matmul(
                            out=ps[:, 0:512],
                            lhsT=w1tb[:, slot + k * 128:slot + (k + 1) * 128],
                            rhs=A_x[:, k * 2048 + n * 512:k * 2048 + (n + 1) * 512],
                            start=(k == 0), stop=(k == 7)),
                            r=[(bw1tb, slot + k * 128, slot + (k + 1) * 128),
                               (bx, k * 2048 + n * 512, k * 2048 + (n + 1) * 512)],
                            w=[(pb, 0, 512)], tag=f"mm1_{ms}_{n}")
                    P.op("act", lambda e, ps=ps, mloc=mloc, n=n, ms=ms: e.activation(
                        out=A_h[:, mloc * 2048 + n * 512:mloc * 2048 + (n + 1) * 512],
                        in_=ps[:, 0:512], func=AF.Gelu, bias=b1l[:, ms:ms + 1]),
                        r=[(pb, 0, 512), (bconst, 0, 1)],
                        w=[(bh, mloc * 2048 + n * 512, mloc * 2048 + (n + 1) * 512)],
                        tag=f"gelu_{ms}_{n}")
            for q in range(2):
                s = pair * 2 + q
                P.dma("sp", lambda e, s=s: e.dma_start(
                    out=A_w[:].rearrange("p (j d) -> p j d", j=8),
                    in_=w2t_scr[s * 8:(s + 1) * 8].rearrange("j p d -> p j d")),
                    r=[(bscr2, s * 8 * 8, (s + 1) * 8 * 8)],
                    w=[(bw, 0, 8192)], tag=f"w2rd{s}")
                for mt in range(NT):
                    ps = (P5, P6)[mt % 2]
                    pb = bp[id(ps)]
                    nfirst = True
                    if s == 0:
                        for half in range(2):
                            P.op("pe", lambda e, ps=ps, half=half: e.matmul(
                                out=ps[:, half * 512:(half + 1) * 512],
                                lhsT=ones1[:],
                                rhs=b2row[:, half * 512:(half + 1) * 512],
                                start=True, stop=False),
                                r=[(bconst, 0, 1), (bb2, 0, 1024)],
                                w=[(pb, half * 512, (half + 1) * 512)],
                                tag=f"b2_{mt}")
                        nfirst = False
                    for j in range(8):
                        mloc = q * 8 + j
                        for half in range(2):
                            P.op("pe", lambda e, ps=ps, mloc=mloc, mt=mt, j=j,
                                 half=half, st=(nfirst and j == 0): e.matmul(
                                out=ps[:, half * 512:(half + 1) * 512],
                                lhsT=A_h[:, mloc * 2048 + mt * 128:mloc * 2048 + (mt + 1) * 128],
                                rhs=A_w[:, j * 1024 + half * 512:j * 1024 + (half + 1) * 512],
                                start=st, stop=(j == 7)),
                                r=[(bh, mloc * 2048 + mt * 128, mloc * 2048 + (mt + 1) * 128),
                                   (bw, j * 1024 + half * 512, j * 1024 + (half + 1) * 512)],
                                w=[(pb, half * 512, (half + 1) * 512)],
                                tag=f"mm2_{s}_{mt}")
                    P.op("dve", lambda e, ps=ps, mt=mt: e.scalar_tensor_tensor(
                        out=A_res[:, mt * 1024:(mt + 1) * 1024], in0=ps[:],
                        scalar=0.5, in1=A_res[:, mt * 1024:(mt + 1) * 1024],
                        op0=ALU.mult, op1=ALU.add),
                        r=[(pb, 0, 1024), (bres, mt * 1024, (mt + 1) * 1024)],
                        w=[(bres, mt * 1024, (mt + 1) * 1024)], tag=f"racc_{s}_{mt}")

    ln_phase("ln1")
    ffn_phase(first=True)

    class _EndAttn(Exception):
        pass
    try:
      if stage >= 2:
        bag = P.buf("dram_cc")
        KT0, VB0 = 0, 16384
        bdbgq = P.buf("dbgq")

        def dump16(idx, buf, ap, lo, hi):
            if not dbg:
                return
            P.dma("pool", lambda e, idx=idx, ap=ap: e.dma_start(
                out=dbgB[idx], in_=ap), r=[(buf, lo, hi)],
                w=[(bdbgq, idx, idx + 1)], tag=f"dbg{idx}")

        def dump32(idx, buf, ap, lo, hi):
            if not dbg:
                return
            P.dma("pool", lambda e, idx=idx, ap=ap: e.dma_start(
                out=dbgF[idx], in_=ap), r=[(buf, lo, hi)],
                w=[(bdbgq, 16 + idx, 17 + idx)], tag=f"dbf{idx}")

        ln_phase("ln2")

        def proj_pass(widx, out_base, scale_evac):
            P.dma("pool", lambda e, widx=widx: e.dma_start(
                out=A_w[:].rearrange("p (k n) -> p k n", k=8),
                in_=wqkv_in[widx].rearrange("(k p) n -> p k n", p=128)),
                r=[(bw, 0, 8192)], w=[(bw, 0, 8192)], tag=f"wl{widx}")
            for c in range(NCH):
                ps = (P5, P6)[c % 2]
                pb = bp[id(ps)]
                for half in range(2):
                    for k in range(8):
                        P.op("pe", lambda e, ps=ps, c=c, k=k, half=half: e.matmul(
                            out=ps[:, half * 512:(half + 1) * 512],
                            lhsT=A_x[:, k * 2048 + c * 128:k * 2048 + (c + 1) * 128],
                            rhs=A_w[:, k * 1024 + half * 512:k * 1024 + (half + 1) * 512],
                            start=(k == 0), stop=(k == 7)),
                            r=[(bx, k * 2048 + c * 128, k * 2048 + (c + 1) * 128),
                               (bw, k * 1024 + half * 512, k * 1024 + (half + 1) * 512)],
                            w=[(pb, half * 512, (half + 1) * 512)], tag=f"pj{widx}_{c}")
                scale_evac(c, ps, pb, out_base)

        def k_evac(c, ps, pb, base):
            P.op("dve", lambda e, ps=ps, c=c, base=base: e.tensor_tensor(
                out=A_h[:, base + c * 1024:base + (c + 1) * 1024],
                in0=ps[:], in1=qksc[:], op=ALU.mult),
                r=[(pb, 0, 1024), (bqk, 0, 1024)],
                w=[(bh, base + c * 1024, base + (c + 1) * 1024)], tag=f"kev{c}")

        def v_evac(c, ps, pb, base):
            P.op("act", lambda e, ps=ps, c=c, base=base: e.activation(
                out=A_h[:, base + c * 1024:base + (c + 1) * 1024],
                in_=ps[:], func=AF.Copy),
                r=[(pb, 0, 1024)],
                w=[(bh, base + c * 1024, base + (c + 1) * 1024)], tag=f"vev{c}")

        proj_pass(1, KT0, k_evac)
        proj_pass(2, VB0, v_evac)
        dump16(0, bh, A_h[:, KT0:KT0 + 1024], KT0, KT0 + 1024)
        dump16(1, bh, A_h[:, VB0:VB0 + 1024], VB0, VB0 + 1024)

        if aparts < 2:
            raise _EndAttn()
        P.op("dve", lambda e: e.memset(stS[:], 0.0),
             w=[(bstS, 0, 512)], tag="st0")
        bp4 = bp[id(P4)]

        def dS_ops(c, tagp):
            bp1_ = bp[id(P1)]
            for pair in range(8):
                ps = (P4, P1)[pair % 2]
                pbp = (bp4, bp1_)[pair % 2]
                reg = (pair // 2) * 128
                P.op("pe", lambda e, c=c, pair=pair, ps=ps, reg=reg: e.matmul(
                    out=ps[:, reg:reg + 128],
                    lhsT=A_h[:, KT0 + c * 1024 + pair * 128:KT0 + c * 1024 + (pair + 1) * 128],
                    rhs=A_h[:, VB0 + c * 1024 + pair * 128:VB0 + c * 1024 + (pair + 1) * 128],
                    start=True, stop=True),
                    r=[(bh, KT0 + c * 1024 + pair * 128, KT0 + c * 1024 + (pair + 1) * 128),
                       (bh, VB0 + c * 1024 + pair * 128, VB0 + c * 1024 + (pair + 1) * 128)],
                    w=[(pbp, reg, reg + 128)], tag=f"{tagp}dS{c}", nop=64)
            for par in range(2):
                for bsel in range(2):
                    ps = (P4, P1)[bsel]
                    pbp = (bp4, bp1_)[bsel]
                    P.op("dve", lambda e, par=par, ps=ps, bsel=bsel: e.tensor_add(
                        out=stS[par * 64:par * 64 + 64, :].rearrange(
                            "p (q v) -> p q v", v=64)[:, bsel::2, :],
                        in0=stS[par * 64:par * 64 + 64, :].rearrange(
                            "p (q v) -> p q v", v=64)[:, bsel::2, :],
                        in1=ps[par * 64:par * 64 + 64, :].rearrange(
                            "p (r v) -> p r v", v=64)[:, par::2, :]),
                        r=[(bstS, 0, 512), (pbp, 0, 512)], w=[(bstS, 0, 512)],
                        tag=f"{tagp}su1_{c}_{par}{bsel}")
            P.op("dve", lambda e: e.tensor_mul(out=stS[:], in0=stS[:], in1=gdecT[:]),
                 r=[(bstS, 0, 512), (bconst, 0, 1)], w=[(bstS, 0, 512)],
                 tag=f"{tagp}su2_{c}")
            if tagp == "b":
                for par in range(2):
                    P.op("act", lambda e, par=par: e.activation(
                        out=stSb[par * 64:par * 64 + 64, :].rearrange(
                            "p (h v) -> p h v", v=64)[:, par::2, :],
                        in_=stS[par * 64:par * 64 + 64, :].rearrange(
                            "p (h v) -> p h v", v=64),
                        func=AF.Copy),
                        r=[(bstS, 0, 512)], w=[(bstSb, 0, 1024)], tag=f"bsb{c}_{par}")

        for c in range(NCH):
            dS_ops(c, "a")

        dump32(0, bstS, stS[:], 0, 512)
        P.dma("pool", lambda e: e.dma_start(out=ag_in, in_=stS[:]),
              r=[(bstS, 0, 512)], w=[(bag, 0, 1)], tag="agw")
        P.cc(lambda e: e.collective_compute(
            "AllGather", ALU.bypass, replica_groups=groups_state,
            ins=[ag_in.opt()], outs=[ag_out.opt()]),
            r=[(bag, 0, 1)], w=[(bag, 1, 2)], tag="ag")
        P.op("dve", lambda e: e.memset(stS[:], 0.0),
             r=[(bag, 0, 1)], w=[(bstS, 0, 512)], tag="stz")
        for r_ in range(4):
            P.dma("sp", lambda e, r_=r_: e.dma_start(
                out=A_s32[:, 0:512], in_=ag_out[r_ * 128:(r_ + 1) * 128, :]),
                r=[(bag, 1, 2)], w=[(bs32, 0, 512)], tag=f"agr{r_}")
            P.dma("sp", lambda e, r_=r_: e.dma_start(
                out=A_s32[:, 512:1024], in_=prefw_in[r_]),
                w=[(bs32, 512, 1024)], tag=f"pwr{r_}")
            P.op("dve", lambda e: e.tensor_mul(
                out=A_s32[:, 0:512], in0=A_s32[:, 0:512], in1=A_s32[:, 512:1024]),
                r=[(bs32, 0, 1024)], w=[(bs32, 0, 512)], tag=f"pwm{r_}")
            P.op("dve", lambda e: e.tensor_add(
                out=stS[:], in0=stS[:], in1=A_s32[:, 0:512]),
                r=[(bstS, 0, 512), (bs32, 0, 512)], w=[(bstS, 0, 512)],
                tag=f"pwa{r_}")

        if aparts < 3:
            raise _EndAttn()
        P.op("dve", lambda e: e.memset(stSb[:], 0.0),
             w=[(bstSb, 0, 1024)], tag="sbz")
        for par in range(2):
            P.op("act", lambda e, par=par: e.activation(
                out=stSb[par * 64:par * 64 + 64, :].rearrange(
                    "p (h v) -> p h v", v=64)[:, par::2, :],
                in_=stS[par * 64:par * 64 + 64, :].rearrange(
                    "p (h v) -> p h v", v=64),
                func=AF.Copy),
                r=[(bstS, 0, 512)], w=[(bstSb, 0, 1024)], tag=f"sb0_{par}")
        dump32(1, bstS, stS[:], 0, 512)

        P.dma("pool", lambda e: e.dma_start(
            out=A_w[:].rearrange("p (k n) -> p k n", k=8),
            in_=wqkv_in[0].rearrange("(k p) n -> p k n", p=128)),
            r=[(bw, 0, 8192)], w=[(bw, 0, 8192)], tag="wlq")
        P.dma("sp", lambda e: e.dma_start(out=qksc[:], in_=qscale_in),
              r=[(bqk, 0, 1024)], w=[(bqk, 0, 1024)], tag="ldqsc")
        bscb_ = P.buf("scb")
        bgn_ = P.buf("gnacc")
        bgnst = P.buf("gnst_t")
        P.op("dve", lambda e: e.memset(gnsum[:], 0.0), w=[(bgn_, 0, 8)], tag="gz1")
        P.op("dve", lambda e: e.memset(gnsq[:], 0.0), w=[(bgn_, 8, 16)], tag="gz2")
        bp6 = bp[id(P6)]
        bp5 = bp[id(P5)]
        for c in range(NCH):
            for dt in (0, 4, 1, 5, 2, 6, 3, 7):
                for k in range(8):
                    P.op("pe", lambda e, c=c, dt=dt, k=k: e.matmul(
                        out=P6[:, dt * 128:(dt + 1) * 128],
                        lhsT=A_w[:, k * 1024 + dt * 128:k * 1024 + (dt + 1) * 128],
                        rhs=A_x[:, k * 2048 + c * 128:k * 2048 + (c + 1) * 128],
                        start=(k == 0), stop=(k == 7)),
                        r=[(bw, k * 1024 + dt * 128, k * 1024 + (dt + 1) * 128),
                           (bx, k * 2048 + c * 128, k * 2048 + (c + 1) * 128)],
                        w=[(bp6, dt * 128, (dt + 1) * 128)], tag=f"qp{c}")
            P.op("dve", lambda e: e.tensor_tensor(
                out=A_s16[:, 0:1024], in0=P6[:], in1=qksc[:], op=ALU.mult),
                r=[(bp6, 0, 1024), (bqk, 0, 1024)], w=[(bs16, 0, 1024)],
                tag=f"qev{c}")
            if c == 0:
                dump16(2, bs16, A_s16[:, 0:1024], 0, 1024)
            for k in range(8):
                P.op("pe", lambda e, c=c, k=k: e.transpose(
                    out=P3[:, k * 128:(k + 1) * 128],
                    in_=A_h[:, KT0 + c * 1024 + k * 128:KT0 + c * 1024 + (k + 1) * 128],
                    identity=ident[:]),
                    r=[(bh, KT0 + c * 1024 + k * 128, KT0 + c * 1024 + (k + 1) * 128),
                       (bconst, 0, 1)],
                    w=[(bp[id(P3)], k * 128, (k + 1) * 128)], tag=f"ktp{c}")
            P.op("act", lambda e: e.activation(out=kdbuf[:], in_=P3[:], func=AF.Copy),
                 r=[(bp[id(P3)], 0, 1024)], w=[(bkd, 0, 1024)], tag=f"kde{c}")
            if c == 0:
                dump16(3, bkd, kdbuf[:], 0, 1024)
            for hb in range(2 if aparts >= 4 else 0):
                for q in range(2):
                    quad = hb * 2 + q
                    ps = (P1, P2)[q]
                    pbq = bp[id(ps)]
                    for hh in range(4):
                        h = quad * 4 + hh
                        pr, dt = (h % 2) * 64, h // 2
                        P.op("pe", lambda e, h=h, hh=hh, pr=pr, dt=dt, ps=ps: e.matmul(
                            out=ps[:, hh * 128:(hh + 1) * 128],
                            lhsT=kdbuf[pr:pr + 64, dt * 128:(dt + 1) * 128],
                            rhs=A_s16[pr:pr + 64, dt * 128:(dt + 1) * 128],
                            start=True, stop=True),
                            r=[(bkd, dt * 128, (dt + 1) * 128),
                               (bs16, dt * 128, (dt + 1) * 128)],
                            w=[(pbq, hh * 128, (hh + 1) * 128)], tag=f"sc{c}_{quad}",
                            nop=64)
                    P.op("dve", lambda e, ps=ps, q=q: e.tensor_tensor(
                        out=scb[:, q * 512:(q + 1) * 512].rearrange(
                            "p (g n) -> p g n", g=4),
                        in0=ps[:].rearrange("p (g n) -> p g n", g=4),
                        in1=causal[:].rearrange("p (g n) -> p g n", g=1
                                                ).broadcast_to((128, 4, 128)),
                        op=ALU.mult),
                        r=[(pbq, 0, 512), (bconst, 0, 1)],
                        w=[(bscb_, q * 512, (q + 1) * 512)], tag=f"sce{c}_{quad}")
                for hh8 in range(8 if aparts >= 5 else 0):
                    h = hb * 8 + hh8
                    pr, dt = (h % 2) * 64, h // 2
                    soff = (hh8 // 4) * 512 + (hh8 % 4) * 128
                    if ap5 in ("both", "intra", "noevac"):
                        P.op("pe", lambda e, h=h, pr=pr, dt=dt, soff=soff, c=c,
                             st=(ap5 not in ("both", "cross")): e.matmul(
                            out=P5[pr:pr + 64, dt * 128:(dt + 1) * 128],
                            lhsT=A_h[:, VB0 + c * 1024 + h * 64:VB0 + c * 1024 + (h + 1) * 64],
                            rhs=scb[:, soff:soff + 128],
                            start=True, stop=st),
                            r=[(bh, VB0 + c * 1024 + h * 64, VB0 + c * 1024 + (h + 1) * 64),
                               (bscb_, soff, soff + 128)],
                            w=[(bp5, dt * 128, (dt + 1) * 128)], tag=f"ri{c}_{h}")
                    if ap5 in ("both", "cross", "noevac"):
                        P.op("pe", lambda e, h=h, pr=pr, dt=dt,
                             st=(ap5 not in ("both", "intra")): e.matmul(
                            out=P5[pr:pr + 64, dt * 128:(dt + 1) * 128],
                            lhsT=stSb[:, h * 64:(h + 1) * 64],
                            rhs=A_s16[:, dt * 128:(dt + 1) * 128],
                            start=st, stop=True),
                            r=[(bstSb, h * 64, (h + 1) * 64),
                               (bs16, dt * 128, (dt + 1) * 128)],
                            w=[(bp5, dt * 128, (dt + 1) * 128)], tag=f"rc{c}_{h}",
                            nop=64)
            if c == 0:
                dump16(4, bscb_, scb[:], 0, 1024)
            if aparts >= 7:
                dS_ops(c, "b")
            if aparts < 5 or ap5 == "noevac":
                continue
            P.op("dve", lambda e, c=c: e.tensor_copy(
                out=A_h[:, KT0 + c * 1024:KT0 + (c + 1) * 1024], in_=P5[:]),
                r=[(bp5, 0, 1024)],
                w=[(bh, KT0 + c * 1024, KT0 + (c + 1) * 1024)], tag=f"roe{c}")
            if aparts < 6:
                continue
            if c == 0:
                dump16(5, bh, A_h[:, KT0:KT0 + 1024], KT0, KT0 + 1024)
            P.op("dve", lambda e: e.tensor_reduce(
                out=gnst[:, 0:8], in_=P5[:].rearrange("p (g n) -> p g n", g=8),
                axis=AX.X, op=ALU.add),
                r=[(bp5, 0, 1024)], w=[(bgnst, 0, 8)], tag=f"gs1{c}")
            P.op("dve", lambda e: e.tensor_add(
                out=gnsum[:], in0=gnsum[:], in1=gnst[:, 0:8]),
                r=[(bgn_, 0, 8), (bgnst, 0, 8)], w=[(bgn_, 0, 8)], tag=f"gs2{c}")
            P.op("act", lambda e: e.activation(
                out=kdbuf[:], in_=P5[:], func=AF.Square),
                r=[(bp5, 0, 1024)], w=[(bkd, 0, 1024)], tag=f"gq1{c}")
            P.op("dve", lambda e: e.tensor_reduce(
                out=gnst[:, 8:16], in_=kdbuf[:].rearrange("p (g n) -> p g n", g=8),
                axis=AX.X, op=ALU.add),
                r=[(bkd, 0, 1024)], w=[(bgnst, 8, 16)], tag=f"gq2{c}")
            P.op("dve", lambda e: e.tensor_add(
                out=gnsq[:], in0=gnsq[:], in1=gnst[:, 8:16]),
                r=[(bgn_, 8, 16), (bgnst, 8, 16)], w=[(bgn_, 8, 16)], tag=f"gq3{c}")

        if aparts < 8:
            raise _EndAttn()
        P.op("dve", lambda e: e.tensor_copy(
            out=gnst[:].rearrange("p (k two) -> p k two", two=2)[:, :, 0:1],
            in_=gnsum[:].rearrange("p (k one) -> p k one", one=1)),
            r=[(bgn_, 0, 8)], w=[(bgnst, 0, 16)], tag="gc1")
        P.op("dve", lambda e: e.tensor_copy(
            out=gnst[:].rearrange("p (k two) -> p k two", two=2)[:, :, 1:2],
            in_=gnsq[:].rearrange("p (k one) -> p k one", one=1)),
            r=[(bgn_, 8, 16)], w=[(bgnst, 0, 16)], tag="gc2")
        for k in range(8):
            P.op("pe", lambda e, k=k: e.matmul(
                out=P4[0:16, 0:2], lhsT=hsum[:, k * 16:(k + 1) * 16],
                rhs=gnst[:, 2 * k:2 * k + 2], start=(k == 0), stop=(k == 7)),
                r=[(bconst, 0, 1), (bgnst, 0, 16)],
                w=[(bp4, 0, 2)], tag=f"gst{k}", nop=(64 if k == 7 else 0))
        P.op("dve", lambda e: e.tensor_copy(out=st16[:], in_=P4[0:16, 0:2]),
             r=[(bp4, 0, 2)], w=[(bgn_, 48, 50)], tag="gse")
        P.dma("pool", lambda e: e.dma_start(out=ar_in, in_=st16[:]),
              r=[(bgn_, 48, 50)], w=[(bag, 2, 3)], tag="arw")
        P.cc(lambda e: e.collective_compute(
            "AllReduce", ALU.add, replica_groups=groups_state,
            ins=[ar_in.opt()], outs=[ar_out.opt()]),
            r=[(bag, 2, 3)], w=[(bag, 3, 4)], tag="ar")
        P.dma("sp", lambda e: e.dma_start(out=st16g[:], in_=ar_out),
              r=[(bag, 3, 4)], w=[(bgn_, 50, 52)], tag="arr")
        P.op("dve", lambda e: e.tensor_scalar(
            out=me16[:], in0=st16g[:], scalar1=1.0 / GN_N, scalar2=None,
            op0=ALU.mult),
            r=[(bgn_, 50, 52)], w=[(bgn_, 52, 54)], tag="gm")
        P.op("dve", lambda e: e.tensor_mul(
            out=t16[:], in0=me16[:, 0:1], in1=me16[:, 0:1]),
            r=[(bgn_, 52, 54)], w=[(bgn_, 54, 55)], tag="gm2")
        P.op("dve", lambda e: e.tensor_sub(
            out=v16[:], in0=me16[:, 1:2], in1=t16[:]),
            r=[(bgn_, 52, 55)], w=[(bgn_, 55, 56)], tag="gv")
        P.op("act", lambda e: e.activation(
            out=v16[:], in_=v16[:], func=AF.Ln, bias=np.float32(EPS)),
            r=[(bgn_, 55, 56)], w=[(bgn_, 55, 56)], tag="gln")
        P.op("act", lambda e: e.activation(
            out=v16[:], in_=v16[:], func=AF.Exp, scale=-0.5),
            r=[(bgn_, 55, 56)], w=[(bgn_, 55, 56)], tag="gexp")
        P.op("dve", lambda e: e.tensor_copy(out=mr16[:, 0:1], in_=me16[:, 0:1]),
             r=[(bgn_, 52, 54)], w=[(bgn_, 56, 57)], tag="gp1")
        P.op("dve", lambda e: e.tensor_copy(out=mr16[:, 1:2], in_=v16[:]),
             r=[(bgn_, 55, 56)], w=[(bgn_, 57, 58)], tag="gp2")
        P.op("act", lambda e: e.activation(out=mr16b[:], in_=mr16[:], func=AF.Copy),
             r=[(bgn_, 56, 58)], w=[(bgn_, 58, 59)], tag="gp3")
        for k in range(8):
            P.op("pe", lambda e, k=k: e.matmul(
                out=P4[:, 32:34], lhsT=hexp[:, k * 128:(k + 1) * 128],
                rhs=mr16b[:], start=True, stop=True),
                r=[(bconst, 0, 1), (bgn_, 58, 59)],
                w=[(bp4, 32, 34)], tag=f"ge{k}", nop=64)
            P.op("dve", lambda e: e.tensor_copy(out=mrk[:], in_=P4[:, 32:34]),
                 r=[(bp4, 32, 34)], w=[(bmrk, 0, 2)], tag=f"gee{k}")
            P.op("dve", lambda e, k=k: e.tensor_mul(
                out=gsc[:, k:k + 1], in0=mrk[:, 1:2], in1=gnw2[:, k:k + 1]),
                r=[(bmrk, 0, 2), (bconst, 0, 1)], w=[(bgn_, 60 + k, 61 + k)],
                tag=f"gsc{k}")
            P.op("dve", lambda e, k=k: e.tensor_mul(
                out=mrk[:, 0:1], in0=mrk[:, 0:1], in1=gsc[:, k:k + 1]),
                r=[(bmrk, 0, 2), (bgn_, 60 + k, 61 + k)], w=[(bmrk, 0, 1)],
                tag=f"gbm{k}")
            P.op("dve", lambda e, k=k: e.tensor_sub(
                out=gbi[:, k:k + 1], in0=gnb2[:, k:k + 1], in1=mrk[:, 0:1]),
                r=[(bmrk, 0, 1), (bconst, 0, 1)], w=[(bgn_, 70 + k, 71 + k)],
                tag=f"gbi{k}")
        for k in range(8):
            P.op("dve", lambda e, k=k: e.tensor_scalar(
                out=A_h[:, KT0:KT0 + 16384].rearrange(
                    "p (c x) -> p c x", c=16)[:, :, k * 128:(k + 1) * 128],
                in0=A_h[:, KT0:KT0 + 16384].rearrange(
                    "p (c x) -> p c x", c=16)[:, :, k * 128:(k + 1) * 128],
                scalar1=gsc[:, k:k + 1], scalar2=gbi[:, k:k + 1],
                op0=ALU.mult, op1=ALU.add),
                r=[(bh, KT0, KT0 + 16384), (bgn_, 60 + k, 61 + k),
                   (bgn_, 70 + k, 71 + k)],
                w=[(bh, KT0, KT0 + 16384)], tag=f"gna{k}")
        P.dma("pool", lambda e: e.dma_start(
            out=A_w[:].rearrange("p (k n) -> p k n", k=8),
            in_=wout_in.rearrange("(k p) n -> p k n", p=128)),
            r=[(bw, 0, 8192)], w=[(bw, 0, 8192)], tag="wlo")
        for mt in range(NT):
            ps = (P5, P6)[mt % 2]
            pb = bp[id(ps)]
            for half in range(2):
                for k in range(8):
                    P.op("pe", lambda e, ps=ps, mt=mt, k=k, half=half: e.matmul(
                        out=ps[:, half * 512:(half + 1) * 512],
                        lhsT=A_h[:, KT0 + mt * 1024 + k * 128:KT0 + mt * 1024 + (k + 1) * 128],
                        rhs=A_w[:, k * 1024 + half * 512:k * 1024 + (half + 1) * 512],
                        start=(k == 0), stop=(k == 7)),
                        r=[(bh, KT0 + mt * 1024 + k * 128, KT0 + mt * 1024 + (k + 1) * 128),
                           (bw, k * 1024 + half * 512, k * 1024 + (half + 1) * 512)],
                        w=[(pb, half * 512, (half + 1) * 512)], tag=f"wo{mt}")
            P.op("dve", lambda e, ps=ps, mt=mt: e.tensor_add(
                out=A_res[:, mt * 1024:(mt + 1) * 1024],
                in0=A_res[:, mt * 1024:(mt + 1) * 1024], in1=ps[:]),
                r=[(pb, 0, 1024), (bres, mt * 1024, (mt + 1) * 1024)],
                w=[(bres, mt * 1024, (mt + 1) * 1024)], tag=f"woa{mt}")

      if stage >= 3:
        ln_phase("ln3")
        ffn_phase(first=False)
    except _EndAttn:
        pass

    for g in range(4):
        P.dma("act", lambda e, g=g: e.dma_start(
            out=y_out.rearrange("(t p) d -> p t d", p=128)[:, g * 4:(g + 1) * 4, :],
            in_=A_res[:].rearrange("p (t n) -> p t n", t=NT)[:, g * 4:(g + 1) * 4, :]),
            r=[(bres, g * 4 * 1024, (g + 1) * 4 * 1024)],
            w=[(bout, g, g + 1)], tag=f"sto{g}")

    P.resolve()
    P.emit(nc, ctx)
    ctx.close()
    return nc



def _consts():
    gam = np.array([1.0 - 2.0 ** (-5.0 - math.floor(8.0 * i / H)) for i in range(H)],
                   dtype=np.float64)
    a = np.arange(CH, dtype=np.float64)
    qs = np.zeros((128, 8 * 128), np.float64)
    ks = np.zeros((128, 8 * 128), np.float64)
    for dt in range(8):
        for par in range(2):
            h = dt * 2 + par
            qs[par * 64:(par + 1) * 64, dt * 128:(dt + 1) * 128] = \
                (gam[h] ** a)[None, :] / 8.0
    for h in range(H):
        ks[:, h * 64:(h + 1) * 64] = (gam[h] ** (-a))[:, None]
    causal = (np.arange(CH)[None, :] >= np.arange(CH)[:, None]).astype(np.float32)
    ident = np.eye(128, dtype=np.float32)
    gdec = np.zeros((128, 512), np.float64)
    for h in range(H):
        gdec[(h % 2) * 64:(h % 2) * 64 + 64, (h // 2) * 64:(h // 2) * 64 + 64] = \
            gam[h] ** CH
    hs = np.zeros((128, 8 * 16), np.float32)
    for k in range(8):
        for par in range(2):
            hs[par * 64:(par + 1) * 64, k * 16 + 2 * k + par] = 1.0
    he = np.zeros((16, 8 * 128), np.float32)
    for k in range(8):
        for par in range(2):
            he[2 * k + par, k * 128 + par * 64:k * 128 + (par + 1) * 64] = 1.0
    return gam, qs, ks, causal, ident, gdec, hs, he


def _prefw(core, gam):
    pos = core % 4
    w = np.zeros((4, 128, 512), np.float64)
    for r in range(4):
        if r < pos:
            for h in range(H):
                w[r, (h % 2) * 64:(h % 2) * 64 + 64,
                  (h // 2) * 64:(h // 2) * 64 + 64] = gam[h] ** (T * (pos - 1 - r))
    return w.astype(np.float32)


def kernel(L0_attn_x, L0_attn_Wq, L0_attn_Wk, L0_attn_Wv, L0_attn_gn_weight,
           L0_attn_gn_bias, L0_attn_Wout, L0_ffn_w1, L0_ffn_b1, L0_ffn_w2,
           L0_ffn_b2):
    import ml_dtypes
    x = np.ascontiguousarray(np.asarray(L0_attn_x, dtype=np.float32))
    w1 = np.ascontiguousarray(np.asarray(L0_ffn_w1, dtype=np.float32))
    w2 = np.ascontiguousarray(np.asarray(L0_ffn_w2, dtype=np.float32))
    b1 = np.asarray(L0_ffn_b1, dtype=np.float32)
    b2 = np.asarray(L0_ffn_b2, dtype=np.float32)
    wq = np.asarray(L0_attn_Wq, dtype=np.float32)
    wk = np.asarray(L0_attn_Wk, dtype=np.float32)
    wv = np.asarray(L0_attn_Wv, dtype=np.float32)
    wout = np.ascontiguousarray(np.asarray(L0_attn_Wout, dtype=np.float32))
    gnw = np.asarray(L0_attn_gn_weight, dtype=np.float32)
    gnb = np.asarray(L0_attn_gn_bias, dtype=np.float32)

    gam, qs, ks, causal, ident, gdec, hs, he = _consts()
    wqkv = np.stack([w.transpose(1, 0, 2).reshape(D, D) for w in (wq, wk, wv)])
    wqkv = np.ascontiguousarray(wqkv)
    b1l = np.ascontiguousarray(b1.reshape(32, 128).T)
    gnw2 = np.ascontiguousarray(gnw.reshape(8, 128).T)
    gnb2 = np.ascontiguousarray(gnb.reshape(8, 128).T)

    shared = {
        "w1": w1, "w2": w2, "b1l": b1l, "b2r": b2[None, :].copy(),
        "wqkv": wqkv, "wout": wout, "gnw": gnw2, "gnb": gnb2,
        "qscale": qs.astype(np.float32),
        "kscale": ks.astype(np.float32),
        "causal": causal, "ident": ident.astype(ml_dtypes.bfloat16),
        "gdecay": gdec.astype(np.float32),
        "hsum16": hs, "hexp": he.astype(ml_dtypes.bfloat16),
        "ones1": np.ones((1, 128), ml_dtypes.bfloat16),
    }
    in_maps = []
    for core in range(NCORES):
        b = core // 4
        t0 = (core % 4) * T
        m = dict(shared)
        m["x"] = np.ascontiguousarray(x[b, t0:t0 + T, :])
        m["prefw"] = _prefw(core, gam)
        in_maps.append(m)

    nc = build_nc(_STAGE, _APARTS, _AP5, _DBG)
    res = run_bass_kernel_spmd(nc, in_maps, core_ids=list(range(NCORES)))
    out = np.empty((B, L, D), dtype=np.float32)
    for core in range(NCORES):
        b = core // 4
        t0 = (core % 4) * T
        out[b, t0:t0 + T, :] = res.results[core]["y"]
    return out
